# version 13
# baseline (speedup 1.0000x reference)
import math
from contextlib import ExitStack

import numpy as np
import ml_dtypes

import concourse.bass as bass
import concourse.mybir as mybir
from concourse.bass_utils import run_bass_kernel_spmd

F32 = mybir.dt.float32
BF16 = mybir.dt.bfloat16
AF = mybir.ActivationFunctionType
ALU = mybir.AluOpType
AX = mybir.AxisListType
BF = ml_dtypes.bfloat16

ENGS = ("sy", "po", "pe", "ac", "ve")
FORCE_STATS0 = False


class Cfg:
    def __init__(self, mini=False):
        if mini:
            self.H, self.TOK, self.NH, self.NKV1, self.QB = 512, 512, 4, 2, 256
        else:
            self.H, self.TOK, self.NH, self.NKV1, self.QB = 4096, 2048, 32, 8, 512
        self.E, self.R = 8, 16
        self.DC = self.H // 128
        self.NQB = self.TOK // self.QB
        self.S = 2 * self.TOK
        self.KT = self.S // 128
        self.KTL = self.TOK // 128
        self.ER = self.E * self.R
        self.REP1 = self.NH // self.NKV1
        assert self.NQB % 2 == 0
        self.NQH = self.NQB // 2
        self.NTT = self.TOK // 128
        self.TPQ = self.QB // 128
        self.SCALE = 1.0 / math.sqrt(128.0)
        self.CBIAS = -30.0
        self.EPS = 1e-5


def _tile_st(Wt):
    d, o = Wt.shape
    dcn, ocn = d // 128, o // 128
    return np.ascontiguousarray(
        Wt.reshape(dcn, 128, ocn, 128).transpose(2, 1, 0, 3)).astype(np.float32)


def _col_bias(b):
    return np.ascontiguousarray(np.asarray(b, np.float32).reshape(-1, 128).T)


def _rope_tables(cfg, half):
    pos = np.arange(cfg.TOK, dtype=np.float64) + half * cfg.TOK
    j = np.arange(64, dtype=np.float64)
    inv = 10000.0 ** (-j / 64.0)
    ang = pos[None, :] * inv[:, None]
    c, s = np.cos(ang), np.sin(ang)
    C = np.concatenate([c, c], 0).astype(np.float32)
    S = np.concatenate([-s, s], 0).astype(np.float32)
    return np.ascontiguousarray(C), np.ascontiguousarray(S)


def _sel_mat(cfg, scaling):
    sel = np.zeros((cfg.E, cfg.ER), np.float32)
    for e in range(cfg.E):
        sel[e, e * cfg.R:(e + 1) * cfg.R] = scaling
    return sel


def prep_shared(cfg, inp):
    f = lambda a: np.asarray(a, dtype=np.float32)
    d = {}
    d["l0a_wq"] = _tile_st(f(inp["L0_attn_Wq"]).T)
    d["l0a_wk"] = _tile_st(f(inp["L0_attn_Wk"]).T)
    d["l0a_wv"] = _tile_st(f(inp["L0_attn_Wv"]).T)
    d["l0a_wo"] = _tile_st(f(inp["L0_attn_Wo"]).T)
    d["l0f_wb"] = _tile_st(f(inp["L0_ffn_W_base"]).T)
    d["l0f_bb"] = _col_bias(inp["L0_ffn_b_base"])
    d["l0f_wg"] = np.ascontiguousarray(
        f(inp["L0_ffn_W_gate"]).T.reshape(cfg.DC, 128, cfg.E).transpose(1, 0, 2))
    d["l0f_wa"] = _tile_st(f(inp["L0_ffn_A"]).reshape(cfg.ER, cfg.H).T)[0]
    d["l0f_wm"] = np.ascontiguousarray(
        f(inp["L0_ffn_B"]).transpose(0, 2, 1).reshape(cfg.ER, cfg.H))
    d["l0f_sel"] = _sel_mat(cfg, 1.0)
    d["l1a_wq"] = _tile_st(f(inp["L1_attn_wq"]))
    d["l1a_bq"] = _col_bias(inp["L1_attn_bq"])
    d["l1a_wk"] = _tile_st(f(inp["L1_attn_wk"]))
    d["l1a_bk"] = _col_bias(inp["L1_attn_bk"])
    d["l1a_wv"] = _tile_st(f(inp["L1_attn_wv"]))
    d["l1a_bv"] = _col_bias(inp["L1_attn_bv"])
    d["l1f_wb"] = _tile_st(f(inp["L1_ffn_W_base"]).T)
    d["l1f_bb"] = _col_bias(inp["L1_ffn_b_base"])
    d["l1f_wg"] = np.ascontiguousarray(
        f(inp["L1_ffn_W_router"]).T.reshape(cfg.DC, 128, cfg.E).transpose(1, 0, 2))
    d["l1f_wa"] = _tile_st(f(inp["L1_ffn_A"]).reshape(cfg.ER, cfg.H).T)[0]
    d["l1f_wm"] = np.ascontiguousarray(
        f(inp["L1_ffn_B"]).transpose(0, 2, 1).reshape(cfg.ER, cfg.H))
    d["l1f_sel"] = _sel_mat(cfg, 2.0)
    d["iden_f"] = np.eye(128, dtype=np.float32)
    d["iden_b"] = np.eye(128, dtype=np.float32)
    return d


WEIGHT_SHAPES = None


class Rec:
    def __init__(self, bld):
        self.b = bld
        self.q = {e: [] for e in ENGS}

    def on(self, eng, fn, waits=None, inc=None, incn=1):
        b = self.b
        wl = []
        if waits:
            for s, v in waits.items():
                if s is None or v is None or v <= 0:
                    continue
                key = (eng, s)
                if b.lastw.get(key, 0) < v:
                    b.lastw[key] = v
                    wl.append((b.sems[s], v))
        inc_h = None
        if inc is not None:
            b.ct[inc] = b.ct.get(inc, 0) + incn
            inc_h = (b.sems[inc], incn)
        newct = b.ct.get(inc, 0) if inc is not None else None

        def thunk(e, wl=wl, fn=fn, inc_h=inc_h):
            for h, v in wl:
                e.wait_ge(h, v)
            ins = fn(e)
            if inc_h is not None:
                ins.then_inc(inc_h[0], inc_h[1])

        self.q[eng].append(thunk)
        return newct


class Builder:

    def __init__(self, cfg, blocks):
        self.cfg = cfg
        self.blocks = blocks
        self.nc = bass.Bass("TRN2", target_bir_lowering=False, debug=False)
        self.ctx = ExitStack()
        self.ct = {}
        self.lastw = {}
        self.sems = {}
        self._semi = 0
        self.dram = {}
        self.sb = {}
        self.dsems = {}
        self.guard = {}
        self.cc_sem = None
        self._declare_dram()
        self._alloc_sbuf()

    def new_sem(self, tag):
        name = f"s{self._semi}_{tag}"
        self._semi += 1
        self.sems[name] = self.ctx.enter_context(self.nc.semaphore(name))
        self.ct[name] = 0
        return name

    def blk_sems(self, tag):
        return {n: self.new_sem(f"{tag}{n}") for n in
                ("eP", "eA", "eV", "eO")}

    def dma_sem(self, stream, slot=0):
        key = (stream, slot)
        if key not in self.dsems:
            self.dsems[key] = self.new_sem(f"d_{stream}{slot}")
        return self.dsems[key]

    def need(self, key):
        return dict(self.guard.get(key, {}))

    def done(self, key, sem, ct):
        g = self.guard.setdefault(key, {})
        if ct is not None and (sem not in g or g[sem] < ct):
            g[sem] = ct

    def set_guard(self, key, d):
        g = self.guard.setdefault(key, {})
        for k, v in d.items():
            if v and (k not in g or g[k] < v):
                g[k] = v

    def _declare_dram(self):
        cfg, nc = self.cfg, self.nc
        H, TOK = cfg.H, cfg.TOK

        def din(name, shape, dt_=F32):
            self.dram[name] = nc.dram_tensor(name, list(shape), dt_, kind="ExternalInput")

        def dint(name, shape, dt_=F32):
            self.dram[name] = nc.dram_tensor(name, list(shape), dt_)

        din("x0", (cfg.DC, cfg.NQB, 128, cfg.QB))
        self.dram["xout"] = nc.dram_tensor(
            "xout", [cfg.DC, cfg.NQB, 128, cfg.QB], F32, kind="ExternalOutput")
        din("ropeC", (128, TOK))
        din("ropeS", (128, TOK))
        din("iden_f", (128, 128))
        din("iden_b", (128, 128))

        shp = {}
        for lay in ("l0", "l1"):
            shp[f"{lay}f_wb"] = (cfg.DC, 128, cfg.DC, 128)
            shp[f"{lay}f_bb"] = (128, cfg.DC)
            shp[f"{lay}f_wg"] = (128, cfg.DC, cfg.E)
            shp[f"{lay}f_wa"] = (128, cfg.DC, 128)
            shp[f"{lay}f_wm"] = (cfg.ER, cfg.H)
            shp[f"{lay}f_sel"] = (cfg.E, cfg.ER)
        for w in ("wq", "wk", "wv", "wo"):
            shp[f"l0a_{w}"] = (cfg.DC, 128, cfg.DC, 128)
        shp["l1a_wq"] = (cfg.DC, 128, cfg.DC, 128)
        shp["l1a_wk"] = (cfg.NKV1, 128, cfg.DC, 128)
        shp["l1a_wv"] = (cfg.NKV1, 128, cfg.DC, 128)
        shp["l1a_bq"] = (128, cfg.DC)
        shp["l1a_bk"] = (128, cfg.NKV1)
        shp["l1a_bv"] = (128, cfg.NKV1)

        need = set()
        for blk in self.blocks:
            lay = blk[:2].lower()
            if blk.endswith("A"):
                if lay == "l0":
                    need |= {"l0a_wq", "l0a_wk", "l0a_wv", "l0a_wo"}
                else:
                    need |= {"l1a_wq", "l1a_bq", "l1a_wk", "l1a_bk", "l1a_wv", "l1a_bv"}
            else:
                need |= {f"{lay}f_{s}" for s in ("wb", "bb", "wg", "wa", "wm", "sel")}
        for n in sorted(need):
            din(n, shp[n])

        import os as _os
        _dbg = _os.environ.get("MK_DBG_MID") == "1"
        tshape = [cfg.DC, cfg.NQB, 128, cfg.QB]
        for i in range(len(self.blocks) - 1):
            if _dbg:
                self.dram[f"xmid{i}"] = nc.dram_tensor(
                    f"xmid{i}", tshape, F32, kind="ExternalOutput")
            else:
                dint(f"xmid{i}", tshape)
        if _dbg:
            self.dram["ybuf"] = nc.dram_tensor("ybuf", tshape, F32,
                                               kind="ExternalOutput")
        else:
            dint("ybuf", tshape)
        if _os.environ.get("MK_DBG_AR") == "1":
            self.dram["dbg_arena"] = nc.dram_tensor(
                "dbg_arena", [128, self.cfg.DC, TOK], BF16, kind="ExternalOutput")
        if _os.environ.get("MK_DBG_RT") == "1":
            cfgE, NTT = self.cfg.E, self.cfg.NTT
            self.dram["dbg_lgtm"] = nc.dram_tensor("dbg_lgtm", [128, NTT * cfgE], F32, kind="ExternalOutput")
            self.dram["dbg_wtm"] = nc.dram_tensor("dbg_wtm", [128, NTT * cfgE], F32, kind="ExternalOutput")
            self.dram["dbg_hw"] = nc.dram_tensor("dbg_hw", [128, TOK], BF16, kind="ExternalOutput")
            self.dram["dbg_h"] = nc.dram_tensor("dbg_h", [128, TOK], F32, kind="ExternalOutput")
            self.dram["dbg_wtb"] = nc.dram_tensor("dbg_wtb", [cfgE, TOK], BF16, kind="ExternalOutput")
        if any(b.endswith("A") for b in self.blocks):
            dint("qrope", (cfg.NH, 128, TOK), BF16)
            dint("kme", (cfg.NH * 128, TOK), BF16)
            dint("kall", (2 * cfg.NH * 128, TOK), BF16)
            dint("vme", (cfg.NH * cfg.KTL * 128, 128), BF16)
            dint("vall", (2 * cfg.NH * cfg.KTL * 128, 128), BF16)

    def xbuf(self, i):
        if i == 0:
            return self.dram["x0"].ap()
        if i == len(self.blocks):
            return self.dram["xout"].ap()
        return self.dram[f"xmid{i-1}"].ap()

    def _alloc_sbuf(self):
        cfg, nc = self.cfg, self.nc
        H, TOK, DC, QB, E = cfg.H, cfg.TOK, cfg.DC, cfg.QB, cfg.E
        rem = nc.sbuf_bytes_remaining
        owner = nc.alloc_sbuf_tensor("OWNER", [128, rem - 256], mybir.dt.uint8)
        base = nc.lookup_mloc(owner).addr
        self._limit = rem - 256

        def at(name, shape, dt_, off):
            off = (off + 31) // 32 * 32
            bpp = int(np.prod(shape[1:])) * mybir.dt.size(dt_)
            assert off + bpp <= self._limit, f"{name}: {off}+{bpp} > {self._limit}"
            t = nc.alloc_sbuf_tensor_at(name, list(shape), dt_, offset=base + off)
            self.sb[name] = t
            return t, (off + bpp + 31) // 32 * 32

        off = 0
        _, off = at("arena", (128, DC, TOK), BF16, off)
        wst_off = off
        _, off = at("wst", (128, 2, DC, 128), BF16, off)
        if cfg.H != 4096:
            _, off = at("Ba", (128, TOK), F32, off)
            _, off = at("Bc", (128, TOK), F32, off)
            _, off = at("ep_x", (128, 4, QB), F32, off)
            _, off = at("ep_t", (128, 2, QB), F32, off)
            _, off = at("ep_y", (128, 4, QB), F32, off)
            _, off = at("ep_o", (128, 2, QB), F32, off)
            _, off = at("ep_b", (128, 2, QB), BF16, off)
            _, off = at("ep_b2", (128, 2, QB), BF16, off)
            _, off = at("prj_raw", (128, 2, QB), F32, off)
            _, off = at("prj_sw", (128, 2, QB), F32, off)
            _, off = at("prj_t1", (128, 2, QB), F32, off)
            _, off = at("prj_qk", (128, 4, QB), BF16, off)
            _, off = at("prj_v", (128, 2, QB), BF16, off)
            _, off = at("prj_vt", (128, 2, cfg.TPQ, 128), BF16, off)
            _, off = at("ropeCsb", (128, TOK), F32, off)
            _, off = at("ropeSsb", (128, TOK), F32, off)
            _, off = at("kbuf", (128, 2, cfg.KT, 128), BF16, off)
            _, off = at("vbuf", (128, 2, cfg.KT, 128), BF16, off)
            _, off = at("qbuf", (128, 2, TOK), BF16, off)
            _, off = at("acc", (128, 2, QB), F32, off)
            _, off = at("ffn_h", (128, TOK), F32, off)
            _, off = at("ffn_hw", (128, TOK), BF16, off)
            _, off = at("lgs", (E, 2, QB), F32, off)
            _, off = at("fy", (128, 4, QB), F32, off)
            _, off = at("wTb", (E, TOK), BF16, off)
            _, off = at("ffn_a", (128, DC, 128), BF16, off)
            _, off = at("ffn_b", (128, H), BF16, off)
            _, off = at("att_y", (128, 4, QB), F32, off)
            _, off = at("att_b", (128, 4, QB), BF16, off)
            _, off = at("att_b2", (128, 4, QB), BF16, off)
        else:
            PH = 46 * 1024 if cfg.H == 4096 else 30 * 1024
            ph0 = off
            off += PH
            o = ph0
            _, o = at("Ba", (128, TOK), F32, o)
            _, o = at("Bc", (128, TOK), F32, o)
            ep_x_off = o
            _, o = at("ep_x", (128, 4, QB), F32, o)
            _, o = at("ep_t", (128, 2, QB), F32, o)
            _, o = at("ep_y", (128, 4, QB), F32, o)
            _, o = at("ep_o", (128, 2, QB), F32, o)
            epb_off = o
            _, o = at("ep_b", (128, 2, QB), BF16, o)
            _, o = at("ep_b2", (128, 2, QB), BF16, o)
            assert o - ph0 <= PH, f"epi overlay {o - ph0} > {PH}"
            self._epb_off = epb_off
            o = ph0
            _, o = at("prj_raw", (128, 2, QB), F32, o)
            _, o = at("prj_sw", (128, 2, QB), F32, o)
            _, o = at("prj_t1", (128, 2, QB), F32, o)
            _, o = at("prj_qk", (128, 4, QB), BF16, o)
            _, o = at("prj_v", (128, 2, QB), BF16, o)
            _, o = at("prj_vt", (128, 2, cfg.TPQ, 128), BF16, o)
            ropec_off = ph0 + (PH - 2 * TOK * 4)
            assert o <= ropec_off
            _, o2 = at("ropeCsb", (128, TOK), F32, ropec_off)
            _, o2 = at("ropeSsb", (128, TOK), F32, o2)
            assert o2 - ph0 <= PH
            o = ph0
            _, o = at("kbuf", (128, 2, cfg.KT, 128), BF16, o)
            _, o = at("vbuf", (128, 2, cfg.KT, 128), BF16, o)
            _, o = at("qbuf", (128, 2, TOK), BF16, o)
            _, o = at("acc", (128, 2, QB), F32, o)
            assert o - ph0 <= PH, f"attn overlay {o - ph0} > {PH}"
            o = ph0
            _, o = at("ffn_h", (128, TOK), F32, o)
            _, o = at("ffn_hw", (128, TOK), BF16, o)
            _, o = at("lgs", (cfg.E, 2, QB), F32, o)
            assert o <= ep_x_off
            _, o = at("fy", (128, 4, QB), F32, ep_x_off)
            o2 = ep_x_off + 4 * QB * 4
            _, o2 = at("wTb", (cfg.E, TOK), BF16, o2)
            _, o2 = at("ffn_a", (128, DC, 128), BF16, o2)
            _, o2 = at("ffn_b", (128, H), BF16, o2)
            assert o2 - ph0 <= PH, f"ffn overlay {o2 - ph0} > {PH}"

            at("att_y", (128, 4, QB), F32, 0)
            at("att_b", (128, 4, QB), BF16, 4 * QB * 4)
            at("att_b2", (128, 4, QB), BF16, 4 * QB * 4 + 4 * QB * 2)

        if cfg.H == 4096:
            assert 6 * QB * 2 <= DC * 128 * 2
            at("pt", (128, 6, QB), BF16, wst_off + DC * 128 * 2)
            at("rb", (128, 2, QB), F32, self._epb_off)
        else:
            _, off = at("pt", (128, 6, QB), BF16, off)
            _, off = at("rb", (128, 2, QB), F32, off)

        _, off = at("stat", (128, max(TOK, 4 * QB)), F32, off)
        _, off = at("iden_fsb", (128, 128), F32, off)
        _, off = at("iden_bsb", (128, 128), BF16, off)
        _, off = at("ones_b", (128, 1), BF16, off)
        _, off = at("ones_f", (128, 128), F32, off)
        _, off = at("konst", (128, 4), F32, off)
        _, off = at("bias_sb", (128, DC), F32, off)
        _, off = at("b_q", (128, DC), F32, off)
        _, off = at("b_k", (128, cfg.NKV1), F32, off)
        _, off = at("b_v", (128, cfg.NKV1), F32, off)
        _, off = at("wg_sb", (128, DC, E), BF16, off)
        _, off = at("sel_sb", (E, cfg.ER), BF16, off)
        _, off = at("r16", (128, 6, cfg.NTT), F32, off)
        _, off = at("r8", (128, 3, E), F32, off)
        _, off = at("lg_tm", (128, cfg.NTT * E), F32, off)
        _, off = at("t_tm", (128, cfg.NTT * E), F32, off)
        _, off = at("w_tm", (128, cfg.NTT * E), F32, off)
        _, off = at("yb2", (128, 2, cfg.QB), BF16, off)
        assert off <= self._limit, f"static overflow {off} > {self._limit}"
        self.sbuf_used = off
        self._psi = 0

    def psum_banks(self, h):
        mloc = self.nc.lookup_mloc(h)
        base = mloc.addr - self.nc.psum_base
        bpp = int(np.prod(h.shape[1:])) * mybir.dt.size(h.dtype)
        b0 = base // 2048
        b1 = (base + bpp - 1) // 2048
        return list(range(b0, b1 + 1))

    def psum_need(self, h):
        w = {}
        for b in self.psum_banks(h):
            for k, v in self.guard.get(("bank", b), {}).items():
                w[k] = max(w.get(k, 0), v)
        return w

    def psum_done(self, h, sem, ct):
        for b in self.psum_banks(h):
            self.done(("bank", b), sem, ct)

    def psum_scope(self, spec):
        es = ExitStack()
        out = {}
        for name, shape, dt_ in spec:
            self._psi += 1
            out[name] = es.enter_context(
                self.nc.psum_tensor(f"{name}_{self._psi}", list(shape), dt_))
        return es, out

    def flush(self, block, rec):
        ename = {"sy": "sync", "po": "gpsimd", "pe": "tensor", "ac": "scalar",
                 "ve": "vector"}
        for e in ENGS:
            thunks = rec.q[e]
            if not thunks:
                continue

            def body(engine, thunks=thunks):
                for t in thunks:
                    t(engine)
            getattr(block, ename[e])(body)


def stat_slots(cfg, banks):
    return [(banks[i // 3], 32 * (i % 3)) for i in range(2 * cfg.NQB)]


RG = [[0, 1], [2, 3], [4, 5], [6, 7]]


def _dma(bld, r, eng, stream, slot, fn, waits=None):
    sem = bld.dma_sem(stream, slot)
    w = bld.need((stream, slot))
    if waits:
        for k, v in waits.items():
            if v:
                w[k] = max(w.get(k, 0), v)
    ct = r.on(eng, fn, waits=w, inc=sem, incn=16)
    return sem, ct


def emit_setup(bld, r, s):
    sb, d = bld.sb, bld.dram
    cts = {}
    cts["iden_f"] = dict([_dma(bld, r, "sy", "iden_f", 0, lambda e: e.dma_start(
        out=sb["iden_fsb"][:, :], in_=d["iden_f"].ap()))])
    cts["iden_b"] = dict([_dma(bld, r, "po", "iden_b", 0, lambda e: e.dma_start(
        out=sb["iden_bsb"][:, :], in_=d["iden_b"].ap()))])
    r.on("po", lambda e: e.memset(sb["ones_b"][:, :], 1.0))
    r.on("po", lambda e: e.memset(sb["ones_f"][:, :], 1.0))
    r.on("po", lambda e: e.memset(sb["konst"][:, 0:1], bld.cfg.CBIAS))
    cts["konst"] = {s["eO"]: r.on("po", lambda e: e.memset(
        sb["konst"][:, 1:2], bld.cfg.EPS), inc=s["eO"])}
    return cts


def emit_stats0(bld, r, s, xin, pst_slots, setup_cts, gate=None):
    cfg, sb = bld.cfg, bld.sb
    DC, NQB, QB = cfg.DC, cfg.NQB, cfg.QB
    s1_ct, s2_ct = {}, {}
    slot_last = {}
    for qb in range(NQB):
        for dc in range(DC):
            idx = qb * DC + dc
            s4, s2 = idx % 4, idx % 2
            dsem, dct = _dma(bld, r, "sy", "x", s4,
                             lambda e, dc=dc, qb=qb, s4=s4: e.dma_start(
                                 out=sb["ep_x"][:, s4, :],
                                 in_=xin[dc, qb]),
                             waits=gate)
            qct = r.on("ac", lambda e, s4=s4, s2=s2: e.square(
                sb["ep_b2"][:, s2, :], sb["ep_x"][:, s4, :]),
                waits={dsem: dct, s["eP"]: s2_ct.get(idx - 2, 0)}, inc=s["eA"])
            b1, m1 = pst_slots[2 * qb]
            b2, m2 = pst_slots[2 * qb + 1]
            w1 = dict(setup_cts["konst"])
            w1[dsem] = max(w1.get(dsem, 0), dct)
            if dc == 0:
                for k2, v2 in bld.psum_need(b1).items():
                    w1[k2] = max(w1.get(k2, 0), v2)
            c1 = r.on("pe", lambda e, b1=b1, m1=m1, s4=s4, dc=dc: e.matmul(
                b1[m1:m1 + 1, :], sb["ones_f"][:, 0:1], sb["ep_x"][:, s4, :],
                start=(dc == 0), stop=(dc == DC - 1)),
                waits=w1, inc=s["eP"])
            s1_ct[idx] = c1
            bld.set_guard(("x", s4), {s["eA"]: qct, s["eP"]: c1})
            w2b = {s["eA"]: qct}
            if dc == 0:
                for k2, v2 in bld.psum_need(b2).items():
                    w2b[k2] = max(w2b.get(k2, 0), v2)
            c2 = r.on("pe", lambda e, b2=b2, m2=m2, s2=s2, dc=dc: e.matmul(
                b2[m2:m2 + 1, :], sb["ones_b"][:, :], sb["ep_b2"][:, s2, :],
                start=(dc == 0), stop=(dc == DC - 1)),
                waits=w2b, inc=s["eP"])
            s2_ct[idx] = c2
            slot_last[2 * qb] = c1
            slot_last[2 * qb + 1] = c2
    return emit_stats_evac(bld, r, s, pst_slots, slot_last)


def emit_stats_evac(bld, r, s, pst_slots, slot_last):
    cfg, sb = bld.cfg, bld.sb
    QB = cfg.QB
    last = 0
    for qb in range(cfg.NQB):
        for k in range(2):
            b, m = pst_slots[2 * qb + k]
            prow = 0 if k == 0 else 32
            last = r.on("ac", lambda e, b=b, m=m, prow=prow, qb=qb: e.copy(
                sb["stat"][prow:prow + 1, qb * QB:(qb + 1) * QB], b[m:m + 1, :]),
                waits={s["eP"]: slot_last[2 * qb + k]}, inc=s["eA"])
            bld.psum_done(b, s["eA"], last)
    return last


def emit_ln_finalize(bld, r, s, stats_ready, pb, setup_cts):
    cfg, sb = bld.cfg, bld.sb
    QB = cfg.QB
    inv_h = 1.0 / cfg.H
    ba_ct = 0
    free_b = {0: 0, 1: 0}
    scr_free = 0
    for qb in range(cfg.NQB):
        w = dict(stats_ready)
        for k2, v2 in setup_cts["konst"].items():
            w[k2] = max(w.get(k2, 0), v2)
        w[s["eA"]] = max(w.get(s["eA"], 0), free_b[0])
        for k2, v2 in bld.psum_need(pb["b0"]).items():
            w[k2] = max(w.get(k2, 0), v2)
        c0 = r.on("pe", lambda e, qb=qb: e.matmul(
            pb["b0"][:, :], sb["ones_f"][0:1, :],
            sb["stat"][0:1, qb * QB:(qb + 1) * QB], start=True, stop=True),
            waits=w, inc=s["eP"])
        w2 = dict(stats_ready)
        for k2, v2 in setup_cts["konst"].items():
            w2[k2] = max(w2.get(k2, 0), v2)
        w2[s["eA"]] = max(w2.get(s["eA"], 0), free_b[1])
        for k2, v2 in bld.psum_need(pb["b1"]).items():
            w2[k2] = max(w2.get(k2, 0), v2)
        c1 = r.on("pe", lambda e, qb=qb: e.matmul(
            pb["b1"][:, :], sb["ones_f"][32:33, :],
            sb["stat"][32:33, qb * QB:(qb + 1) * QB], start=True, stop=True),
            waits=w2, inc=s["eP"])
        cm = r.on("ac", lambda e: e.mul(sb["ep_t"][:, 0, :], pb["b0"][:, :], inv_h),
                  waits={s["eP"]: c0, s["eV"]: scr_free}, inc=s["eA"])
        free_b[0] = cm
        bld.psum_done(pb["b0"], s["eA"], cm)
        ce = r.on("ac", lambda e: e.mul(sb["ep_t"][:, 1, :], pb["b1"][:, :], inv_h),
                  waits={s["eP"]: c1}, inc=s["eA"])
        free_b[1] = ce
        bld.psum_done(pb["b1"], s["eA"], ce)
        cq = r.on("ac", lambda e: e.square(sb["ep_o"][:, 0, :], sb["ep_t"][:, 0, :]),
                  waits={s["eA"]: cm}, inc=s["eA"])
        cv = r.on("ve", lambda e: e.tensor_sub(
            sb["ep_o"][:, 1, :], sb["ep_t"][:, 1, :], sb["ep_o"][:, 0, :]),
            waits={s["eA"]: cq}, inc=s["eV"])
        cs = r.on("ac", lambda e: e.activation(
            sb["ep_t"][:, 1, :], sb["ep_o"][:, 1, :], AF.Ln,
            bias=sb["konst"][:, 1:2], scale=1.0),
            waits=dict(setup_cts["konst"], **{s["eV"]: cv}), inc=s["eA"])
        cr = r.on("ac", lambda e, qb=qb: e.activation(
            sb["Ba"][:, qb * QB:(qb + 1) * QB], sb["ep_t"][:, 1, :],
            AF.Exp, scale=-0.5),
            waits={s["eA"]: cs}, inc=s["eA"])
        ba_ct = r.on("ve", lambda e, qb=qb: e.tensor_mul(
            sb["Bc"][:, qb * QB:(qb + 1) * QB], sb["ep_t"][:, 0, :],
            sb["Ba"][:, qb * QB:(qb + 1) * QB]),
            waits={s["eA"]: cr}, inc=s["eV"])
        scr_free = ba_ct
    return ba_ct


def emit_ln_apply(bld, r, s, xin, ba_ct, extra_wait=None):
    cfg, sb = bld.cfg, bld.sb
    DC, NQB, QB = cfg.DC, cfg.NQB, cfg.QB
    o_ct = {}
    last = 0
    arena_q = []
    for qb in range(NQB):
        for dc in range(DC):
            idx = qb * DC + dc
            s4, s2 = idx % 4, idx % 2
            dsem, dct = _dma(bld, r, "sy", "x", s4,
                             lambda e, dc=dc, qb=qb, s4=s4: e.dma_start(
                                 out=sb["ep_x"][:, s4, :],
                                 in_=xin[dc, qb]),
                             waits=extra_wait)
            c = r.on("po", lambda e, s4=s4, qb=qb: e.tensor_mul(
                sb["ep_x"][:, s4, :], sb["ep_x"][:, s4, :],
                sb["Ba"][:, qb * QB:(qb + 1) * QB]),
                waits={dsem: dct, s["eV"]: ba_ct},
                inc=s["eO"])
            last = r.on("ve", lambda e, s4=s4, dc=dc, qb=qb: e.tensor_tensor(
                bld.sb["arena"][:, dc, qb * QB:(qb + 1) * QB],
                sb["ep_x"][:, s4, :], sb["Bc"][:, qb * QB:(qb + 1) * QB],
                ALU.subtract),
                waits={s["eO"]: c}, inc=s["eV"])
            bld.set_guard(("x", s4), {s["eV"]: last})
            o_ct[idx] = last
        arena_q.append({s["eV"]: last})
    return {s["eV"]: last}, arena_q


def emit_proj(bld, r, s, lay, kind, arena_rdy, pp, ptr, setup_cts,
              bias_ld=None, rope_ld=None, wst_free=None, preloaded=None,
              arena_q=None):
    cfg, sb, d = bld.cfg, bld.sb, bld.dram
    DC, NQB, QB, TPQ = cfg.DC, cfg.NQB, cfg.QB, cfg.TPQ
    if lay == "l0":
        OC, bias = DC, None
        wname = f"l0a_w{kind}"
    else:
        OC = DC if kind == "q" else cfg.NKV1
        bias = {"q": "b_q", "k": "b_k", "v": "b_v"}[kind]
        wname = f"l1a_w{kind}"
    rope = kind in ("q", "k")
    evac_ct, mm_last = {}, {}
    raw_free, tr_ct, vt_ct = {}, {}, {}
    out_sems = {}
    ev = 0
    for oc in range(OC):
        par = oc % 2
        if preloaded and oc in preloaded:
            wsem, wct = preloaded[oc]
        else:
            w = {s["eP"]: mm_last.get(oc - 2, 0)}
            if wst_free:
                for k2, v2 in wst_free.items():
                    w[k2] = max(w.get(k2, 0), v2)
            wsem, wct = _dma(bld, r, "po", "w", par,
                             lambda e, oc=oc, par=par: e.dma_start(
                                 out=sb["wst"][:, par, :, :],
                                 in_=d[wname].ap()[oc]),
                             waits=w)
        for qb in range(NQB):
            bank = pp[qb % 2]
            eidx = oc * NQB + qb
            for dc in range(DC):
                w = {}
                if dc == 0:
                    w = dict(arena_q[qb]) if arena_q else dict(arena_rdy)
                    w[wsem] = max(w.get(wsem, 0), wct)
                    for k2, v2 in bld.psum_need(bank).items():
                        w[k2] = max(w.get(k2, 0), v2)
                c = r.on("pe", lambda e, par=par, dc=dc, qb=qb, bank=bank: e.matmul(
                    bank[:, :], sb["wst"][:, par, dc, :],
                    sb["arena"][:, dc, qb * QB:(qb + 1) * QB],
                    start=(dc == 0), stop=(dc == DC - 1)),
                    waits=w, inc=s["eP"])
            mm_last[oc] = c
            bld.set_guard(("w", par), {s["eP"]: c})
            if rope:
                r2, s4 = ev % 2, ev % 4
                wv = {s["eP"]: c, s["eV"]: raw_free.get(ev - 2, 0)}
                if bias:
                    for k2, v2 in bias_ld.items():
                        wv[k2] = max(wv.get(k2, 0), v2)
                    ct_raw = r.on("ac", lambda e, bank=bank, r2=r2, oc=oc,
                                  bias=bias: e.activation(
                                      sb["prj_raw"][:, r2, :], bank[:, :],
                                      AF.Identity, bias=sb[bias][:, oc:oc + 1],
                                      scale=1.0),
                                  waits=wv, inc=s["eA"])
                else:
                    ct_raw = r.on("ac", lambda e, bank=bank, r2=r2: e.copy(
                        sb["prj_raw"][:, r2, :], bank[:, :]), waits=wv, inc=s["eA"])
                evac_ct[eidx] = ct_raw
                bld.psum_done(bank, s["eA"], ct_raw)
                r.on("ve", lambda e, r2=r2: e.tensor_copy(
                    sb["prj_sw"][0:64, r2, :], sb["prj_raw"][64:128, r2, :]),
                    waits={s["eA"]: ct_raw}, inc=s["eV"])
                c2 = r.on("ve", lambda e, r2=r2: e.tensor_copy(
                    sb["prj_sw"][64:128, r2, :], sb["prj_raw"][0:64, r2, :]),
                    waits={s["eA"]: ct_raw}, inc=s["eV"])
                wv = {s["eA"]: ct_raw}
                for k2, v2 in rope_ld.items():
                    wv[k2] = max(wv.get(k2, 0), v2)
                ct_t1 = r.on("ve", lambda e, r2=r2, qb=qb: e.tensor_mul(
                    sb["prj_t1"][:, r2, :], sb["prj_raw"][:, r2, :],
                    sb["ropeCsb"][:, qb * QB:(qb + 1) * QB]),
                    waits=wv, inc=s["eV"])
                wv2 = {s["eV"]: c2}
                for k2, v2 in rope_ld.items():
                    wv2[k2] = max(wv2.get(k2, 0), v2)
                ct_sw = r.on("ve", lambda e, r2=r2, qb=qb: e.tensor_mul(
                    sb["prj_sw"][:, r2, :], sb["prj_sw"][:, r2, :],
                    sb["ropeSsb"][:, qb * QB:(qb + 1) * QB]),
                    waits=wv2, inc=s["eV"])
                wq = bld.need(("qk", s4))
                wq[s["eV"]] = max(wq.get(s["eV"], 0), ct_t1, ct_sw)
                ct_out = r.on("ve", lambda e, r2=r2, s4=s4: e.tensor_tensor(
                    sb["prj_qk"][:, s4, :], sb["prj_t1"][:, r2, :],
                    sb["prj_sw"][:, r2, :], ALU.add),
                    waits=wq, inc=s["eV"])
                raw_free[ev] = ct_out
                if kind == "q":
                    dst = d["qrope"].ap()[oc, :, qb * QB:(qb + 1) * QB]
                else:
                    dst = d["kme"].ap()[oc * 128:(oc + 1) * 128,
                                        qb * QB:(qb + 1) * QB]
                osem, oct = _dma(bld, r, "sy", "qk", s4,
                                 lambda e, s4=s4, dst=dst: e.dma_start(
                                     out=dst, in_=sb["prj_qk"][:, s4, :]),
                                 waits={s["eV"]: ct_out})
                bld.set_guard(("qk", s4), {osem: oct})
                out_sems[osem] = oct
            else:
                sv = ev % 2
                wv = {s["eP"]: max(c, tr_ct.get(ev - 2, 0))}
                if bias:
                    for k2, v2 in bias_ld.items():
                        wv[k2] = max(wv.get(k2, 0), v2)
                    ct_v = r.on("ac", lambda e, bank=bank, sv=sv, oc=oc, bias=bias:
                                e.activation(sb["prj_v"][:, sv, :], bank[:, :],
                                             AF.Identity,
                                             bias=sb[bias][:, oc:oc + 1], scale=1.0),
                                waits=wv, inc=s["eA"])
                else:
                    ct_v = r.on("ac", lambda e, bank=bank, sv=sv: e.copy(
                        sb["prj_v"][:, sv, :], bank[:, :]), waits=wv, inc=s["eA"])
                evac_ct[eidx] = ct_v
                bld.psum_done(bank, s["eA"], ct_v)
                trc = 0
                wv = {s["eA"]: max(ct_v, vt_ct.get(ev - 1, 0))}
                for k2, v2 in bld.psum_need(ptr).items():
                    wv[k2] = max(wv.get(k2, 0), v2)
                for k2, v2 in setup_cts["iden_b"].items():
                    wv[k2] = max(wv.get(k2, 0), v2)
                for t in range(TPQ):
                    trc = r.on("pe", lambda e, sv=sv, t=t: e.transpose(
                        ptr[:, t * 128:(t + 1) * 128],
                        sb["prj_v"][:, sv, t * 128:(t + 1) * 128],
                        sb["iden_bsb"][:, :]),
                        waits=wv, inc=s["eP"])
                tr_ct[ev] = trc
                wv = bld.need(("vt", sv))
                wv[s["eP"]] = max(wv.get(s["eP"], 0), trc)
                cvt = r.on("ac", lambda e, sv=sv: e.copy(
                    sb["prj_vt"][:, sv, :, :].rearrange("p a b -> p (a b)"),
                    ptr[:, 0:TPQ * 128]),
                    waits=wv, inc=s["eA"])
                vt_ct[ev] = cvt
                bld.psum_done(ptr, s["eA"], cvt)
                vview = d["vme"].ap().rearrange(
                    "(h p kt) dd -> h p kt dd", kt=cfg.KTL, p=128)
                dst = vview[oc, :, qb * TPQ:(qb + 1) * TPQ, :]
                osem, oct = _dma(bld, r, "sy", "vt", sv,
                                 lambda e, sv=sv, dst=dst: e.dma_start(
                                     out=dst, in_=sb["prj_vt"][:, sv, :, :]),
                                 waits={s["eA"]: cvt})
                bld.set_guard(("vt", sv), {osem: oct})
                out_sems[osem] = oct
            ev += 1
    return {"out": out_sems, "mm_last": max(mm_last.values())}


def emit_exchange(bld, r, s, k_out, v_out, nkv):
    d = bld.dram
    cfg = bld.cfg
    if bld.cc_sem is None:
        bld.cc_sem = bld.new_sem("cc")
    cc = bld.cc_sem
    krows = nkv * 128
    vrows = nkv * cfg.KTL * 128
    r.on("po", lambda e: e.collective_compute(
        "AllGather", ALU.bypass, replica_groups=RG,
        ins=[d["kme"].ap()[0:krows, :].opt()],
        outs=[d["kall"].ap()[0:2 * krows, :].opt()]),
        waits=dict(k_out), inc=cc)
    ct = r.on("po", lambda e: e.collective_compute(
        "AllGather", ALU.bypass, replica_groups=RG,
        ins=[d["vme"].ap()[0:vrows, :].opt()],
        outs=[d["vall"].ap()[0:2 * vrows, :].opt()]),
        waits=dict(v_out), inc=cc)
    return {cc: ct}


def emit_attention(bld, r, s, lay, setup_cts, kv_cc, q_ready):
    cfg, sb, d = bld.cfg, bld.sb, bld.dram
    QB, KT, KTL = cfg.QB, cfg.KT, cfg.KTL
    NKV = cfg.NH if lay == "l0" else cfg.NKV1
    REP = 1 if lay == "l0" else cfg.REP1
    es, ps = bld.psum_scope(
        [("s00", (128, QB), F32), ("s01", (128, QB), F32),
         ("s10", (128, QB), F32), ("s11", (128, QB), F32),
         ("pl", (128, QB), F32), ("po0", (128, QB), F32),
         ("po1", (128, QB), F32)])
    sbank = {(0, 0): ps["s00"], (0, 1): ps["s01"],
             (1, 0): ps["s10"], (1, 1): ps["s11"]}

    kall = d["kall"].ap()[0:2 * NKV * 128, :].rearrange(
        "(rr h p) t -> rr h p t", rr=2, p=128)
    vall = d["vall"].ap()[0:2 * NKV * KTL * 128, :].rearrange(
        "(rr h p kt) dd -> rr h p kt dd", rr=2, kt=KTL, p=128)
    kv_rd, q_rd = {}, {}
    kv_used, q_used = {}, {}
    bank_free = {}
    pt_free = {}
    acc_free = {0: {}, 1: {}}
    o_free = {}
    y_out = {}
    out_eV = 0

    def load_kv(kvi):
        if kvi >= NKV:
            return
        par = kvi % 2
        w = dict(kv_cc)
        for k2, v2 in q_ready.items():
            w[k2] = max(w.get(k2, 0), v2)
        w[s["eP"]] = kv_used.get(kvi - 2, 0)
        c = 0
        for rr in range(2):
            c = r.on("sy", lambda e, rr=rr, kvi=kvi, par=par: e.dma_start(
                out=sb["kbuf"][:, par, rr * KTL:(rr + 1) * KTL, :],
                in_=kall[rr, kvi].rearrange("p (kt j) -> p kt j", j=128)),
                waits=w, inc=bld.dma_sem("kv", par), incn=16)
            c = r.on("sy", lambda e, rr=rr, kvi=kvi, par=par: e.dma_start(
                out=sb["vbuf"][:, par, rr * KTL:(rr + 1) * KTL, :],
                in_=vall[rr, kvi]),
                waits=w, inc=bld.dma_sem("kv", par), incn=16)
        kv_rd[kvi] = {bld.dma_sem("kv", par): c}

    def load_q(qi):
        if qi >= NKV * REP:
            return
        qpar = qi % 2
        qs = bld.dma_sem("q", qpar)
        qc = r.on("sy", lambda e, qi=qi, qpar=qpar: e.dma_start(
            out=sb["qbuf"][:, qpar, :], in_=d["qrope"].ap()[qi]),
            waits=dict(q_ready, **{s["eP"]: q_used.get(qi - 2, 0)}),
            inc=qs, incn=16)
        q_rd[qi] = {qs: qc}

    load_kv(0)
    load_kv(1)
    load_q(0)
    load_q(1)
    for kvi in range(NKV):
        par = kvi % 2
        for rep in range(REP):
            qi = kvi * REP + rep
            qpar = qi % 2
            for half in range(cfg.NQH):
                exp_ct = {}
                sc_ct = {}
                add_ct = {}
                pv_last = 0
                for step in range(KT + 2):
                    if step < KT:
                        kt = step
                        for j in (0, 1):
                            qb = 2 * half + j
                            bank = sbank[(kt % 2, j)]
                            if kt >= 2 and j == 0:
                                w = {s["eA"]: exp_ct[(1, kt - 2)]}
                            elif kt >= 2:
                                w = {}
                            else:
                                w = dict(kv_rd[kvi])
                                for k2, v2 in q_rd[qi].items():
                                    w[k2] = max(w.get(k2, 0), v2)
                                for k2, v2 in bld.psum_need(bank).items():
                                    w[k2] = max(w.get(k2, 0), v2)
                            c = r.on("pe", lambda e, bank=bank, par=par, kt=kt,
                                     qpar=qpar, qb=qb: e.matmul(
                                         bank[:, :], sb["kbuf"][:, par, kt, :],
                                         sb["qbuf"][:, qpar,
                                                    qb * QB:(qb + 1) * QB],
                                         start=True, stop=True),
                                     waits=w, inc=s["eP"])
                            sc_ct[(j, kt)] = c
                            ptslot = (kt * 2 + j) % 6
                            wv = dict(setup_cts["konst"])
                            wv[s["eP"]] = max(wv.get(s["eP"], 0), c)
                            for k2, v2 in pt_free.get(ptslot, {}).items():
                                wv[k2] = max(wv.get(k2, 0), v2)
                            cexp = r.on("ac", lambda e, bank=bank, ptslot=ptslot:
                                        e.activation(
                                            sb["pt"][:, ptslot, :], bank[:, :],
                                            AF.Exp, bias=sb["konst"][:, 0:1],
                                            scale=cfg.SCALE),
                                        waits=wv, inc=s["eA"])
                            exp_ct[(j, kt)] = cexp
                            bld.psum_done(bank, s["eA"], cexp)
                            eng, esem = (("ve", s["eV"]) if j == 0
                                         else ("po", s["eO"]))
                            wadd = {s["eA"]: cexp}
                            if kt == 0:
                                for k2, v2 in acc_free[j].items():
                                    wadd[k2] = max(wadd.get(k2, 0), v2)
                                cadd = r.on(eng, lambda e, j=j, ptslot=ptslot:
                                            e.tensor_copy(
                                                sb["acc"][:, j, :],
                                                sb["pt"][:, ptslot, :]),
                                            waits=wadd, inc=esem)
                            else:
                                cadd = r.on(eng, lambda e, j=j, ptslot=ptslot:
                                            e.tensor_add(
                                                sb["acc"][:, j, :],
                                                sb["acc"][:, j, :],
                                                sb["pt"][:, ptslot, :]),
                                            waits=wadd, inc=esem)
                            add_ct[(j, kt)] = (esem, cadd)
                    if step >= 2:
                        kt2 = step - 2
                        for j in (0, 1):
                            ptslot = (kt2 * 2 + j) % 6
                            pob = ps["po0"] if j == 0 else ps["po1"]
                            w = {s["eA"]: exp_ct[(1, kt2)]} if j == 0 else {}
                            if kt2 == 0:
                                for k2, v2 in bld.psum_need(pob).items():
                                    w[k2] = max(w.get(k2, 0), v2)
                            cpv = r.on("pe", lambda e, pob=pob, par=par,
                                       kt2=kt2, ptslot=ptslot: e.matmul(
                                           pob[:, :],
                                           sb["vbuf"][:, par, kt2, :],
                                           sb["pt"][:, ptslot, :],
                                           start=(kt2 == 0),
                                           stop=(kt2 == KT - 1)),
                                       waits=w, inc=s["eP"])
                            pv_last = cpv
                            asem, act_ = add_ct[(j, kt2)]
                            pt_free[ptslot] = {s["eP"]: cpv, asem: act_}
                kv_used[kvi] = pv_last
                q_used[qi] = pv_last
                for j in (0, 1):
                    qb = 2 * half + j
                    bcbank = sbank[(1, j)]
                    pob = ps["po0"] if j == 0 else ps["po1"]
                    asem, act_ = add_ct[(j, KT - 1)]
                    wred = {asem: act_}
                    for k2, v2 in bld.psum_need(ps["pl"]).items():
                        wred[k2] = max(wred.get(k2, 0), v2)
                    red = r.on("pe", lambda e, j=j: e.matmul(
                        ps["pl"][32 * j:32 * j + 1, :], sb["ones_f"][:, 0:1],
                        sb["acc"][:, j, :], start=True, stop=True),
                        waits=wred, inc=s["eP"])
                    acc_free[j] = {s["eP"]: red}
                    cl = r.on("ac", lambda e, j=j: e.activation(
                        sb["stat"][64:65, j * QB:(j + 1) * QB],
                        ps["pl"][32 * j:32 * j + 1, :], AF.Ln),
                        waits={s["eP"]: red}, inc=s["eA"])
                    bld.psum_done(ps["pl"], s["eA"], cl)
                    cr = r.on("ac", lambda e, j=j: e.activation(
                        sb["stat"][64:65, (2 + j) * QB:(3 + j) * QB],
                        sb["stat"][64:65, j * QB:(j + 1) * QB],
                        AF.Exp, scale=-1.0),
                        waits={s["eA"]: cl}, inc=s["eA"])
                    wbc = dict(setup_cts["konst"])
                    wbc[s["eA"]] = max(wbc.get(s["eA"], 0), cr)
                    for k2, v2 in bld.psum_need(bcbank).items():
                        wbc[k2] = max(wbc.get(k2, 0), v2)
                    cbc = r.on("pe", lambda e, bcbank=bcbank, j=j: e.matmul(
                        bcbank[:, :], sb["ones_f"][64:65, :],
                        sb["stat"][64:65, (2 + j) * QB:(3 + j) * QB],
                        start=True, stop=True),
                        waits=wbc, inc=s["eP"])
                    crb = r.on("ac", lambda e, bcbank=bcbank, j=j: e.copy(
                        sb["rb"][:, j, :], bcbank[:, :]),
                        waits={s["eP"]: cbc}, inc=s["eA"])
                    bld.psum_done(bcbank, s["eA"], crb)
                    if lay == "l0":
                        co = r.on("ve", lambda e, pob=pob, qi=qi, qb=qb, j=j:
                                  e.tensor_mul(
                                      sb["arena"][:, qi, qb * QB:(qb + 1) * QB],
                                      pob[:, :], sb["rb"][:, j, :]),
                                  waits={s["eA"]: crb, s["eP"]: pv_last},
                                  inc=s["eV"])
                        o_free[j] = co
                        bld.psum_done(pob, s["eV"], co)
                        out_eV = co
                    else:
                        ss = (qi * cfg.NQB + qb) % 4
                        wv = bld.need(("att_y", ss))
                        wv[s["eA"]] = max(wv.get(s["eA"], 0), crb)
                        wv[s["eP"]] = pv_last
                        co = r.on("ve", lambda e, pob=pob, ss=ss, j=j:
                                  e.tensor_mul(
                                      sb["att_y"][:, ss, :], pob[:, :],
                                      sb["rb"][:, j, :]),
                                  waits=wv, inc=s["eV"])
                        o_free[j] = co
                        bld.psum_done(pob, s["eV"], co)
                        ysem, yct = _dma(bld, r, "sy", "yw", ss,
                                         lambda e, qi=qi, qb=qb, ss=ss:
                                         e.dma_start(
                                             out=d["ybuf"].ap()[qi, qb],
                                             in_=sb["att_y"][:, ss, :]),
                                         waits={s["eV"]: co})
                        bld.set_guard(("att_y", ss), {ysem: yct})
                        y_out[ysem] = yct
            load_q(qi + 2)
        load_kv(kvi + 2)
    res = {"mm_last": bld.ct[s["eP"]], "out_eV": out_eV, "y_out": y_out}
    es.close()
    return res


def emit_y_proj(bld, r, s, wname, rhs_ready, pm, bias_ld=None,
                extra=None, wst_wait=None, arena_q=None, pst_slots=None,
                setup_cts=None):
    cfg, sb, d = bld.cfg, bld.sb, bld.dram
    DC, NQB, QB = cfg.DC, cfg.NQB, cfg.QB
    assert len(pm) >= NQB
    mm_last, evac = {}, {}
    s1p, s2p, slot_last = {}, {}, {}
    y_out = {}
    for oc in range(DC):
        par = oc % 2
        w = {s["eP"]: mm_last.get(oc - 2, 0)}
        if wst_wait:
            for k2, v2 in wst_wait.items():
                w[k2] = max(w.get(k2, 0), v2)
        wsem, wct = _dma(bld, r, "po", "w", par,
                         lambda e, oc=oc, par=par: e.dma_start(
                             out=sb["wst"][:, par, :, :], in_=d[wname].ap()[oc]),
                         waits=w)
        mmq = {}
        for qb in range(NQB):
            bank = pm[qb]
            last_stop = extra is None
            for dc in range(DC):
                w = {}
                if dc == 0:
                    w = dict(arena_q[qb]) if arena_q else dict(rhs_ready)
                    w[wsem] = max(w.get(wsem, 0), wct)
                    for k2, v2 in bld.psum_need(bank).items():
                        w[k2] = max(w.get(k2, 0), v2)
                c = r.on("pe", lambda e, par=par, dc=dc, qb=qb, bank=bank: e.matmul(
                    bank[:, :], sb["wst"][:, par, dc, :],
                    sb["arena"][:, dc, qb * QB:(qb + 1) * QB],
                    start=(dc == 0), stop=(last_stop and dc == DC - 1)),
                    waits=w, inc=s["eP"])
            mmq[qb] = c
        mm_last[oc] = mmq[NQB - 1]
        bld.set_guard(("w", par), {s["eP"]: mmq[NQB - 1]})
        if extra is not None:
            for qb in range(NQB):
                mmq[qb] = extra(oc, qb, pm[qb])
        for qb in range(NQB):
            idx = oc * NQB + qb
            bank = pm[qb]
            s4 = idx % 4
            wv = bld.need(("fy", s4))
            wv[s["eP"]] = max(wv.get(s["eP"], 0), mmq[qb])
            if bias_ld is not None:
                for k2, v2 in bias_ld.items():
                    wv[k2] = max(wv.get(k2, 0), v2)
                ce = r.on("ac", lambda e, bank=bank, s4=s4, oc=oc: e.activation(
                    sb["fy"][:, s4, :], bank[:, :], AF.Identity,
                    bias=sb["bias_sb"][:, oc:oc + 1], scale=1.0),
                    waits=wv, inc=s["eA"])
            else:
                ce = r.on("ac", lambda e, bank=bank, s4=s4: e.copy(
                    sb["fy"][:, s4, :], bank[:, :]), waits=wv, inc=s["eA"])
            evac[idx] = ce
            bld.psum_done(bank, s["eA"], ce)
            ysem, yct = _dma(bld, r, "sy", "yw", s4,
                             lambda e, oc=oc, qb=qb, s4=s4: e.dma_start(
                                 out=d["ybuf"].ap()[oc, qb],
                                 in_=sb["fy"][:, s4, :]),
                             waits={s["eA"]: ce})
            fg = {ysem: yct}
            if pst_slots is not None:
                s2y = idx % 2
                cq = r.on("ac", lambda e, s4=s4, s2y=s2y: e.square(
                    sb["yb2"][:, s2y, :], sb["fy"][:, s4, :]),
                    waits={s["eP"]: s2p.get(idx - 2, 0)}, inc=s["eA"])
                b1, m1 = pst_slots[2 * qb]
                b2, m2 = pst_slots[2 * qb + 1]
                w1 = dict(setup_cts["konst"])
                w1[s["eA"]] = max(w1.get(s["eA"], 0), ce)
                if oc == 0:
                    for k2, v2 in bld.psum_need(b1).items():
                        w1[k2] = max(w1.get(k2, 0), v2)
                c1 = r.on("pe", lambda e, b1=b1, m1=m1, s4=s4, oc=oc: e.matmul(
                    b1[m1:m1 + 1, :], sb["ones_f"][:, 0:1],
                    sb["fy"][:, s4, :],
                    start=(oc == 0), stop=(oc == DC - 1)),
                    waits=w1, inc=s["eP"])
                w2b = {s["eA"]: cq}
                if oc == 0:
                    for k2, v2 in bld.psum_need(b2).items():
                        w2b[k2] = max(w2b.get(k2, 0), v2)
                c2 = r.on("pe", lambda e, b2=b2, m2=m2, s2y=s2y, oc=oc:
                          e.matmul(
                              b2[m2:m2 + 1, :], sb["ones_b"][:, :],
                              sb["yb2"][:, s2y, :],
                              start=(oc == 0), stop=(oc == DC - 1)),
                          waits=w2b, inc=s["eP"])
                s1p[idx] = c1
                s2p[idx] = c2
                slot_last[2 * qb] = c1
                slot_last[2 * qb + 1] = c2
                fg[s["eA"]] = cq
                fg[s["eP"]] = c1
            bld.set_guard(("fy", s4), fg)
            y_out[ysem] = yct
    return {"y_out": y_out, "mm_last": max(mm_last.values()),
            "slot_last": slot_last}


def emit_ffn_pre(bld, r, s, lay, arena_rdy, setup_cts, region_gate=None,
                 arena_q=None):
    cfg, sb, d = bld.cfg, bld.sb, bld.dram
    DC, NQB, QB, TPQ, NTT, E = cfg.DC, cfg.NQB, cfg.QB, cfg.TPQ, cfg.NTT, cfg.E
    la = f"{lay}f"
    es, ps = bld.psum_scope(
        [("plg", (128, QB), F32), ("ptt", (128, 128), F32),
         ("ptt2", (128, QB), F32), ("pwx", (128, QB), F32),
         ("ph0", (128, QB), F32), ("ph1", (128, QB), F32)])
    gW = dict(region_gate) if region_gate else dict(arena_rdy)
    wg_ld = dict([_dma(bld, r, "po", "wg", 0, lambda e: e.dma_start(
        out=sb["wg_sb"][:, :, :], in_=d[f"{la}_wg"].ap()), waits=gW)])
    sel_ld = dict([_dma(bld, r, "po", "sel", 0, lambda e: e.dma_start(
        out=sb["sel_sb"][:, :], in_=d[f"{la}_sel"].ap()), waits=gW)])
    a_ld = dict([_dma(bld, r, "po", "wa", 0, lambda e: e.dma_start(
        out=sb["ffn_a"][:, :, :], in_=d[f"{la}_wa"].ap()), waits=gW)])
    b_ld = dict([_dma(bld, r, "po", "wbm", 0, lambda e: e.dma_start(
        out=sb["ffn_b"][:, :], in_=d[f"{la}_wm"].ap()), waits=dict(arena_rdy))])
    bias_ld = dict([_dma(bld, r, "sy", "fbias", 0, lambda e: e.dma_start(
        out=sb["bias_sb"][:, :], in_=d[f"{la}_bb"].ap()), waits=gW)])

    lg_ct = {}
    tr_last = 0
    for qb in range(NQB):
        sq = qb % 2
        for dc in range(DC):
            w = {}
            if dc == 0:
                w = dict(wg_ld)
                for k2, v2 in (arena_q[qb] if arena_q else arena_rdy).items():
                    w[k2] = max(w.get(k2, 0), v2)
                for k2, v2 in bld.psum_need(ps["plg"]).items():
                    w[k2] = max(w.get(k2, 0), v2)
            c = r.on("pe", lambda e, dc=dc, qb=qb: e.matmul(
                ps["plg"][0:E, :], sb["wg_sb"][:, dc, :],
                sb["arena"][:, dc, qb * QB:(qb + 1) * QB],
                start=(dc == 0), stop=(dc == DC - 1)),
                waits=w, inc=s["eP"])
        ce = r.on("ac", lambda e, sq=sq: e.copy(
            sb["lgs"][:, sq, :], ps["plg"][0:E, :]),
            waits={s["eP"]: c}, inc=s["eA"])
        lg_ct[qb] = ce
        bld.psum_done(ps["plg"], s["eA"], ce)
        wv = dict(setup_cts["iden_f"])
        wv[s["eA"]] = ce
        for k2, v2 in bld.psum_need(ps["ptt"]).items():
            wv[k2] = max(wv.get(k2, 0), v2)
        for t in range(TPQ):
            tt = qb * TPQ + t
            tr_last = r.on("pe", lambda e, sq=sq, t=t, tt=tt: e.transpose(
                ps["ptt"][:, tt * E:(tt + 1) * E],
                sb["lgs"][:, sq, t * 128:(t + 1) * 128],
                sb["iden_fsb"][0:E, 0:E]),
                waits=wv, inc=s["eP"])
    clg = r.on("ac", lambda e: e.copy(sb["lg_tm"][:, :], ps["ptt"][:, 0:NTT * E]),
               waits={s["eP"]: tr_last}, inc=s["eA"])
    bld.psum_done(ps["ptt"], s["eA"], clg)

    hprev = {}
    for qb in range(NQB):
        phb = ps["ph0"] if qb % 2 == 0 else ps["ph1"]
        for dc in range(DC):
            w = {}
            if dc == 0:
                w = dict(a_ld)
                for k2, v2 in (arena_q[qb] if arena_q else arena_rdy).items():
                    w[k2] = max(w.get(k2, 0), v2)
                for k2, v2 in bld.psum_need(phb).items():
                    w[k2] = max(w.get(k2, 0), v2)
            c = r.on("pe", lambda e, dc=dc, qb=qb, phb=phb: e.matmul(
                phb[:, :], sb["ffn_a"][:, dc, :],
                sb["arena"][:, dc, qb * QB:(qb + 1) * QB],
                start=(dc == 0), stop=(dc == DC - 1)),
                waits=w, inc=s["eP"])
        ch = r.on("ac", lambda e, qb=qb, phb=phb: e.copy(
            sb["ffn_h"][:, qb * QB:(qb + 1) * QB], phb[:, :]),
            waits={s["eP"]: c}, inc=s["eA"])
        hprev[qb] = ch
        bld.psum_done(phb, s["eA"], ch)

    mprev = 0
    for tt in range(NTT):
        c1 = r.on("ve", lambda e, tt=tt: e.reduce_max(
            sb["r16"][:, 0, tt:tt + 1], sb["lg_tm"][:, tt * E:(tt + 1) * E],
            axis=AX.X), waits={s["eA"]: clg}, inc=s["eV"])
        c2 = r.on("ve", lambda e, tt=tt: e.tensor_scalar(
            sb["r8"][:, 0, :], sb["lg_tm"][:, tt * E:(tt + 1) * E],
            sb["r16"][:, 0, tt:tt + 1], None, ALU.is_equal),
            waits={s["eV"]: c1}, inc=s["eV"])
        c3 = r.on("ve", lambda e, tt=tt: e.scalar_tensor_tensor(
            sb["t_tm"][:, tt * E:(tt + 1) * E], sb["r8"][:, 0, :], -1e30,
            sb["lg_tm"][:, tt * E:(tt + 1) * E], ALU.mult, ALU.add),
            waits={s["eV"]: c2}, inc=s["eV"])
        mprev = r.on("ve", lambda e, tt=tt: e.reduce_max(
            sb["r16"][:, 1, tt:tt + 1], sb["t_tm"][:, tt * E:(tt + 1) * E],
            axis=AX.X), waits={s["eV"]: c3}, inc=s["eV"])
    cp = r.on("ve", lambda e: e.tensor_sub(
        sb["r16"][:, 2, :], sb["r16"][:, 1, :], sb["r16"][:, 0, :]),
        waits={s["eV"]: mprev}, inc=s["eV"])
    cpe = r.on("ac", lambda e: e.activation(
        sb["r16"][:, 3, :], sb["r16"][:, 2, :], AF.Exp),
        waits={s["eV"]: cp}, inc=s["eA"])
    cd = r.on("ac", lambda e: e.add(sb["r16"][:, 4, :], sb["r16"][:, 3, :], 1.0),
              waits={s["eA"]: cpe}, inc=s["eA"])
    crec = r.on("ve", lambda e: e.reciprocal(sb["r16"][:, 5, :], sb["r16"][:, 4, :]),
                waits={s["eA"]: cd}, inc=s["eV"])
    cw2 = r.on("ve", lambda e: e.tensor_mul(
        sb["r16"][:, 2, :], sb["r16"][:, 3, :], sb["r16"][:, 5, :]),
        waits={s["eV"]: crec}, inc=s["eV"])
    wlast = cw2
    for tt in range(NTT):
        c1 = r.on("ve", lambda e, tt=tt: e.tensor_scalar(
            sb["r8"][:, 0, :], sb["lg_tm"][:, tt * E:(tt + 1) * E],
            sb["r16"][:, 0, tt:tt + 1], None, ALU.is_equal),
            waits={s["eV"]: cw2}, inc=s["eV"])
        c2 = r.on("ve", lambda e, tt=tt: e.tensor_scalar(
            sb["r8"][:, 1, :], sb["t_tm"][:, tt * E:(tt + 1) * E],
            sb["r16"][:, 1, tt:tt + 1], None, ALU.is_equal),
            waits={s["eV"]: c1}, inc=s["eV"])
        c3 = r.on("ve", lambda e, tt=tt: e.tensor_scalar(
            sb["r8"][:, 2, :], sb["r8"][:, 0, :],
            sb["r16"][:, 5, tt:tt + 1], None, ALU.mult),
            waits={s["eV"]: c2}, inc=s["eV"])
        wlast = r.on("ve", lambda e, tt=tt: e.scalar_tensor_tensor(
            sb["w_tm"][:, tt * E:(tt + 1) * E], sb["r8"][:, 1, :],
            sb["r16"][:, 2, tt:tt + 1], sb["r8"][:, 2, :], ALU.mult, ALU.add),
            waits={s["eV"]: c3}, inc=s["eV"])
    hw_last = 0
    wT_ct = {}
    for qb in range(NQB):
        trl = 0
        wv = dict(setup_cts["iden_f"])
        wv[s["eV"]] = wlast
        for k2, v2 in bld.psum_need(ps["ptt2"]).items():
            wv[k2] = max(wv.get(k2, 0), v2)
        for t in range(TPQ):
            tt = qb * TPQ + t
            trl = r.on("pe", lambda e, t=t, tt=tt: e.transpose(
                ps["ptt2"][0:E, t * 128:(t + 1) * 128],
                sb["w_tm"][:, tt * E:(tt + 1) * E], sb["iden_fsb"][:, :]),
                waits=wv, inc=s["eP"])
        cwt = r.on("ac", lambda e, qb=qb: e.copy(
            sb["wTb"][:, qb * QB:(qb + 1) * QB], ps["ptt2"][0:E, 0:QB]),
            waits={s["eP"]: trl}, inc=s["eA"])
        wT_ct[qb] = cwt
        bld.psum_done(ps["ptt2"], s["eA"], cwt)
    hw_q = {}
    for qb in range(NQB):
        wv = dict(sel_ld)
        wv[s["eA"]] = wT_ct[qb]
        for k2, v2 in bld.psum_need(ps["pwx"]).items():
            wv[k2] = max(wv.get(k2, 0), v2)
        cwx = r.on("pe", lambda e, qb=qb: e.matmul(
            ps["pwx"][:, :], sb["sel_sb"][:, :],
            sb["wTb"][:, qb * QB:(qb + 1) * QB], start=True, stop=True),
            waits=wv, inc=s["eP"])
        hw_last = r.on("ve", lambda e, qb=qb: e.tensor_mul(
            sb["ffn_hw"][:, qb * QB:(qb + 1) * QB],
            sb["ffn_h"][:, qb * QB:(qb + 1) * QB], ps["pwx"][:, :]),
            waits={s["eA"]: hprev[qb], s["eP"]: cwx}, inc=s["eV"])
        hw_q[qb] = hw_last
        bld.psum_done(ps["pwx"], s["eV"], hw_last)
    import os as _os
    if _os.environ.get("MK_DBG_RT") == "1":
        for nm, t in (("dbg_lgtm", sb["lg_tm"]), ("dbg_wtm", sb["w_tm"]),
                      ("dbg_hw", sb["ffn_hw"]), ("dbg_h", sb["ffn_h"]),
                      ("dbg_wtb", sb["wTb"])):
            ap = bld.dram[nm].ap()
            r.on("sy", lambda e, ap=ap, t=t: e.dma_start(
                out=ap, in_=t[:, :] if len(t.shape) == 2 else t[:, :, :]),
                waits={s["eV"]: hw_last}, inc=bld.dma_sem("dbg", 0), incn=16)
    es.close()
    return {"hw_eV": hw_last, "hw_q": hw_q, "b_ld": b_ld, "bias_ld": bias_ld}


def emit_epilogue(bld, r, s, xin, yin, xout, ba_ct, pst_slots, want_stats,
                  setup_cts, gate=None):
    cfg, sb = bld.cfg, bld.sb
    DC, NQB, QB = cfg.DC, cfg.NQB, cfg.QB
    slot_last = {}
    s1p, s2p = {}, {}
    out_sems = {}
    for qb in range(NQB):
        for dc in range(DC):
            idx = qb * DC + dc
            s4, s2 = idx % 4, idx % 2
            w = {s["eV"]: ba_ct}
            if gate:
                for k2, v2 in gate.items():
                    w[k2] = max(w.get(k2, 0), v2)
            xsem, dx = _dma(bld, r, "sy", "x", s4,
                            lambda e, dc=dc, qb=qb, s4=s4: e.dma_start(
                                out=sb["ep_x"][:, s4, :],
                                in_=xin[dc, qb]),
                            waits=w)
            ysem, dy = _dma(bld, r, "sy", "y_in", s4,
                            lambda e, dc=dc, qb=qb, s4=s4: e.dma_start(
                                out=sb["ep_y"][:, s4, :],
                                in_=yin[dc, qb]),
                            waits=w)
            ct = r.on("ve", lambda e, s4=s4, qb=qb: e.tensor_mul(
                sb["ep_y"][:, s4, :], sb["ep_y"][:, s4, :],
                sb["Ba"][:, qb * QB:(qb + 1) * QB]),
                waits={ysem: dy, s["eV"]: ba_ct}, inc=s["eV"])
            co = r.on("po", lambda e, s4=s4, qb=qb: e.tensor_tensor(
                sb["ep_x"][:, s4, :], sb["ep_x"][:, s4, :],
                sb["Bc"][:, qb * QB:(qb + 1) * QB], ALU.subtract),
                waits={xsem: dx, s["eV"]: ba_ct}, inc=s["eO"])
            cx = r.on("ve", lambda e, s4=s4: e.tensor_add(
                sb["ep_x"][:, s4, :], sb["ep_x"][:, s4, :],
                sb["ep_y"][:, s4, :]),
                waits={s["eO"]: co}, inc=s["eV"])
            bld.set_guard(("y_in", s4), {s["eV"]: cx})
            osem, oct = _dma(bld, r, "sy", "xo", s4,
                             lambda e, dc=dc, qb=qb, s4=s4: e.dma_start(
                                 out=xout[dc, qb],
                                 in_=sb["ep_x"][:, s4, :]),
                             waits={s["eV"]: cx})
            out_sems[osem] = oct
            xg = {osem: oct}
            if want_stats:
                cq = r.on("ac", lambda e, s4=s4, s2=s2: e.square(
                    sb["ep_b2"][:, s2, :], sb["ep_x"][:, s4, :]),
                    waits={s["eV"]: cx, s["eP"]: s2p.get(idx - 2, 0)},
                    inc=s["eA"])
                b1, m1 = pst_slots[2 * qb]
                b2, m2 = pst_slots[2 * qb + 1]
                w1 = dict(setup_cts["konst"])
                w1[s["eV"]] = max(w1.get(s["eV"], 0), cx)
                if dc == 0:
                    for k2, v2 in bld.psum_need(b1).items():
                        w1[k2] = max(w1.get(k2, 0), v2)
                c1 = r.on("pe", lambda e, b1=b1, m1=m1, s4=s4, dc=dc: e.matmul(
                    b1[m1:m1 + 1, :], sb["ones_f"][:, 0:1],
                    sb["ep_x"][:, s4, :],
                    start=(dc == 0), stop=(dc == DC - 1)),
                    waits=w1, inc=s["eP"])
                w2b = {s["eA"]: cq}
                if dc == 0:
                    for k2, v2 in bld.psum_need(b2).items():
                        w2b[k2] = max(w2b.get(k2, 0), v2)
                c2 = r.on("pe", lambda e, b2=b2, m2=m2, s2=s2, dc=dc: e.matmul(
                    b2[m2:m2 + 1, :], sb["ones_b"][:, :], sb["ep_b2"][:, s2, :],
                    start=(dc == 0), stop=(dc == DC - 1)),
                    waits=w2b, inc=s["eP"])
                s1p[idx] = c1
                s2p[idx] = c2
                slot_last[2 * qb] = c1
                slot_last[2 * qb + 1] = c2
                xg[s["eA"]] = cq
                xg[s["eP"]] = c1
            bld.set_guard(("x", s4), xg)
    return out_sems, (slot_last if want_stats else None)


def emit_block(bld, r, blk, bi, prev, is_last):
    cfg = bld.cfg
    lay = blk[:2].lower()
    is_attn = blk.endswith("A")
    s = bld.blk_sems(blk)
    xin = bld.xbuf(bi)
    xnext = bld.xbuf(bi + 1)
    d, sb = bld.dram, bld.sb

    setup_cts = emit_setup(bld, r, s) if prev is None else prev["setup"]

    if prev is None or "stats_ready" not in prev or FORCE_STATS0:
        es0, ps0 = bld.psum_scope(
            [("st0", (128, cfg.QB), F32), ("st1", (128, cfg.QB), F32),
             ("st2", (128, cfg.QB), F32)])
        slots0 = stat_slots(cfg, [ps0["st0"], ps0["st1"], ps0["st2"]])
        c = emit_stats0(bld, r, s, xin, slots0, setup_cts,
                        gate=(dict(prev["x_ready"]) if prev and "x_ready" in prev
                              else None))
        es0.close()
        stats_ready = {s["eA"]: c}
    else:
        stats_ready = prev["stats_ready"]

    esb, pbb = bld.psum_scope([("b0", (128, cfg.QB), F32),
                               ("b1", (128, cfg.QB), F32)])
    ba = emit_ln_finalize(bld, r, s, stats_ready, pbb, setup_cts)
    esb.close()
    extra = dict(prev.get("arena_free", {})) if prev else {}
    if prev and "x_ready" in prev:
        for k, v in prev["x_ready"].items():
            extra[k] = max(extra.get(k, 0), v)
    epi_gate = dict(prev.get("epi_done", {})) if prev else None

    if is_attn:
        kw = "l0a_wk" if lay == "l0" else "l1a_wk"
        pre_w = {}
        for oc in (0, 1):
            w = dict(prev.get("wst_free", {})) if prev else {}
            pre_w[oc] = _dma(bld, r, "po", "w", oc % 2,
                             lambda e, oc=oc: e.dma_start(
                                 out=bld.sb["wst"][:, oc % 2, :, :],
                                 in_=d[kw].ap()[oc]),
                             waits=w)
    arena_rdy, arena_q = emit_ln_apply(bld, r, s, xin, ba,
                                       extra_wait=extra or None)
    wst_free = dict(prev.get("wst_free", {})) if prev else {}

    if is_attn:
        rl = _dma(bld, r, "sy", "rope", 0, lambda e: e.dma_start(
            out=sb["ropeCsb"][:, :], in_=d["ropeC"].ap()),
            waits=dict(arena_rdy))
        rl2 = _dma(bld, r, "sy", "rope", 0, lambda e: e.dma_start(
            out=sb["ropeSsb"][:, :], in_=d["ropeS"].ap()))
        rope_ld = {rl2[0]: rl2[1]}
        bias_ld = {}
        if lay == "l1":
            for src, dst in (("l1a_bq", "b_q"), ("l1a_bk", "b_k"),
                             ("l1a_bv", "b_v")):
                bl = _dma(bld, r, "sy", "abias", 0,
                          lambda e, src=src, dst=dst: e.dma_start(
                              out=sb[dst][:, :], in_=d[src].ap()),
                          waits=(epi_gate or arena_rdy))
            bias_ld = {bl[0]: bl[1]}
        esp, psp = bld.psum_scope(
            [("pp0", (128, cfg.QB), F32), ("pp1", (128, cfg.QB), F32),
             ("ptr", (128, 1024), BF16)])
        pp = [psp["pp0"], psp["pp1"]]
        kres = emit_proj(bld, r, s, lay, "k", arena_rdy, pp, psp["ptr"],
                         setup_cts, bias_ld, rope_ld, wst_free,
                         preloaded=pre_w, arena_q=arena_q)
        vres = emit_proj(bld, r, s, lay, "v", arena_rdy, pp, psp["ptr"],
                         setup_cts, bias_ld, rope_ld, wst_free,
                         arena_q=arena_q)
        nkv = cfg.NH if lay == "l0" else cfg.NKV1
        kv_cc = emit_exchange(bld, r, s, kres["out"], vres["out"], nkv)
        qres = emit_proj(bld, r, s, lay, "q", arena_rdy, pp, psp["ptr"],
                         setup_cts, bias_ld, rope_ld, wst_free,
                         arena_q=arena_q)
        esp.close()

        if lay == "l1":
            ares = emit_attention(bld, r, s, lay, setup_cts, kv_cc,
                                  qres["out"])
            y_gate = dict(ares["y_out"])
            arena_reader = {s["eP"]: ares["mm_last"], s["eA"]: bld.ct[s["eA"]],
                            s["eV"]: bld.ct[s["eV"]]}
            wst_next = {s["eP"]: ares["mm_last"], s["eV"]: bld.ct[s["eV"]]}
        else:
            ares = emit_attention(bld, r, s, lay, setup_cts, kv_cc, qres["out"])
            esy, psy = bld.psum_scope(
                [("st0", (128, cfg.QB), F32), ("st1", (128, cfg.QB), F32),
                 ("st2", (128, cfg.QB), F32)])
            slotsy = stat_slots(cfg, [psy["st0"], psy["st1"], psy["st2"]])
            espm, pspm = bld.psum_scope(
                [("pm0", (128, cfg.QB), F32), ("pm1", (128, cfg.QB), F32),
                 ("pm2", (128, cfg.QB), F32), ("pm3", (128, cfg.QB), F32)])
            wres = emit_y_proj(
                bld, r, s, "l0a_wo", {s["eV"]: ares["out_eV"]},
                [pspm["pm0"], pspm["pm1"], pspm["pm2"], pspm["pm3"]],
                wst_wait={s["eP"]: ares["mm_last"], s["eV"]: ares["out_eV"]},
                pst_slots=slotsy, setup_cts=setup_cts)
            espm.close()
            stats_eA = emit_stats_evac(bld, r, s, slotsy, wres["slot_last"])
            esy.close()
            y_gate = dict(wres["y_out"])
            arena_reader = {s["eP"]: wres["mm_last"]}
            wst_next = {s["eP"]: wres["mm_last"]}
    else:
        pre = emit_ffn_pre(bld, r, s, lay, arena_rdy, setup_cts,
                           region_gate=epi_gate, arena_q=arena_q)
        esy, psy = bld.psum_scope(
            [("st0", (128, cfg.QB), F32), ("st1", (128, cfg.QB), F32),
             ("st2", (128, cfg.QB), F32)])
        slotsy = stat_slots(cfg, [psy["st0"], psy["st1"], psy["st2"]])
        espm, pspm = bld.psum_scope(
            [("pm0", (128, cfg.QB), F32), ("pm1", (128, cfg.QB), F32),
             ("pm2", (128, cfg.QB), F32), ("pm3", (128, cfg.QB), F32)])

        def extra2(oc, qb, bank):
            w = dict(pre["b_ld"])
            w[s["eV"]] = pre["hw_q"][qb]
            return r.on("pe", lambda e, oc=oc, qb=qb, bank=bank: e.matmul(
                bank[:, :], sb["ffn_b"][:, oc * 128:(oc + 1) * 128],
                sb["ffn_hw"][:, qb * cfg.QB:(qb + 1) * cfg.QB],
                start=False, stop=True),
                waits=w, inc=s["eP"])

        import os as _os
        wres = emit_y_proj(bld, r, s, f"{lay}f_wb", dict(arena_rdy),
                           [pspm["pm0"], pspm["pm1"], pspm["pm2"],
                            pspm["pm3"]],
                           bias_ld=pre["bias_ld"],
                           extra=(None if _os.environ.get("MK_NODELTA") == "1"
                                  else extra2),
                           wst_wait=wst_free, arena_q=arena_q,
                           pst_slots=slotsy, setup_cts=setup_cts)
        espm.close()
        stats_eA = emit_stats_evac(bld, r, s, slotsy, wres["slot_last"])
        esy.close()
        y_gate = dict(wres["y_out"])
        arena_reader = {s["eP"]: wres["mm_last"]}
        wst_next = {s["eP"]: wres["mm_last"]}

    if is_attn and lay == "l1":
        esy, psy = bld.psum_scope(
            [("st0", (128, cfg.QB), F32), ("st1", (128, cfg.QB), F32),
             ("st2", (128, cfg.QB), F32)])
        slotsy = stat_slots(cfg, [psy["st0"], psy["st1"], psy["st2"]])
        stats_eA = emit_stats0(bld, r, s, d["ybuf"].ap(), slotsy, setup_cts,
                               gate=y_gate)
        esy.close()

    esb2, pbb2 = bld.psum_scope([("b0", (128, cfg.QB), F32),
                                 ("b1", (128, cfg.QB), F32)])
    ba2 = emit_ln_finalize(bld, r, s, {s["eA"]: stats_eA}, pbb2, setup_cts)
    esb2.close()
    want_stats = (not is_last) and not FORCE_STATS0
    if want_stats:
        es3, ps3 = bld.psum_scope(
            [("st0", (128, cfg.QB), F32), ("st1", (128, cfg.QB), F32),
             ("st2", (128, cfg.QB), F32)])
        slots3 = stat_slots(cfg, [ps3["st0"], ps3["st1"], ps3["st2"]])
    else:
        slots3 = None
    out_sems, slot_last = emit_epilogue(
        bld, r, s, xin, d["ybuf"].ap(), xnext, ba2, slots3, want_stats,
        setup_cts, gate=y_gate)
    handoff = {"setup": setup_cts, "wst_free": wst_next,
               "arena_free": arena_reader,
               "x_ready": dict(out_sems), "sems": s,
               "epi_done": {s["eV"]: bld.ct[s["eV"]], s["eO"]: bld.ct[s["eO"]],
                            s["eA"]: bld.ct[s["eA"]], s["eP"]: bld.ct[s["eP"]]}}
    if want_stats:
        c = emit_stats_evac(bld, r, s, slots3, slot_last)
        es3.close()
        handoff["stats_ready"] = {s["eA"]: c}
    if is_last:
        import os as _os
        if _os.environ.get("MK_DBG_AR") == "1":
            r.on("sy", lambda e: e.dma_start(
                out=bld.dram["dbg_arena"].ap(),
                in_=bld.sb["arena"][:, :, :]),
                waits=dict(out_sems), inc=bld.dma_sem("dbg", 1), incn=16)
        r.on("sy", lambda e: None, waits=dict(out_sems))
    return handoff


def build_program(cfg, blocks):
    bld = Builder(cfg, blocks)
    with bld.nc.Block() as blk:
        prev = None
        for bi, b in enumerate(blocks):
            r = Rec(bld)
            prev = emit_block(bld, r, b, bi, prev, bi == len(blocks) - 1)
            bld.flush(blk, r)
    return bld


MODE_GROUPS = [["L0A", "L0F", "L1A", "L1F"]]
_CACHE = {}


def _get_builder(cfg_key, blocks):
    key = (cfg_key, tuple(blocks))
    if key not in _CACHE:
        _CACHE[key] = build_program(Cfg(mini=(cfg_key == "mini")), list(blocks))
    return _CACHE[key]


def kernel(**inputs):
    cfg = Cfg(mini=False)
    shared = prep_shared(cfg, inputs)
    x_full = np.asarray(inputs["L0_attn_x"], dtype=np.float32)
    B = x_full.shape[0]
    ncores = 8
    core_x = []
    ropes = []
    for c in range(ncores):
        b, half = c // 2, c % 2
        slab = x_full[b, half * cfg.TOK:(half + 1) * cfg.TOK, :].T
        core_x.append(np.ascontiguousarray(
            slab.reshape(cfg.DC, 128, cfg.NQB, cfg.QB).transpose(0, 2, 1, 3)))
        ropes.append(_rope_tables(cfg, half))

    for blocks in MODE_GROUPS:
        bld = _get_builder("full", blocks)
        names = [n for n, t in bld.dram.items()
                 if bld.nc.lookup_mls(t).kind == "ExternalInput"]
        in_maps = []
        for c in range(ncores):
            m = {}
            for n in names:
                if n == "x0":
                    m[n] = core_x[c]
                elif n == "ropeC":
                    m[n] = ropes[c][0]
                elif n == "ropeS":
                    m[n] = ropes[c][1]
                elif n == "iden_b":
                    m[n] = shared["iden_b"]
                else:
                    m[n] = shared[n]
            in_maps.append(m)
        res = run_bass_kernel_spmd(bld.nc, in_maps, core_ids=list(range(ncores)))
        core_x = [np.asarray(res.results[c]["xout"]) for c in range(ncores)]

    out = np.empty((B, 2 * cfg.TOK, cfg.H), dtype=np.float32)
    for c in range(ncores):
        b, half = c // 2, c % 2
        slab = np.asarray(core_x[c]).transpose(0, 2, 1, 3).reshape(
            cfg.H, cfg.TOK)
        out[b, half * cfg.TOK:(half + 1) * cfg.TOK, :] = slab.T
    return out


# version 14
# speedup vs baseline: 1.0861x; 1.0861x over previous
import math
from contextlib import ExitStack

import numpy as np
import ml_dtypes

import concourse.bass as bass
import concourse.mybir as mybir
from concourse.bass_utils import run_bass_kernel_spmd

F32 = mybir.dt.float32
BF16 = mybir.dt.bfloat16
AF = mybir.ActivationFunctionType
ALU = mybir.AluOpType
AX = mybir.AxisListType
BF = ml_dtypes.bfloat16

ENGS = ("sy", "po", "pe", "ac", "ve")
FORCE_STATS0 = False


class Cfg:
    def __init__(self, mini=False):
        if mini:
            self.H, self.TOK, self.NH, self.NKV1, self.QB = 512, 512, 4, 2, 256
        else:
            self.H, self.TOK, self.NH, self.NKV1, self.QB = 4096, 2048, 32, 8, 512
        self.E, self.R = 8, 16
        self.DC = self.H // 128
        self.NQB = self.TOK // self.QB
        self.S = 2 * self.TOK
        self.KT = self.S // 128
        self.KTL = self.TOK // 128
        self.ER = self.E * self.R
        self.REP1 = self.NH // self.NKV1
        assert self.NQB % 2 == 0
        self.NQH = self.NQB // 2
        self.NTT = self.TOK // 128
        self.TPQ = self.QB // 128
        self.SCALE = 1.0 / math.sqrt(128.0)
        self.CBIAS = -30.0
        self.EPS = 1e-5


def _tile_st(Wt):
    d, o = Wt.shape
    dcn, ocn = d // 128, o // 128
    return np.ascontiguousarray(
        Wt.reshape(dcn, 128, ocn, 128).transpose(2, 1, 0, 3)).astype(np.float32)


def _col_bias(b):
    return np.ascontiguousarray(np.asarray(b, np.float32).reshape(-1, 128).T)


def _rope_tables(cfg, half):
    pos = np.arange(cfg.TOK, dtype=np.float64) + half * cfg.TOK
    j = np.arange(64, dtype=np.float64)
    inv = 10000.0 ** (-j / 64.0)
    ang = pos[None, :] * inv[:, None]
    c, s = np.cos(ang), np.sin(ang)
    C = np.concatenate([c, c], 0).astype(np.float32)
    S = np.concatenate([-s, s], 0).astype(np.float32)
    return np.ascontiguousarray(C), np.ascontiguousarray(S)


def _sel_mat(cfg, scaling):
    sel = np.zeros((cfg.E, cfg.ER), np.float32)
    for e in range(cfg.E):
        sel[e, e * cfg.R:(e + 1) * cfg.R] = scaling
    return sel


def prep_shared(cfg, inp):
    f = lambda a: np.asarray(a, dtype=np.float32)
    d = {}
    d["l0a_wq"] = _tile_st(f(inp["L0_attn_Wq"]).T)
    d["l0a_wk"] = _tile_st(f(inp["L0_attn_Wk"]).T)
    d["l0a_wv"] = _tile_st(f(inp["L0_attn_Wv"]).T)
    d["l0a_wo"] = _tile_st(f(inp["L0_attn_Wo"]).T)
    d["l0f_wb"] = _tile_st(f(inp["L0_ffn_W_base"]).T)
    d["l0f_bb"] = _col_bias(inp["L0_ffn_b_base"])
    d["l0f_wg"] = np.ascontiguousarray(
        f(inp["L0_ffn_W_gate"]).T.reshape(cfg.DC, 128, cfg.E).transpose(1, 0, 2))
    d["l0f_wa"] = _tile_st(f(inp["L0_ffn_A"]).reshape(cfg.ER, cfg.H).T)[0]
    d["l0f_wm"] = np.ascontiguousarray(
        f(inp["L0_ffn_B"]).transpose(0, 2, 1).reshape(cfg.ER, cfg.H))
    d["l0f_sel"] = _sel_mat(cfg, 1.0)
    d["l1a_wq"] = _tile_st(f(inp["L1_attn_wq"]))
    d["l1a_bq"] = _col_bias(inp["L1_attn_bq"])
    d["l1a_wk"] = _tile_st(f(inp["L1_attn_wk"]))
    d["l1a_bk"] = _col_bias(inp["L1_attn_bk"])
    d["l1a_wv"] = _tile_st(f(inp["L1_attn_wv"]))
    d["l1a_bv"] = _col_bias(inp["L1_attn_bv"])
    d["l1f_wb"] = _tile_st(f(inp["L1_ffn_W_base"]).T)
    d["l1f_bb"] = _col_bias(inp["L1_ffn_b_base"])
    d["l1f_wg"] = np.ascontiguousarray(
        f(inp["L1_ffn_W_router"]).T.reshape(cfg.DC, 128, cfg.E).transpose(1, 0, 2))
    d["l1f_wa"] = _tile_st(f(inp["L1_ffn_A"]).reshape(cfg.ER, cfg.H).T)[0]
    d["l1f_wm"] = np.ascontiguousarray(
        f(inp["L1_ffn_B"]).transpose(0, 2, 1).reshape(cfg.ER, cfg.H))
    d["l1f_sel"] = _sel_mat(cfg, 2.0)
    d["iden_f"] = np.eye(128, dtype=np.float32)
    d["iden_b"] = np.eye(128, dtype=np.float32)
    return d


WEIGHT_SHAPES = None


class Rec:
    def __init__(self, bld):
        self.b = bld
        self.q = {e: [] for e in ENGS}

    def on(self, eng, fn, waits=None, inc=None, incn=1):
        b = self.b
        wl = []
        if waits:
            for s, v in waits.items():
                if s is None or v is None or v <= 0:
                    continue
                key = (eng, s)
                if b.lastw.get(key, 0) < v:
                    b.lastw[key] = v
                    wl.append((b.sems[s], v))
        inc_h = None
        if inc is not None:
            b.ct[inc] = b.ct.get(inc, 0) + incn
            inc_h = (b.sems[inc], incn)
        newct = b.ct.get(inc, 0) if inc is not None else None

        def thunk(e, wl=wl, fn=fn, inc_h=inc_h):
            for h, v in wl:
                e.wait_ge(h, v)
            ins = fn(e)
            if inc_h is not None:
                ins.then_inc(inc_h[0], inc_h[1])

        self.q[eng].append(thunk)
        return newct


class Builder:

    def __init__(self, cfg, blocks):
        self.cfg = cfg
        self.blocks = blocks
        self.nc = bass.Bass("TRN2", target_bir_lowering=False, debug=False)
        self.ctx = ExitStack()
        self.ct = {}
        self.lastw = {}
        self.sems = {}
        self._semi = 0
        self.dram = {}
        self.sb = {}
        self.dsems = {}
        self.guard = {}
        self.cc_sem = None
        self._declare_dram()
        self._alloc_sbuf()

    def new_sem(self, tag):
        name = f"s{self._semi}_{tag}"
        self._semi += 1
        self.sems[name] = self.ctx.enter_context(self.nc.semaphore(name))
        self.ct[name] = 0
        return name

    def blk_sems(self, tag):
        return {n: self.new_sem(f"{tag}{n}") for n in
                ("eP", "eA", "eV", "eO")}

    def dma_sem(self, stream, slot=0):
        key = (stream, slot)
        if key not in self.dsems:
            self.dsems[key] = self.new_sem(f"d_{stream}{slot}")
        return self.dsems[key]

    def need(self, key):
        return dict(self.guard.get(key, {}))

    def done(self, key, sem, ct):
        g = self.guard.setdefault(key, {})
        if ct is not None and (sem not in g or g[sem] < ct):
            g[sem] = ct

    def set_guard(self, key, d):
        g = self.guard.setdefault(key, {})
        for k, v in d.items():
            if v and (k not in g or g[k] < v):
                g[k] = v

    def _declare_dram(self):
        cfg, nc = self.cfg, self.nc
        H, TOK = cfg.H, cfg.TOK

        def din(name, shape, dt_=F32):
            self.dram[name] = nc.dram_tensor(name, list(shape), dt_, kind="ExternalInput")

        def dint(name, shape, dt_=F32):
            self.dram[name] = nc.dram_tensor(name, list(shape), dt_)

        din("x0", (cfg.DC, cfg.NQB, 128, cfg.QB))
        self.dram["xout"] = nc.dram_tensor(
            "xout", [cfg.DC, cfg.NQB, 128, cfg.QB], F32, kind="ExternalOutput")
        din("ropeC", (128, TOK))
        din("ropeS", (128, TOK))
        din("iden_f", (128, 128))
        din("iden_b", (128, 128))

        shp = {}
        for lay in ("l0", "l1"):
            shp[f"{lay}f_wb"] = (cfg.DC, 128, cfg.DC, 128)
            shp[f"{lay}f_bb"] = (128, cfg.DC)
            shp[f"{lay}f_wg"] = (128, cfg.DC, cfg.E)
            shp[f"{lay}f_wa"] = (128, cfg.DC, 128)
            shp[f"{lay}f_wm"] = (cfg.ER, cfg.H)
            shp[f"{lay}f_sel"] = (cfg.E, cfg.ER)
        for w in ("wq", "wk", "wv", "wo"):
            shp[f"l0a_{w}"] = (cfg.DC, 128, cfg.DC, 128)
        shp["l1a_wq"] = (cfg.DC, 128, cfg.DC, 128)
        shp["l1a_wk"] = (cfg.NKV1, 128, cfg.DC, 128)
        shp["l1a_wv"] = (cfg.NKV1, 128, cfg.DC, 128)
        shp["l1a_bq"] = (128, cfg.DC)
        shp["l1a_bk"] = (128, cfg.NKV1)
        shp["l1a_bv"] = (128, cfg.NKV1)

        need = set()
        for blk in self.blocks:
            lay = blk[:2].lower()
            if blk.endswith("A"):
                if lay == "l0":
                    need |= {"l0a_wq", "l0a_wk", "l0a_wv", "l0a_wo"}
                else:
                    need |= {"l1a_wq", "l1a_bq", "l1a_wk", "l1a_bk", "l1a_wv", "l1a_bv"}
            else:
                need |= {f"{lay}f_{s}" for s in ("wb", "bb", "wg", "wa", "wm", "sel")}
        for n in sorted(need):
            din(n, shp[n])

        import os as _os
        _dbg = _os.environ.get("MK_DBG_MID") == "1"
        tshape = [cfg.DC, cfg.NQB, 128, cfg.QB]
        for i in range(len(self.blocks) - 1):
            if _dbg:
                self.dram[f"xmid{i}"] = nc.dram_tensor(
                    f"xmid{i}", tshape, F32, kind="ExternalOutput")
            else:
                dint(f"xmid{i}", tshape)
        if _dbg:
            self.dram["ybuf"] = nc.dram_tensor("ybuf", tshape, F32,
                                               kind="ExternalOutput")
        else:
            dint("ybuf", tshape)
        if _os.environ.get("MK_DBG_AR") == "1":
            self.dram["dbg_arena"] = nc.dram_tensor(
                "dbg_arena", [128, self.cfg.DC, TOK], BF16, kind="ExternalOutput")
        if _os.environ.get("MK_DBG_RT") == "1":
            cfgE, NTT = self.cfg.E, self.cfg.NTT
            self.dram["dbg_lgtm"] = nc.dram_tensor("dbg_lgtm", [128, NTT * cfgE], F32, kind="ExternalOutput")
            self.dram["dbg_wtm"] = nc.dram_tensor("dbg_wtm", [128, NTT * cfgE], F32, kind="ExternalOutput")
            self.dram["dbg_hw"] = nc.dram_tensor("dbg_hw", [128, TOK], BF16, kind="ExternalOutput")
            self.dram["dbg_h"] = nc.dram_tensor("dbg_h", [128, TOK], F32, kind="ExternalOutput")
            self.dram["dbg_wtb"] = nc.dram_tensor("dbg_wtb", [cfgE, TOK], BF16, kind="ExternalOutput")
        if any(b.endswith("A") for b in self.blocks):
            dint("qrope", (cfg.NH, 128, TOK), BF16)
            dint("kme", (cfg.NH * 128, TOK), BF16)
            dint("kall", (2 * cfg.NH * 128, TOK), BF16)
            dint("vme", (cfg.NH * cfg.KTL * 128, 128), BF16)
            dint("vall", (2 * cfg.NH * cfg.KTL * 128, 128), BF16)

    def xbuf(self, i):
        if i == 0:
            return self.dram["x0"].ap()
        if i == len(self.blocks):
            return self.dram["xout"].ap()
        return self.dram[f"xmid{i-1}"].ap()

    def _alloc_sbuf(self):
        cfg, nc = self.cfg, self.nc
        H, TOK, DC, QB, E = cfg.H, cfg.TOK, cfg.DC, cfg.QB, cfg.E
        rem = nc.sbuf_bytes_remaining
        owner = nc.alloc_sbuf_tensor("OWNER", [128, rem - 256], mybir.dt.uint8)
        base = nc.lookup_mloc(owner).addr
        self._limit = rem - 256

        def at(name, shape, dt_, off):
            off = (off + 31) // 32 * 32
            bpp = int(np.prod(shape[1:])) * mybir.dt.size(dt_)
            assert off + bpp <= self._limit, f"{name}: {off}+{bpp} > {self._limit}"
            t = nc.alloc_sbuf_tensor_at(name, list(shape), dt_, offset=base + off)
            self.sb[name] = t
            return t, (off + bpp + 31) // 32 * 32

        off = 0
        _, off = at("arena", (128, DC, TOK), BF16, off)
        wst_off = off
        _, off = at("wst", (128, 2, DC, 128), BF16, off)
        if cfg.H != 4096:
            _, off = at("Ba", (128, TOK), F32, off)
            _, off = at("Bc", (128, TOK), F32, off)
            _, off = at("ep_x", (128, 4, QB), F32, off)
            _, off = at("ep_t", (128, 2, QB), F32, off)
            _, off = at("ep_y", (128, 4, QB), F32, off)
            _, off = at("ep_o", (128, 2, QB), F32, off)
            _, off = at("ep_b", (128, 2, QB), BF16, off)
            _, off = at("ep_b2", (128, 2, QB), BF16, off)
            _, off = at("prj_raw", (128, 2, QB), F32, off)
            _, off = at("prj_sw", (128, 2, QB), F32, off)
            _, off = at("prj_t1", (128, 2, QB), F32, off)
            _, off = at("prj_qk", (128, 4, QB), BF16, off)
            _, off = at("prj_v", (128, 2, QB), BF16, off)
            _, off = at("prj_vt", (128, 2, cfg.TPQ, 128), BF16, off)
            _, off = at("ropeCsb", (128, TOK), F32, off)
            _, off = at("ropeSsb", (128, TOK), F32, off)
            _, off = at("kbuf", (128, 2, cfg.KT, 128), BF16, off)
            _, off = at("vbuf", (128, 2, cfg.KT, 128), BF16, off)
            _, off = at("qbuf", (128, 2, TOK), BF16, off)
            _, off = at("acc", (128, 2, QB), F32, off)
            _, off = at("ffn_h", (128, TOK), F32, off)
            _, off = at("ffn_hw", (128, TOK), BF16, off)
            _, off = at("lgs", (E, 2, QB), F32, off)
            _, off = at("fy", (128, 4, QB), F32, off)
            _, off = at("wTb", (E, TOK), BF16, off)
            _, off = at("ffn_a", (128, DC, 128), BF16, off)
            _, off = at("ffn_b", (128, H), BF16, off)
            _, off = at("att_y", (128, 4, QB), F32, off)
            _, off = at("att_b", (128, 4, QB), BF16, off)
            _, off = at("att_b2", (128, 4, QB), BF16, off)
        else:
            PH = 46 * 1024 if cfg.H == 4096 else 30 * 1024
            ph0 = off
            off += PH
            o = ph0
            _, o = at("Ba", (128, TOK), F32, o)
            _, o = at("Bc", (128, TOK), F32, o)
            ep_x_off = o
            _, o = at("ep_x", (128, 4, QB), F32, o)
            _, o = at("ep_t", (128, 2, QB), F32, o)
            _, o = at("ep_y", (128, 4, QB), F32, o)
            _, o = at("ep_o", (128, 2, QB), F32, o)
            epb_off = o
            _, o = at("ep_b", (128, 2, QB), BF16, o)
            _, o = at("ep_b2", (128, 2, QB), BF16, o)
            assert o - ph0 <= PH, f"epi overlay {o - ph0} > {PH}"
            self._epb_off = epb_off
            o = ph0
            _, o = at("prj_raw", (128, 2, QB), F32, o)
            _, o = at("prj_sw", (128, 2, QB), F32, o)
            _, o = at("prj_t1", (128, 2, QB), F32, o)
            _, o = at("prj_qk", (128, 4, QB), BF16, o)
            _, o = at("prj_v", (128, 2, QB), BF16, o)
            _, o = at("prj_vt", (128, 2, cfg.TPQ, 128), BF16, o)
            ropec_off = ph0 + (PH - 2 * TOK * 4)
            assert o <= ropec_off
            _, o2 = at("ropeCsb", (128, TOK), F32, ropec_off)
            _, o2 = at("ropeSsb", (128, TOK), F32, o2)
            assert o2 - ph0 <= PH
            o = ph0
            _, o = at("kbuf", (128, 2, cfg.KT, 128), BF16, o)
            _, o = at("vbuf", (128, 2, cfg.KT, 128), BF16, o)
            _, o = at("qbuf", (128, 2, TOK), BF16, o)
            _, o = at("acc", (128, 2, QB), F32, o)
            assert o - ph0 <= PH, f"attn overlay {o - ph0} > {PH}"
            o = ph0
            _, o = at("ffn_h", (128, TOK), F32, o)
            _, o = at("ffn_hw", (128, TOK), BF16, o)
            _, o = at("lgs", (cfg.E, 2, QB), F32, o)
            assert o <= ep_x_off
            _, o = at("fy", (128, 4, QB), F32, ep_x_off)
            o2 = ep_x_off + 4 * QB * 4
            _, o2 = at("wTb", (cfg.E, TOK), BF16, o2)
            _, o2 = at("ffn_a", (128, DC, 128), BF16, o2)
            _, o2 = at("ffn_b", (128, H), BF16, o2)
            assert o2 - ph0 <= PH, f"ffn overlay {o2 - ph0} > {PH}"

            at("att_y", (128, 4, QB), F32, 0)
            at("att_b", (128, 4, QB), BF16, 4 * QB * 4)
            at("att_b2", (128, 4, QB), BF16, 4 * QB * 4 + 4 * QB * 2)

        if cfg.H == 4096:
            assert 6 * QB * 2 <= DC * 128 * 2
            at("pt", (128, 6, QB), BF16, wst_off + DC * 128 * 2)
            at("rb", (128, 2, QB), F32, self._epb_off)
        else:
            _, off = at("pt", (128, 6, QB), BF16, off)
            _, off = at("rb", (128, 2, QB), F32, off)

        _, off = at("stat", (128, max(TOK, 4 * QB)), F32, off)
        _, off = at("iden_fsb", (128, 128), F32, off)
        _, off = at("iden_bsb", (128, 128), BF16, off)
        _, off = at("ones_b", (128, 1), BF16, off)
        _, off = at("ones_f", (128, 128), F32, off)
        _, off = at("konst", (128, 4), F32, off)
        _, off = at("bias_sb", (128, DC), F32, off)
        _, off = at("b_q", (128, DC), F32, off)
        _, off = at("b_k", (128, cfg.NKV1), F32, off)
        _, off = at("b_v", (128, cfg.NKV1), F32, off)
        _, off = at("wg_sb", (128, DC, E), BF16, off)
        _, off = at("sel_sb", (E, cfg.ER), BF16, off)
        _, off = at("r16", (128, 6, cfg.NTT), F32, off)
        _, off = at("r8", (128, 3, E), F32, off)
        _, off = at("lg_tm", (128, cfg.NTT * E), F32, off)
        _, off = at("t_tm", (128, cfg.NTT * E), F32, off)
        _, off = at("w_tm", (128, cfg.NTT * E), F32, off)
        _, off = at("yb2", (128, 2, cfg.QB), BF16, off)
        assert off <= self._limit, f"static overflow {off} > {self._limit}"
        self.sbuf_used = off
        self._psi = 0

    def psum_banks(self, h):
        mloc = self.nc.lookup_mloc(h)
        base = mloc.addr - self.nc.psum_base
        bpp = int(np.prod(h.shape[1:])) * mybir.dt.size(h.dtype)
        b0 = base // 2048
        b1 = (base + bpp - 1) // 2048
        return list(range(b0, b1 + 1))

    def psum_need(self, h):
        w = {}
        for b in self.psum_banks(h):
            for k, v in self.guard.get(("bank", b), {}).items():
                w[k] = max(w.get(k, 0), v)
        return w

    def psum_done(self, h, sem, ct):
        for b in self.psum_banks(h):
            self.done(("bank", b), sem, ct)

    def psum_scope(self, spec):
        es = ExitStack()
        out = {}
        for name, shape, dt_ in spec:
            self._psi += 1
            out[name] = es.enter_context(
                self.nc.psum_tensor(f"{name}_{self._psi}", list(shape), dt_))
        return es, out

    def flush(self, block, rec):
        ename = {"sy": "sync", "po": "gpsimd", "pe": "tensor", "ac": "scalar",
                 "ve": "vector"}
        for e in ENGS:
            thunks = rec.q[e]
            if not thunks:
                continue

            def body(engine, thunks=thunks):
                for t in thunks:
                    t(engine)
            getattr(block, ename[e])(body)


def stat_slots(cfg, banks):
    return [(banks[i // 3], 32 * (i % 3)) for i in range(2 * cfg.NQB)]


RG = [[0, 1], [2, 3], [4, 5], [6, 7]]


def _dma(bld, r, eng, stream, slot, fn, waits=None):
    sem = bld.dma_sem(stream, slot)
    w = bld.need((stream, slot))
    if waits:
        for k, v in waits.items():
            if v:
                w[k] = max(w.get(k, 0), v)
    ct = r.on(eng, fn, waits=w, inc=sem, incn=16)
    return sem, ct


def emit_setup(bld, r, s):
    sb, d = bld.sb, bld.dram
    cts = {}
    cts["iden_f"] = dict([_dma(bld, r, "sy", "iden_f", 0, lambda e: e.dma_start(
        out=sb["iden_fsb"][:, :], in_=d["iden_f"].ap()))])
    cts["iden_b"] = dict([_dma(bld, r, "po", "iden_b", 0, lambda e: e.dma_start(
        out=sb["iden_bsb"][:, :], in_=d["iden_b"].ap()))])
    r.on("po", lambda e: e.memset(sb["ones_b"][:, :], 1.0))
    r.on("po", lambda e: e.memset(sb["ones_f"][:, :], 1.0))
    r.on("po", lambda e: e.memset(sb["konst"][:, 0:1], bld.cfg.CBIAS))
    cts["konst"] = {s["eO"]: r.on("po", lambda e: e.memset(
        sb["konst"][:, 1:2], bld.cfg.EPS), inc=s["eO"])}
    return cts


def emit_stats0(bld, r, s, xin, pst_slots, setup_cts, gate=None):
    cfg, sb = bld.cfg, bld.sb
    DC, NQB, QB = cfg.DC, cfg.NQB, cfg.QB
    s1_ct, s2_ct = {}, {}
    slot_last = {}
    for qb in range(NQB):
        for dc in range(DC):
            idx = qb * DC + dc
            s4, s2 = idx % 4, idx % 2
            dsem, dct = _dma(bld, r, "sy", "x", s4,
                             lambda e, dc=dc, qb=qb, s4=s4: e.dma_start(
                                 out=sb["ep_x"][:, s4, :],
                                 in_=xin[dc, qb]),
                             waits=gate)
            qct = r.on("ac", lambda e, s4=s4, s2=s2: e.square(
                sb["ep_b2"][:, s2, :], sb["ep_x"][:, s4, :]),
                waits={dsem: dct, s["eP"]: s2_ct.get(idx - 2, 0)}, inc=s["eA"])
            b1, m1 = pst_slots[2 * qb]
            b2, m2 = pst_slots[2 * qb + 1]
            w1 = dict(setup_cts["konst"])
            w1[dsem] = max(w1.get(dsem, 0), dct)
            if dc == 0:
                for k2, v2 in bld.psum_need(b1).items():
                    w1[k2] = max(w1.get(k2, 0), v2)
            c1 = r.on("pe", lambda e, b1=b1, m1=m1, s4=s4, dc=dc: e.matmul(
                b1[m1:m1 + 1, :], sb["ones_f"][:, 0:1], sb["ep_x"][:, s4, :],
                start=(dc == 0), stop=(dc == DC - 1)),
                waits=w1, inc=s["eP"])
            s1_ct[idx] = c1
            bld.set_guard(("x", s4), {s["eA"]: qct, s["eP"]: c1})
            w2b = {s["eA"]: qct}
            if dc == 0:
                for k2, v2 in bld.psum_need(b2).items():
                    w2b[k2] = max(w2b.get(k2, 0), v2)
            c2 = r.on("pe", lambda e, b2=b2, m2=m2, s2=s2, dc=dc: e.matmul(
                b2[m2:m2 + 1, :], sb["ones_b"][:, :], sb["ep_b2"][:, s2, :],
                start=(dc == 0), stop=(dc == DC - 1)),
                waits=w2b, inc=s["eP"])
            s2_ct[idx] = c2
            slot_last[2 * qb] = c1
            slot_last[2 * qb + 1] = c2
    return emit_stats_evac(bld, r, s, pst_slots, slot_last)


def emit_stats_evac(bld, r, s, pst_slots, slot_last):
    cfg, sb = bld.cfg, bld.sb
    QB = cfg.QB
    last = 0
    for qb in range(cfg.NQB):
        for k in range(2):
            b, m = pst_slots[2 * qb + k]
            prow = 0 if k == 0 else 32
            last = r.on("ac", lambda e, b=b, m=m, prow=prow, qb=qb: e.copy(
                sb["stat"][prow:prow + 1, qb * QB:(qb + 1) * QB], b[m:m + 1, :]),
                waits={s["eP"]: slot_last[2 * qb + k]}, inc=s["eA"])
            bld.psum_done(b, s["eA"], last)
    return last


def emit_ln_finalize(bld, r, s, stats_ready, pb, setup_cts):
    cfg, sb = bld.cfg, bld.sb
    QB = cfg.QB
    inv_h = 1.0 / cfg.H
    ba_ct = 0
    free_b = {0: 0, 1: 0}
    scr_free = 0
    for qb in range(cfg.NQB):
        w = dict(stats_ready)
        for k2, v2 in setup_cts["konst"].items():
            w[k2] = max(w.get(k2, 0), v2)
        w[s["eA"]] = max(w.get(s["eA"], 0), free_b[0])
        for k2, v2 in bld.psum_need(pb["b0"]).items():
            w[k2] = max(w.get(k2, 0), v2)
        c0 = r.on("pe", lambda e, qb=qb: e.matmul(
            pb["b0"][:, :], sb["ones_f"][0:1, :],
            sb["stat"][0:1, qb * QB:(qb + 1) * QB], start=True, stop=True),
            waits=w, inc=s["eP"])
        w2 = dict(stats_ready)
        for k2, v2 in setup_cts["konst"].items():
            w2[k2] = max(w2.get(k2, 0), v2)
        w2[s["eA"]] = max(w2.get(s["eA"], 0), free_b[1])
        for k2, v2 in bld.psum_need(pb["b1"]).items():
            w2[k2] = max(w2.get(k2, 0), v2)
        c1 = r.on("pe", lambda e, qb=qb: e.matmul(
            pb["b1"][:, :], sb["ones_f"][32:33, :],
            sb["stat"][32:33, qb * QB:(qb + 1) * QB], start=True, stop=True),
            waits=w2, inc=s["eP"])
        cm = r.on("ac", lambda e: e.mul(sb["ep_t"][:, 0, :], pb["b0"][:, :], inv_h),
                  waits={s["eP"]: c0, s["eV"]: scr_free}, inc=s["eA"])
        free_b[0] = cm
        bld.psum_done(pb["b0"], s["eA"], cm)
        ce = r.on("ac", lambda e: e.mul(sb["ep_t"][:, 1, :], pb["b1"][:, :], inv_h),
                  waits={s["eP"]: c1}, inc=s["eA"])
        free_b[1] = ce
        bld.psum_done(pb["b1"], s["eA"], ce)
        cq = r.on("ac", lambda e: e.square(sb["ep_o"][:, 0, :], sb["ep_t"][:, 0, :]),
                  waits={s["eA"]: cm}, inc=s["eA"])
        cv = r.on("ve", lambda e: e.tensor_sub(
            sb["ep_o"][:, 1, :], sb["ep_t"][:, 1, :], sb["ep_o"][:, 0, :]),
            waits={s["eA"]: cq}, inc=s["eV"])
        cs = r.on("ac", lambda e: e.activation(
            sb["ep_t"][:, 1, :], sb["ep_o"][:, 1, :], AF.Ln,
            bias=sb["konst"][:, 1:2], scale=1.0),
            waits=dict(setup_cts["konst"], **{s["eV"]: cv}), inc=s["eA"])
        cr = r.on("ac", lambda e, qb=qb: e.activation(
            sb["Ba"][:, qb * QB:(qb + 1) * QB], sb["ep_t"][:, 1, :],
            AF.Exp, scale=-0.5),
            waits={s["eA"]: cs}, inc=s["eA"])
        ba_ct = r.on("ve", lambda e, qb=qb: e.tensor_mul(
            sb["Bc"][:, qb * QB:(qb + 1) * QB], sb["ep_t"][:, 0, :],
            sb["Ba"][:, qb * QB:(qb + 1) * QB]),
            waits={s["eA"]: cr}, inc=s["eV"])
        scr_free = ba_ct
    return ba_ct


def emit_ln_apply(bld, r, s, xin, ba_ct, extra_wait=None):
    cfg, sb = bld.cfg, bld.sb
    DC, NQB, QB = cfg.DC, cfg.NQB, cfg.QB
    o_ct = {}
    last = 0
    arena_q = []
    for qb in range(NQB):
        for dc in range(DC):
            idx = qb * DC + dc
            s4, s2 = idx % 4, idx % 2
            dsem, dct = _dma(bld, r, "sy", "x", s4,
                             lambda e, dc=dc, qb=qb, s4=s4: e.dma_start(
                                 out=sb["ep_x"][:, s4, :],
                                 in_=xin[dc, qb]),
                             waits=extra_wait)
            c = r.on("po", lambda e, s4=s4, qb=qb: e.tensor_mul(
                sb["ep_x"][:, s4, :], sb["ep_x"][:, s4, :],
                sb["Ba"][:, qb * QB:(qb + 1) * QB]),
                waits={dsem: dct, s["eV"]: ba_ct},
                inc=s["eO"])
            last = r.on("ve", lambda e, s4=s4, dc=dc, qb=qb: e.tensor_tensor(
                bld.sb["arena"][:, dc, qb * QB:(qb + 1) * QB],
                sb["ep_x"][:, s4, :], sb["Bc"][:, qb * QB:(qb + 1) * QB],
                ALU.subtract),
                waits={s["eO"]: c}, inc=s["eV"])
            bld.set_guard(("x", s4), {s["eV"]: last})
            o_ct[idx] = last
        arena_q.append({s["eV"]: last})
    return {s["eV"]: last}, arena_q


def emit_proj(bld, r, s, lay, kind, arena_rdy, pp, ptr, setup_cts,
              bias_ld=None, rope_ld=None, wst_free=None, preloaded=None,
              arena_q=None):
    cfg, sb, d = bld.cfg, bld.sb, bld.dram
    DC, NQB, QB, TPQ = cfg.DC, cfg.NQB, cfg.QB, cfg.TPQ
    if lay == "l0":
        OC, bias = DC, None
        wname = f"l0a_w{kind}"
    else:
        OC = DC if kind == "q" else cfg.NKV1
        bias = {"q": "b_q", "k": "b_k", "v": "b_v"}[kind]
        wname = f"l1a_w{kind}"
    rope = kind in ("q", "k")
    evac_ct, mm_last = {}, {}
    raw_free, tr_ct, vt_ct = {}, {}, {}
    out_sems = {}
    ev = 0
    for oc in range(OC):
        par = oc % 2
        if preloaded and oc in preloaded:
            wsem, wct = preloaded[oc]
        else:
            w = {s["eP"]: mm_last.get(oc - 2, 0)}
            if wst_free:
                for k2, v2 in wst_free.items():
                    w[k2] = max(w.get(k2, 0), v2)
            wsem, wct = _dma(bld, r, "po", "w", par,
                             lambda e, oc=oc, par=par: e.dma_start(
                                 out=sb["wst"][:, par, :, :],
                                 in_=d[wname].ap()[oc]),
                             waits=w)
        for qb in range(NQB):
            bank = pp[qb % 2]
            eidx = oc * NQB + qb
            for dc in range(DC):
                w = {}
                if dc == 0:
                    w = dict(arena_q[qb]) if arena_q else dict(arena_rdy)
                    w[wsem] = max(w.get(wsem, 0), wct)
                    for k2, v2 in bld.psum_need(bank).items():
                        w[k2] = max(w.get(k2, 0), v2)
                c = r.on("pe", lambda e, par=par, dc=dc, qb=qb, bank=bank: e.matmul(
                    bank[:, :], sb["wst"][:, par, dc, :],
                    sb["arena"][:, dc, qb * QB:(qb + 1) * QB],
                    start=(dc == 0), stop=(dc == DC - 1)),
                    waits=w, inc=s["eP"])
            mm_last[oc] = c
            bld.set_guard(("w", par), {s["eP"]: c})
            if rope:
                r2, s4 = ev % 2, ev % 4
                wv = {s["eP"]: c, s["eV"]: raw_free.get(ev - 2, 0)}
                if bias:
                    for k2, v2 in bias_ld.items():
                        wv[k2] = max(wv.get(k2, 0), v2)
                    ct_raw = r.on("ac", lambda e, bank=bank, r2=r2, oc=oc,
                                  bias=bias: e.activation(
                                      sb["prj_raw"][:, r2, :], bank[:, :],
                                      AF.Identity, bias=sb[bias][:, oc:oc + 1],
                                      scale=1.0),
                                  waits=wv, inc=s["eA"])
                else:
                    ct_raw = r.on("ac", lambda e, bank=bank, r2=r2: e.copy(
                        sb["prj_raw"][:, r2, :], bank[:, :]), waits=wv, inc=s["eA"])
                evac_ct[eidx] = ct_raw
                bld.psum_done(bank, s["eA"], ct_raw)
                r.on("ve", lambda e, r2=r2: e.tensor_copy(
                    sb["prj_sw"][0:64, r2, :], sb["prj_raw"][64:128, r2, :]),
                    waits={s["eA"]: ct_raw}, inc=s["eV"])
                c2 = r.on("ve", lambda e, r2=r2: e.tensor_copy(
                    sb["prj_sw"][64:128, r2, :], sb["prj_raw"][0:64, r2, :]),
                    waits={s["eA"]: ct_raw}, inc=s["eV"])
                wv = {s["eA"]: ct_raw}
                for k2, v2 in rope_ld.items():
                    wv[k2] = max(wv.get(k2, 0), v2)
                ct_t1 = r.on("ve", lambda e, r2=r2, qb=qb: e.tensor_mul(
                    sb["prj_t1"][:, r2, :], sb["prj_raw"][:, r2, :],
                    sb["ropeCsb"][:, qb * QB:(qb + 1) * QB]),
                    waits=wv, inc=s["eV"])
                wv2 = {s["eV"]: c2}
                for k2, v2 in rope_ld.items():
                    wv2[k2] = max(wv2.get(k2, 0), v2)
                ct_sw = r.on("ve", lambda e, r2=r2, qb=qb: e.tensor_mul(
                    sb["prj_sw"][:, r2, :], sb["prj_sw"][:, r2, :],
                    sb["ropeSsb"][:, qb * QB:(qb + 1) * QB]),
                    waits=wv2, inc=s["eV"])
                wq = bld.need(("qk", s4))
                wq[s["eV"]] = max(wq.get(s["eV"], 0), ct_t1, ct_sw)
                ct_out = r.on("ve", lambda e, r2=r2, s4=s4: e.tensor_tensor(
                    sb["prj_qk"][:, s4, :], sb["prj_t1"][:, r2, :],
                    sb["prj_sw"][:, r2, :], ALU.add),
                    waits=wq, inc=s["eV"])
                raw_free[ev] = ct_out
                if kind == "q":
                    dst = d["qrope"].ap()[oc, :, qb * QB:(qb + 1) * QB]
                else:
                    dst = d["kme"].ap()[oc * 128:(oc + 1) * 128,
                                        qb * QB:(qb + 1) * QB]
                osem, oct = _dma(bld, r, "sy", "qk", s4,
                                 lambda e, s4=s4, dst=dst: e.dma_start(
                                     out=dst, in_=sb["prj_qk"][:, s4, :]),
                                 waits={s["eV"]: ct_out})
                bld.set_guard(("qk", s4), {osem: oct})
                out_sems[osem] = oct
            else:
                sv = ev % 2
                wv = {s["eP"]: max(c, tr_ct.get(ev - 2, 0))}
                if bias:
                    for k2, v2 in bias_ld.items():
                        wv[k2] = max(wv.get(k2, 0), v2)
                    ct_v = r.on("ac", lambda e, bank=bank, sv=sv, oc=oc, bias=bias:
                                e.activation(sb["prj_v"][:, sv, :], bank[:, :],
                                             AF.Identity,
                                             bias=sb[bias][:, oc:oc + 1], scale=1.0),
                                waits=wv, inc=s["eA"])
                else:
                    ct_v = r.on("ac", lambda e, bank=bank, sv=sv: e.copy(
                        sb["prj_v"][:, sv, :], bank[:, :]), waits=wv, inc=s["eA"])
                evac_ct[eidx] = ct_v
                bld.psum_done(bank, s["eA"], ct_v)
                trc = 0
                wv = {s["eA"]: max(ct_v, vt_ct.get(ev - 1, 0))}
                for k2, v2 in bld.psum_need(ptr).items():
                    wv[k2] = max(wv.get(k2, 0), v2)
                for k2, v2 in setup_cts["iden_b"].items():
                    wv[k2] = max(wv.get(k2, 0), v2)
                for t in range(TPQ):
                    trc = r.on("pe", lambda e, sv=sv, t=t: e.transpose(
                        ptr[:, t * 128:(t + 1) * 128],
                        sb["prj_v"][:, sv, t * 128:(t + 1) * 128],
                        sb["iden_bsb"][:, :]),
                        waits=wv, inc=s["eP"])
                tr_ct[ev] = trc
                wv = bld.need(("vt", sv))
                wv[s["eP"]] = max(wv.get(s["eP"], 0), trc)
                cvt = r.on("ac", lambda e, sv=sv: e.copy(
                    sb["prj_vt"][:, sv, :, :].rearrange("p a b -> p (a b)"),
                    ptr[:, 0:TPQ * 128]),
                    waits=wv, inc=s["eA"])
                vt_ct[ev] = cvt
                bld.psum_done(ptr, s["eA"], cvt)
                vview = d["vme"].ap().rearrange(
                    "(h p kt) dd -> h p kt dd", kt=cfg.KTL, p=128)
                dst = vview[oc, :, qb * TPQ:(qb + 1) * TPQ, :]
                osem, oct = _dma(bld, r, "sy", "vt", sv,
                                 lambda e, sv=sv, dst=dst: e.dma_start(
                                     out=dst, in_=sb["prj_vt"][:, sv, :, :]),
                                 waits={s["eA"]: cvt})
                bld.set_guard(("vt", sv), {osem: oct})
                out_sems[osem] = oct
            ev += 1
    return {"out": out_sems, "mm_last": max(mm_last.values())}


def emit_exchange(bld, r, s, k_out, v_out, nkv):
    d = bld.dram
    cfg = bld.cfg
    if bld.cc_sem is None:
        bld.cc_sem = bld.new_sem("cc")
    cc = bld.cc_sem
    krows = nkv * 128
    vrows = nkv * cfg.KTL * 128
    r.on("po", lambda e: e.collective_compute(
        "AllGather", ALU.bypass, replica_groups=RG,
        ins=[d["kme"].ap()[0:krows, :].opt()],
        outs=[d["kall"].ap()[0:2 * krows, :].opt()]),
        waits=dict(k_out), inc=cc)
    ct = r.on("po", lambda e: e.collective_compute(
        "AllGather", ALU.bypass, replica_groups=RG,
        ins=[d["vme"].ap()[0:vrows, :].opt()],
        outs=[d["vall"].ap()[0:2 * vrows, :].opt()]),
        waits=dict(v_out), inc=cc)
    return {cc: ct}


def emit_attention(bld, r, s, lay, setup_cts, kv_cc, q_ready):
    cfg, sb, d = bld.cfg, bld.sb, bld.dram
    QB, KT, KTL = cfg.QB, cfg.KT, cfg.KTL
    NKV = cfg.NH if lay == "l0" else cfg.NKV1
    REP = 1 if lay == "l0" else cfg.REP1
    PREF = min(8, max(KT - 2, 0))
    es, ps = bld.psum_scope(
        [("s00", (128, QB), F32), ("s01", (128, QB), F32),
         ("s10", (128, QB), F32), ("s11", (128, QB), F32),
         ("pl", (128, QB), F32), ("po0", (128, QB), F32),
         ("po1", (128, QB), F32)])
    sbank = {(0, 0): ps["s00"], (0, 1): ps["s01"],
             (1, 0): ps["s10"], (1, 1): ps["s11"]}

    kall = d["kall"].ap()[0:2 * NKV * 128, :].rearrange(
        "(rr h p) t -> rr h p t", rr=2, p=128)
    vall = d["vall"].ap()[0:2 * NKV * KTL * 128, :].rearrange(
        "(rr h p kt) dd -> rr h p kt dd", rr=2, kt=KTL, p=128)
    kv_rd, q_rd = {}, {}
    kv_used, q_used = {}, {}
    bank_free = {}
    pt_free = {}
    acc_free = {0: {}, 1: {}}
    o_free = {}
    y_out = {}
    out_eV = 0

    def load_kv(kvi):
        if kvi >= NKV:
            return
        par = kvi % 2
        w = dict(kv_cc)
        for k2, v2 in q_ready.items():
            w[k2] = max(w.get(k2, 0), v2)
        w[s["eP"]] = kv_used.get(kvi - 2, 0)
        c = 0
        for rr in range(2):
            c = r.on("sy", lambda e, rr=rr, kvi=kvi, par=par: e.dma_start(
                out=sb["kbuf"][:, par, rr * KTL:(rr + 1) * KTL, :],
                in_=kall[rr, kvi].rearrange("p (kt j) -> p kt j", j=128)),
                waits=w, inc=bld.dma_sem("kv", par), incn=16)
            c = r.on("sy", lambda e, rr=rr, kvi=kvi, par=par: e.dma_start(
                out=sb["vbuf"][:, par, rr * KTL:(rr + 1) * KTL, :],
                in_=vall[rr, kvi]),
                waits=w, inc=bld.dma_sem("kv", par), incn=16)
        kv_rd[kvi] = {bld.dma_sem("kv", par): c}

    def load_q(qi):
        if qi >= NKV * REP:
            return
        qpar = qi % 2
        qs = bld.dma_sem("q", qpar)
        qc = r.on("sy", lambda e, qi=qi, qpar=qpar: e.dma_start(
            out=sb["qbuf"][:, qpar, :], in_=d["qrope"].ap()[qi]),
            waits=dict(q_ready, **{s["eP"]: q_used.get(qi - 2, 0)}),
            inc=qs, incn=16)
        q_rd[qi] = {qs: qc}

    load_kv(0)
    load_kv(1)
    load_q(0)
    load_q(1)
    for kvi in range(NKV):
        par = kvi % 2
        for rep in range(REP):
            qi = kvi * REP + rep
            qpar = qi % 2
            for half in range(cfg.NQH):
                exp_ct = {}
                sc_ct = {}
                add_ct = {}
                pv_last = 0
                for step in range(KT + 2):
                    if step < KT:
                        kt = step
                        for j in (0, 1):
                            qb = 2 * half + j
                            bank = sbank[(kt % 2, j)]
                            if kt >= 2 and j == 0:
                                w = {s["eA"]: exp_ct[(1, kt - 2)]}
                            elif kt >= 2:
                                w = {}
                            else:
                                w = dict(kv_rd[kvi])
                                for k2, v2 in q_rd[qi].items():
                                    w[k2] = max(w.get(k2, 0), v2)
                                for k2, v2 in bld.psum_need(bank).items():
                                    w[k2] = max(w.get(k2, 0), v2)
                            c = r.on("pe", lambda e, bank=bank, par=par, kt=kt,
                                     qpar=qpar, qb=qb: e.matmul(
                                         bank[:, :], sb["kbuf"][:, par, kt, :],
                                         sb["qbuf"][:, qpar,
                                                    qb * QB:(qb + 1) * QB],
                                         start=True, stop=True),
                                     waits=w, inc=s["eP"])
                            sc_ct[(j, kt)] = c
                            ptslot = (kt * 2 + j) % 6
                            wv = dict(setup_cts["konst"])
                            wv[s["eP"]] = max(wv.get(s["eP"], 0), c)
                            for k2, v2 in pt_free.get(ptslot, {}).items():
                                wv[k2] = max(wv.get(k2, 0), v2)
                            cexp = r.on("ac", lambda e, bank=bank, ptslot=ptslot:
                                        e.activation(
                                            sb["pt"][:, ptslot, :], bank[:, :],
                                            AF.Exp, bias=sb["konst"][:, 0:1],
                                            scale=cfg.SCALE),
                                        waits=wv, inc=s["eA"])
                            exp_ct[(j, kt)] = cexp
                            bld.psum_done(bank, s["eA"], cexp)
                            if kt >= PREF:
                                eng, esem = (("ve", s["eV"]) if j == 0
                                             else ("po", s["eO"]))
                                wadd = {s["eA"]: cexp}
                                if kt == PREF:
                                    for k2, v2 in acc_free[j].items():
                                        wadd[k2] = max(wadd.get(k2, 0), v2)
                                    cadd = r.on(eng, lambda e, j=j,
                                                ptslot=ptslot: e.tensor_copy(
                                                    sb["acc"][:, j, :],
                                                    sb["pt"][:, ptslot, :]),
                                                waits=wadd, inc=esem)
                                else:
                                    cadd = r.on(eng, lambda e, j=j,
                                                ptslot=ptslot: e.tensor_add(
                                                    sb["acc"][:, j, :],
                                                    sb["acc"][:, j, :],
                                                    sb["pt"][:, ptslot, :]),
                                                waits=wadd, inc=esem)
                                add_ct[(j, kt)] = (esem, cadd)
                    if step >= 2:
                        kt2 = step - 2
                        for j in (0, 1):
                            ptslot = (kt2 * 2 + j) % 6
                            pob = ps["po0"] if j == 0 else ps["po1"]
                            if kt2 < PREF:
                                wl = ({s["eA"]: exp_ct[(1, kt2)]} if j == 0
                                      else {})
                                if kt2 == 0:
                                    for k2, v2 in bld.psum_need(
                                            ps["pl"]).items():
                                        wl[k2] = max(wl.get(k2, 0), v2)
                                    for k2, v2 in acc_free[j].items():
                                        wl[k2] = max(wl.get(k2, 0), v2)
                                cl2 = r.on("pe", lambda e, ptslot=ptslot, j=j,
                                           kt2=kt2: e.matmul(
                                               ps["pl"][32 * j:32 * j + 1, :],
                                               sb["ones_b"][:, :],
                                               sb["pt"][:, ptslot, :],
                                               start=(kt2 == 0), stop=False),
                                           waits=wl, inc=s["eP"])
                            w = {s["eA"]: exp_ct[(1, kt2)]} if j == 0 else {}
                            if kt2 == 0:
                                for k2, v2 in bld.psum_need(pob).items():
                                    w[k2] = max(w.get(k2, 0), v2)
                            cpv = r.on("pe", lambda e, pob=pob, par=par,
                                       kt2=kt2, ptslot=ptslot: e.matmul(
                                           pob[:, :],
                                           sb["vbuf"][:, par, kt2, :],
                                           sb["pt"][:, ptslot, :],
                                           start=(kt2 == 0),
                                           stop=(kt2 == KT - 1)),
                                       waits=w, inc=s["eP"])
                            pv_last = cpv
                            pf = {s["eP"]: cpv}
                            if kt2 >= PREF:
                                asem, act_ = add_ct[(j, kt2)]
                                pf[asem] = act_
                            pt_free[ptslot] = pf
                kv_used[kvi] = pv_last
                q_used[qi] = pv_last
                for j in (0, 1):
                    qb = 2 * half + j
                    bcbank = sbank[(1, j)]
                    pob = ps["po0"] if j == 0 else ps["po1"]
                    asem, act_ = add_ct[(j, KT - 1)]
                    wred = {asem: act_}
                    red = r.on("pe", lambda e, j=j: e.matmul(
                        ps["pl"][32 * j:32 * j + 1, :], sb["ones_f"][:, 0:1],
                        sb["acc"][:, j, :], start=False, stop=True),
                        waits=wred, inc=s["eP"])
                    acc_free[j] = {s["eP"]: red}
                    cl = r.on("ac", lambda e, j=j: e.activation(
                        sb["stat"][64:65, j * QB:(j + 1) * QB],
                        ps["pl"][32 * j:32 * j + 1, :], AF.Ln),
                        waits={s["eP"]: red}, inc=s["eA"])
                    bld.psum_done(ps["pl"], s["eA"], cl)
                    cr = r.on("ac", lambda e, j=j: e.activation(
                        sb["stat"][64:65, (2 + j) * QB:(3 + j) * QB],
                        sb["stat"][64:65, j * QB:(j + 1) * QB],
                        AF.Exp, scale=-1.0),
                        waits={s["eA"]: cl}, inc=s["eA"])
                    wbc = dict(setup_cts["konst"])
                    wbc[s["eA"]] = max(wbc.get(s["eA"], 0), cr)
                    for k2, v2 in bld.psum_need(bcbank).items():
                        wbc[k2] = max(wbc.get(k2, 0), v2)
                    cbc = r.on("pe", lambda e, bcbank=bcbank, j=j: e.matmul(
                        bcbank[:, :], sb["ones_f"][64:65, :],
                        sb["stat"][64:65, (2 + j) * QB:(3 + j) * QB],
                        start=True, stop=True),
                        waits=wbc, inc=s["eP"])
                    crb = r.on("ac", lambda e, bcbank=bcbank, j=j: e.copy(
                        sb["rb"][:, j, :], bcbank[:, :]),
                        waits={s["eP"]: cbc}, inc=s["eA"])
                    bld.psum_done(bcbank, s["eA"], crb)
                    if lay == "l0":
                        co = r.on("ve", lambda e, pob=pob, qi=qi, qb=qb, j=j:
                                  e.tensor_mul(
                                      sb["arena"][:, qi, qb * QB:(qb + 1) * QB],
                                      pob[:, :], sb["rb"][:, j, :]),
                                  waits={s["eA"]: crb, s["eP"]: pv_last},
                                  inc=s["eV"])
                        o_free[j] = co
                        bld.psum_done(pob, s["eV"], co)
                        out_eV = co
                    else:
                        ss = (qi * cfg.NQB + qb) % 4
                        wv = bld.need(("att_y", ss))
                        wv[s["eA"]] = max(wv.get(s["eA"], 0), crb)
                        wv[s["eP"]] = pv_last
                        co = r.on("ve", lambda e, pob=pob, ss=ss, j=j:
                                  e.tensor_mul(
                                      sb["att_y"][:, ss, :], pob[:, :],
                                      sb["rb"][:, j, :]),
                                  waits=wv, inc=s["eV"])
                        o_free[j] = co
                        bld.psum_done(pob, s["eV"], co)
                        ysem, yct = _dma(bld, r, "sy", "yw", ss,
                                         lambda e, qi=qi, qb=qb, ss=ss:
                                         e.dma_start(
                                             out=d["ybuf"].ap()[qi, qb],
                                             in_=sb["att_y"][:, ss, :]),
                                         waits={s["eV"]: co})
                        bld.set_guard(("att_y", ss), {ysem: yct})
                        y_out[ysem] = yct
            load_q(qi + 2)
        load_kv(kvi + 2)
    res = {"mm_last": bld.ct[s["eP"]], "out_eV": out_eV, "y_out": y_out}
    es.close()
    return res


def emit_y_proj(bld, r, s, wname, rhs_ready, pm, bias_ld=None,
                extra=None, wst_wait=None, arena_q=None, pst_slots=None,
                setup_cts=None):
    cfg, sb, d = bld.cfg, bld.sb, bld.dram
    DC, NQB, QB = cfg.DC, cfg.NQB, cfg.QB
    assert len(pm) >= NQB
    mm_last, evac = {}, {}
    s1p, s2p, slot_last = {}, {}, {}
    y_out = {}
    for oc in range(DC):
        par = oc % 2
        w = {s["eP"]: mm_last.get(oc - 2, 0)}
        if wst_wait:
            for k2, v2 in wst_wait.items():
                w[k2] = max(w.get(k2, 0), v2)
        wsem, wct = _dma(bld, r, "po", "w", par,
                         lambda e, oc=oc, par=par: e.dma_start(
                             out=sb["wst"][:, par, :, :], in_=d[wname].ap()[oc]),
                         waits=w)
        mmq = {}
        for qb in range(NQB):
            bank = pm[qb]
            last_stop = extra is None
            for dc in range(DC):
                w = {}
                if dc == 0:
                    w = dict(arena_q[qb]) if arena_q else dict(rhs_ready)
                    w[wsem] = max(w.get(wsem, 0), wct)
                    for k2, v2 in bld.psum_need(bank).items():
                        w[k2] = max(w.get(k2, 0), v2)
                c = r.on("pe", lambda e, par=par, dc=dc, qb=qb, bank=bank: e.matmul(
                    bank[:, :], sb["wst"][:, par, dc, :],
                    sb["arena"][:, dc, qb * QB:(qb + 1) * QB],
                    start=(dc == 0), stop=(last_stop and dc == DC - 1)),
                    waits=w, inc=s["eP"])
            mmq[qb] = c
        mm_last[oc] = mmq[NQB - 1]
        bld.set_guard(("w", par), {s["eP"]: mmq[NQB - 1]})
        if extra is not None:
            for qb in range(NQB):
                mmq[qb] = extra(oc, qb, pm[qb])
        for qb in range(NQB):
            idx = oc * NQB + qb
            bank = pm[qb]
            s4 = idx % 4
            wv = bld.need(("fy", s4))
            wv[s["eP"]] = max(wv.get(s["eP"], 0), mmq[qb])
            if bias_ld is not None:
                for k2, v2 in bias_ld.items():
                    wv[k2] = max(wv.get(k2, 0), v2)
                ce = r.on("ac", lambda e, bank=bank, s4=s4, oc=oc: e.activation(
                    sb["fy"][:, s4, :], bank[:, :], AF.Identity,
                    bias=sb["bias_sb"][:, oc:oc + 1], scale=1.0),
                    waits=wv, inc=s["eA"])
            else:
                ce = r.on("ac", lambda e, bank=bank, s4=s4: e.copy(
                    sb["fy"][:, s4, :], bank[:, :]), waits=wv, inc=s["eA"])
            evac[idx] = ce
            bld.psum_done(bank, s["eA"], ce)
            ysem, yct = _dma(bld, r, "sy", "yw", s4,
                             lambda e, oc=oc, qb=qb, s4=s4: e.dma_start(
                                 out=d["ybuf"].ap()[oc, qb],
                                 in_=sb["fy"][:, s4, :]),
                             waits={s["eA"]: ce})
            fg = {ysem: yct}
            if pst_slots is not None:
                s2y = idx % 2
                cq = r.on("ac", lambda e, s4=s4, s2y=s2y: e.square(
                    sb["yb2"][:, s2y, :], sb["fy"][:, s4, :]),
                    waits={s["eP"]: s2p.get(idx - 2, 0)}, inc=s["eA"])
                b1, m1 = pst_slots[2 * qb]
                b2, m2 = pst_slots[2 * qb + 1]
                w1 = dict(setup_cts["konst"])
                w1[s["eA"]] = max(w1.get(s["eA"], 0), ce)
                if oc == 0:
                    for k2, v2 in bld.psum_need(b1).items():
                        w1[k2] = max(w1.get(k2, 0), v2)
                c1 = r.on("pe", lambda e, b1=b1, m1=m1, s4=s4, oc=oc: e.matmul(
                    b1[m1:m1 + 1, :], sb["ones_f"][:, 0:1],
                    sb["fy"][:, s4, :],
                    start=(oc == 0), stop=(oc == DC - 1)),
                    waits=w1, inc=s["eP"])
                w2b = {s["eA"]: cq}
                if oc == 0:
                    for k2, v2 in bld.psum_need(b2).items():
                        w2b[k2] = max(w2b.get(k2, 0), v2)
                c2 = r.on("pe", lambda e, b2=b2, m2=m2, s2y=s2y, oc=oc:
                          e.matmul(
                              b2[m2:m2 + 1, :], sb["ones_b"][:, :],
                              sb["yb2"][:, s2y, :],
                              start=(oc == 0), stop=(oc == DC - 1)),
                          waits=w2b, inc=s["eP"])
                s1p[idx] = c1
                s2p[idx] = c2
                slot_last[2 * qb] = c1
                slot_last[2 * qb + 1] = c2
                fg[s["eA"]] = cq
                fg[s["eP"]] = c1
            bld.set_guard(("fy", s4), fg)
            y_out[ysem] = yct
    return {"y_out": y_out, "mm_last": max(mm_last.values()),
            "slot_last": slot_last}


def emit_ffn_pre(bld, r, s, lay, arena_rdy, setup_cts, region_gate=None,
                 arena_q=None):
    cfg, sb, d = bld.cfg, bld.sb, bld.dram
    DC, NQB, QB, TPQ, NTT, E = cfg.DC, cfg.NQB, cfg.QB, cfg.TPQ, cfg.NTT, cfg.E
    la = f"{lay}f"
    es, ps = bld.psum_scope(
        [("plg", (128, QB), F32), ("ptt", (128, 128), F32),
         ("ptt2", (128, QB), F32), ("pwx", (128, QB), F32),
         ("ph0", (128, QB), F32), ("ph1", (128, QB), F32)])
    gW = dict(region_gate) if region_gate else dict(arena_rdy)
    wg_ld = dict([_dma(bld, r, "po", "wg", 0, lambda e: e.dma_start(
        out=sb["wg_sb"][:, :, :], in_=d[f"{la}_wg"].ap()), waits=gW)])
    sel_ld = dict([_dma(bld, r, "po", "sel", 0, lambda e: e.dma_start(
        out=sb["sel_sb"][:, :], in_=d[f"{la}_sel"].ap()), waits=gW)])
    a_ld = dict([_dma(bld, r, "po", "wa", 0, lambda e: e.dma_start(
        out=sb["ffn_a"][:, :, :], in_=d[f"{la}_wa"].ap()), waits=gW)])
    b_ld = dict([_dma(bld, r, "po", "wbm", 0, lambda e: e.dma_start(
        out=sb["ffn_b"][:, :], in_=d[f"{la}_wm"].ap()), waits=dict(arena_rdy))])
    bias_ld = dict([_dma(bld, r, "sy", "fbias", 0, lambda e: e.dma_start(
        out=sb["bias_sb"][:, :], in_=d[f"{la}_bb"].ap()), waits=gW)])

    lg_ct = {}
    tr_last = 0
    for qb in range(NQB):
        sq = qb % 2
        for dc in range(DC):
            w = {}
            if dc == 0:
                w = dict(wg_ld)
                for k2, v2 in (arena_q[qb] if arena_q else arena_rdy).items():
                    w[k2] = max(w.get(k2, 0), v2)
                for k2, v2 in bld.psum_need(ps["plg"]).items():
                    w[k2] = max(w.get(k2, 0), v2)
            c = r.on("pe", lambda e, dc=dc, qb=qb: e.matmul(
                ps["plg"][0:E, :], sb["wg_sb"][:, dc, :],
                sb["arena"][:, dc, qb * QB:(qb + 1) * QB],
                start=(dc == 0), stop=(dc == DC - 1)),
                waits=w, inc=s["eP"])
        ce = r.on("ac", lambda e, sq=sq: e.copy(
            sb["lgs"][:, sq, :], ps["plg"][0:E, :]),
            waits={s["eP"]: c}, inc=s["eA"])
        lg_ct[qb] = ce
        bld.psum_done(ps["plg"], s["eA"], ce)
        wv = dict(setup_cts["iden_f"])
        wv[s["eA"]] = ce
        for k2, v2 in bld.psum_need(ps["ptt"]).items():
            wv[k2] = max(wv.get(k2, 0), v2)
        for t in range(TPQ):
            tt = qb * TPQ + t
            tr_last = r.on("pe", lambda e, sq=sq, t=t, tt=tt: e.transpose(
                ps["ptt"][:, tt * E:(tt + 1) * E],
                sb["lgs"][:, sq, t * 128:(t + 1) * 128],
                sb["iden_fsb"][0:E, 0:E]),
                waits=wv, inc=s["eP"])
    clg = r.on("ac", lambda e: e.copy(sb["lg_tm"][:, :], ps["ptt"][:, 0:NTT * E]),
               waits={s["eP"]: tr_last}, inc=s["eA"])
    bld.psum_done(ps["ptt"], s["eA"], clg)

    hprev = {}
    for qb in range(NQB):
        phb = ps["ph0"] if qb % 2 == 0 else ps["ph1"]
        for dc in range(DC):
            w = {}
            if dc == 0:
                w = dict(a_ld)
                for k2, v2 in (arena_q[qb] if arena_q else arena_rdy).items():
                    w[k2] = max(w.get(k2, 0), v2)
                for k2, v2 in bld.psum_need(phb).items():
                    w[k2] = max(w.get(k2, 0), v2)
            c = r.on("pe", lambda e, dc=dc, qb=qb, phb=phb: e.matmul(
                phb[:, :], sb["ffn_a"][:, dc, :],
                sb["arena"][:, dc, qb * QB:(qb + 1) * QB],
                start=(dc == 0), stop=(dc == DC - 1)),
                waits=w, inc=s["eP"])
        ch = r.on("ac", lambda e, qb=qb, phb=phb: e.copy(
            sb["ffn_h"][:, qb * QB:(qb + 1) * QB], phb[:, :]),
            waits={s["eP"]: c}, inc=s["eA"])
        hprev[qb] = ch
        bld.psum_done(phb, s["eA"], ch)

    mprev = 0
    for tt in range(NTT):
        c1 = r.on("ve", lambda e, tt=tt: e.reduce_max(
            sb["r16"][:, 0, tt:tt + 1], sb["lg_tm"][:, tt * E:(tt + 1) * E],
            axis=AX.X), waits={s["eA"]: clg}, inc=s["eV"])
        c2 = r.on("ve", lambda e, tt=tt: e.tensor_scalar(
            sb["r8"][:, 0, :], sb["lg_tm"][:, tt * E:(tt + 1) * E],
            sb["r16"][:, 0, tt:tt + 1], None, ALU.is_equal),
            waits={s["eV"]: c1}, inc=s["eV"])
        c3 = r.on("ve", lambda e, tt=tt: e.scalar_tensor_tensor(
            sb["t_tm"][:, tt * E:(tt + 1) * E], sb["r8"][:, 0, :], -1e30,
            sb["lg_tm"][:, tt * E:(tt + 1) * E], ALU.mult, ALU.add),
            waits={s["eV"]: c2}, inc=s["eV"])
        mprev = r.on("ve", lambda e, tt=tt: e.reduce_max(
            sb["r16"][:, 1, tt:tt + 1], sb["t_tm"][:, tt * E:(tt + 1) * E],
            axis=AX.X), waits={s["eV"]: c3}, inc=s["eV"])
    cp = r.on("ve", lambda e: e.tensor_sub(
        sb["r16"][:, 2, :], sb["r16"][:, 1, :], sb["r16"][:, 0, :]),
        waits={s["eV"]: mprev}, inc=s["eV"])
    cpe = r.on("ac", lambda e: e.activation(
        sb["r16"][:, 3, :], sb["r16"][:, 2, :], AF.Exp),
        waits={s["eV"]: cp}, inc=s["eA"])
    cd = r.on("ac", lambda e: e.add(sb["r16"][:, 4, :], sb["r16"][:, 3, :], 1.0),
              waits={s["eA"]: cpe}, inc=s["eA"])
    crec = r.on("ve", lambda e: e.reciprocal(sb["r16"][:, 5, :], sb["r16"][:, 4, :]),
                waits={s["eA"]: cd}, inc=s["eV"])
    cw2 = r.on("ve", lambda e: e.tensor_mul(
        sb["r16"][:, 2, :], sb["r16"][:, 3, :], sb["r16"][:, 5, :]),
        waits={s["eV"]: crec}, inc=s["eV"])
    wlast = cw2
    for tt in range(NTT):
        c1 = r.on("ve", lambda e, tt=tt: e.tensor_scalar(
            sb["r8"][:, 0, :], sb["lg_tm"][:, tt * E:(tt + 1) * E],
            sb["r16"][:, 0, tt:tt + 1], None, ALU.is_equal),
            waits={s["eV"]: cw2}, inc=s["eV"])
        c2 = r.on("ve", lambda e, tt=tt: e.tensor_scalar(
            sb["r8"][:, 1, :], sb["t_tm"][:, tt * E:(tt + 1) * E],
            sb["r16"][:, 1, tt:tt + 1], None, ALU.is_equal),
            waits={s["eV"]: c1}, inc=s["eV"])
        c3 = r.on("ve", lambda e, tt=tt: e.tensor_scalar(
            sb["r8"][:, 2, :], sb["r8"][:, 0, :],
            sb["r16"][:, 5, tt:tt + 1], None, ALU.mult),
            waits={s["eV"]: c2}, inc=s["eV"])
        wlast = r.on("ve", lambda e, tt=tt: e.scalar_tensor_tensor(
            sb["w_tm"][:, tt * E:(tt + 1) * E], sb["r8"][:, 1, :],
            sb["r16"][:, 2, tt:tt + 1], sb["r8"][:, 2, :], ALU.mult, ALU.add),
            waits={s["eV"]: c3}, inc=s["eV"])
    hw_last = 0
    wT_ct = {}
    for qb in range(NQB):
        trl = 0
        wv = dict(setup_cts["iden_f"])
        wv[s["eV"]] = wlast
        for k2, v2 in bld.psum_need(ps["ptt2"]).items():
            wv[k2] = max(wv.get(k2, 0), v2)
        for t in range(TPQ):
            tt = qb * TPQ + t
            trl = r.on("pe", lambda e, t=t, tt=tt: e.transpose(
                ps["ptt2"][0:E, t * 128:(t + 1) * 128],
                sb["w_tm"][:, tt * E:(tt + 1) * E], sb["iden_fsb"][:, :]),
                waits=wv, inc=s["eP"])
        cwt = r.on("ac", lambda e, qb=qb: e.copy(
            sb["wTb"][:, qb * QB:(qb + 1) * QB], ps["ptt2"][0:E, 0:QB]),
            waits={s["eP"]: trl}, inc=s["eA"])
        wT_ct[qb] = cwt
        bld.psum_done(ps["ptt2"], s["eA"], cwt)
    hw_q = {}
    for qb in range(NQB):
        wv = dict(sel_ld)
        wv[s["eA"]] = wT_ct[qb]
        for k2, v2 in bld.psum_need(ps["pwx"]).items():
            wv[k2] = max(wv.get(k2, 0), v2)
        cwx = r.on("pe", lambda e, qb=qb: e.matmul(
            ps["pwx"][:, :], sb["sel_sb"][:, :],
            sb["wTb"][:, qb * QB:(qb + 1) * QB], start=True, stop=True),
            waits=wv, inc=s["eP"])
        hw_last = r.on("ve", lambda e, qb=qb: e.tensor_mul(
            sb["ffn_hw"][:, qb * QB:(qb + 1) * QB],
            sb["ffn_h"][:, qb * QB:(qb + 1) * QB], ps["pwx"][:, :]),
            waits={s["eA"]: hprev[qb], s["eP"]: cwx}, inc=s["eV"])
        hw_q[qb] = hw_last
        bld.psum_done(ps["pwx"], s["eV"], hw_last)
    import os as _os
    if _os.environ.get("MK_DBG_RT") == "1":
        for nm, t in (("dbg_lgtm", sb["lg_tm"]), ("dbg_wtm", sb["w_tm"]),
                      ("dbg_hw", sb["ffn_hw"]), ("dbg_h", sb["ffn_h"]),
                      ("dbg_wtb", sb["wTb"])):
            ap = bld.dram[nm].ap()
            r.on("sy", lambda e, ap=ap, t=t: e.dma_start(
                out=ap, in_=t[:, :] if len(t.shape) == 2 else t[:, :, :]),
                waits={s["eV"]: hw_last}, inc=bld.dma_sem("dbg", 0), incn=16)
    es.close()
    return {"hw_eV": hw_last, "hw_q": hw_q, "b_ld": b_ld, "bias_ld": bias_ld}


def emit_epilogue(bld, r, s, xin, yin, xout, ba_ct, pst_slots, want_stats,
                  setup_cts, gate=None):
    cfg, sb = bld.cfg, bld.sb
    DC, NQB, QB = cfg.DC, cfg.NQB, cfg.QB
    slot_last = {}
    s1p, s2p = {}, {}
    out_sems = {}
    for qb in range(NQB):
        for dc in range(DC):
            idx = qb * DC + dc
            s4, s2 = idx % 4, idx % 2
            w = {s["eV"]: ba_ct}
            if gate:
                for k2, v2 in gate.items():
                    w[k2] = max(w.get(k2, 0), v2)
            xsem, dx = _dma(bld, r, "sy", "x", s4,
                            lambda e, dc=dc, qb=qb, s4=s4: e.dma_start(
                                out=sb["ep_x"][:, s4, :],
                                in_=xin[dc, qb]),
                            waits=w)
            ysem, dy = _dma(bld, r, "sy", "y_in", s4,
                            lambda e, dc=dc, qb=qb, s4=s4: e.dma_start(
                                out=sb["ep_y"][:, s4, :],
                                in_=yin[dc, qb]),
                            waits=w)
            ct = r.on("ve", lambda e, s4=s4, qb=qb: e.tensor_mul(
                sb["ep_y"][:, s4, :], sb["ep_y"][:, s4, :],
                sb["Ba"][:, qb * QB:(qb + 1) * QB]),
                waits={ysem: dy, s["eV"]: ba_ct}, inc=s["eV"])
            co = r.on("po", lambda e, s4=s4, qb=qb: e.tensor_tensor(
                sb["ep_x"][:, s4, :], sb["ep_x"][:, s4, :],
                sb["Bc"][:, qb * QB:(qb + 1) * QB], ALU.subtract),
                waits={xsem: dx, s["eV"]: ba_ct}, inc=s["eO"])
            cx = r.on("ve", lambda e, s4=s4: e.tensor_add(
                sb["ep_x"][:, s4, :], sb["ep_x"][:, s4, :],
                sb["ep_y"][:, s4, :]),
                waits={s["eO"]: co}, inc=s["eV"])
            bld.set_guard(("y_in", s4), {s["eV"]: cx})
            osem, oct = _dma(bld, r, "ac", "xo", s4,
                             lambda e, dc=dc, qb=qb, s4=s4: e.dma_start(
                                 out=xout[dc, qb],
                                 in_=sb["ep_x"][:, s4, :]),
                             waits={s["eV"]: cx})
            out_sems[osem] = oct
            xg = {osem: oct}
            if want_stats:
                cq = r.on("ac", lambda e, s4=s4, s2=s2: e.square(
                    sb["ep_b2"][:, s2, :], sb["ep_x"][:, s4, :]),
                    waits={s["eV"]: cx, s["eP"]: s2p.get(idx - 2, 0)},
                    inc=s["eA"])
                b1, m1 = pst_slots[2 * qb]
                b2, m2 = pst_slots[2 * qb + 1]
                w1 = dict(setup_cts["konst"])
                w1[s["eV"]] = max(w1.get(s["eV"], 0), cx)
                if dc == 0:
                    for k2, v2 in bld.psum_need(b1).items():
                        w1[k2] = max(w1.get(k2, 0), v2)
                c1 = r.on("pe", lambda e, b1=b1, m1=m1, s4=s4, dc=dc: e.matmul(
                    b1[m1:m1 + 1, :], sb["ones_f"][:, 0:1],
                    sb["ep_x"][:, s4, :],
                    start=(dc == 0), stop=(dc == DC - 1)),
                    waits=w1, inc=s["eP"])
                w2b = {s["eA"]: cq}
                if dc == 0:
                    for k2, v2 in bld.psum_need(b2).items():
                        w2b[k2] = max(w2b.get(k2, 0), v2)
                c2 = r.on("pe", lambda e, b2=b2, m2=m2, s2=s2, dc=dc: e.matmul(
                    b2[m2:m2 + 1, :], sb["ones_b"][:, :], sb["ep_b2"][:, s2, :],
                    start=(dc == 0), stop=(dc == DC - 1)),
                    waits=w2b, inc=s["eP"])
                s1p[idx] = c1
                s2p[idx] = c2
                slot_last[2 * qb] = c1
                slot_last[2 * qb + 1] = c2
                xg[s["eA"]] = cq
                xg[s["eP"]] = c1
            bld.set_guard(("x", s4), xg)
    return out_sems, (slot_last if want_stats else None)


def emit_block(bld, r, blk, bi, prev, is_last):
    cfg = bld.cfg
    lay = blk[:2].lower()
    is_attn = blk.endswith("A")
    s = bld.blk_sems(blk)
    xin = bld.xbuf(bi)
    xnext = bld.xbuf(bi + 1)
    d, sb = bld.dram, bld.sb

    setup_cts = emit_setup(bld, r, s) if prev is None else prev["setup"]

    if prev is None or "stats_ready" not in prev or FORCE_STATS0:
        es0, ps0 = bld.psum_scope(
            [("st0", (128, cfg.QB), F32), ("st1", (128, cfg.QB), F32),
             ("st2", (128, cfg.QB), F32)])
        slots0 = stat_slots(cfg, [ps0["st0"], ps0["st1"], ps0["st2"]])
        c = emit_stats0(bld, r, s, xin, slots0, setup_cts,
                        gate=(dict(prev["x_ready"]) if prev and "x_ready" in prev
                              else None))
        es0.close()
        stats_ready = {s["eA"]: c}
    else:
        stats_ready = prev["stats_ready"]

    esb, pbb = bld.psum_scope([("b0", (128, cfg.QB), F32),
                               ("b1", (128, cfg.QB), F32)])
    ba = emit_ln_finalize(bld, r, s, stats_ready, pbb, setup_cts)
    esb.close()
    extra = dict(prev.get("arena_free", {})) if prev else {}
    if prev and "x_ready" in prev:
        for k, v in prev["x_ready"].items():
            extra[k] = max(extra.get(k, 0), v)
    epi_gate = dict(prev.get("epi_done", {})) if prev else None

    if is_attn:
        kw = "l0a_wk" if lay == "l0" else "l1a_wk"
        pre_w = {}
        for oc in (0, 1):
            w = dict(prev.get("wst_free", {})) if prev else {}
            pre_w[oc] = _dma(bld, r, "po", "w", oc % 2,
                             lambda e, oc=oc: e.dma_start(
                                 out=bld.sb["wst"][:, oc % 2, :, :],
                                 in_=d[kw].ap()[oc]),
                             waits=w)
    arena_rdy, arena_q = emit_ln_apply(bld, r, s, xin, ba,
                                       extra_wait=extra or None)
    wst_free = dict(prev.get("wst_free", {})) if prev else {}

    if is_attn:
        rl = _dma(bld, r, "sy", "rope", 0, lambda e: e.dma_start(
            out=sb["ropeCsb"][:, :], in_=d["ropeC"].ap()),
            waits=dict(arena_rdy))
        rl2 = _dma(bld, r, "sy", "rope", 0, lambda e: e.dma_start(
            out=sb["ropeSsb"][:, :], in_=d["ropeS"].ap()))
        rope_ld = {rl2[0]: rl2[1]}
        bias_ld = {}
        if lay == "l1":
            for src, dst in (("l1a_bq", "b_q"), ("l1a_bk", "b_k"),
                             ("l1a_bv", "b_v")):
                bl = _dma(bld, r, "sy", "abias", 0,
                          lambda e, src=src, dst=dst: e.dma_start(
                              out=sb[dst][:, :], in_=d[src].ap()),
                          waits=(epi_gate or arena_rdy))
            bias_ld = {bl[0]: bl[1]}
        esp, psp = bld.psum_scope(
            [("pp0", (128, cfg.QB), F32), ("pp1", (128, cfg.QB), F32),
             ("ptr", (128, 1024), BF16)])
        pp = [psp["pp0"], psp["pp1"]]
        kres = emit_proj(bld, r, s, lay, "k", arena_rdy, pp, psp["ptr"],
                         setup_cts, bias_ld, rope_ld, wst_free,
                         preloaded=pre_w, arena_q=arena_q)
        vres = emit_proj(bld, r, s, lay, "v", arena_rdy, pp, psp["ptr"],
                         setup_cts, bias_ld, rope_ld, wst_free,
                         arena_q=arena_q)
        nkv = cfg.NH if lay == "l0" else cfg.NKV1
        kv_cc = emit_exchange(bld, r, s, kres["out"], vres["out"], nkv)
        qres = emit_proj(bld, r, s, lay, "q", arena_rdy, pp, psp["ptr"],
                         setup_cts, bias_ld, rope_ld, wst_free,
                         arena_q=arena_q)
        esp.close()

        if lay == "l1":
            ares = emit_attention(bld, r, s, lay, setup_cts, kv_cc,
                                  qres["out"])
            y_gate = dict(ares["y_out"])
            arena_reader = {s["eP"]: ares["mm_last"], s["eA"]: bld.ct[s["eA"]],
                            s["eV"]: bld.ct[s["eV"]]}
            wst_next = {s["eP"]: ares["mm_last"], s["eV"]: bld.ct[s["eV"]]}
        else:
            ares = emit_attention(bld, r, s, lay, setup_cts, kv_cc, qres["out"])
            esy, psy = bld.psum_scope(
                [("st0", (128, cfg.QB), F32), ("st1", (128, cfg.QB), F32),
                 ("st2", (128, cfg.QB), F32)])
            slotsy = stat_slots(cfg, [psy["st0"], psy["st1"], psy["st2"]])
            espm, pspm = bld.psum_scope(
                [("pm0", (128, cfg.QB), F32), ("pm1", (128, cfg.QB), F32),
                 ("pm2", (128, cfg.QB), F32), ("pm3", (128, cfg.QB), F32)])
            wres = emit_y_proj(
                bld, r, s, "l0a_wo", {s["eV"]: ares["out_eV"]},
                [pspm["pm0"], pspm["pm1"], pspm["pm2"], pspm["pm3"]],
                wst_wait={s["eP"]: ares["mm_last"], s["eV"]: ares["out_eV"]},
                pst_slots=slotsy, setup_cts=setup_cts)
            espm.close()
            stats_eA = emit_stats_evac(bld, r, s, slotsy, wres["slot_last"])
            esy.close()
            y_gate = dict(wres["y_out"])
            arena_reader = {s["eP"]: wres["mm_last"]}
            wst_next = {s["eP"]: wres["mm_last"]}
    else:
        pre = emit_ffn_pre(bld, r, s, lay, arena_rdy, setup_cts,
                           region_gate=epi_gate, arena_q=arena_q)
        esy, psy = bld.psum_scope(
            [("st0", (128, cfg.QB), F32), ("st1", (128, cfg.QB), F32),
             ("st2", (128, cfg.QB), F32)])
        slotsy = stat_slots(cfg, [psy["st0"], psy["st1"], psy["st2"]])
        espm, pspm = bld.psum_scope(
            [("pm0", (128, cfg.QB), F32), ("pm1", (128, cfg.QB), F32),
             ("pm2", (128, cfg.QB), F32), ("pm3", (128, cfg.QB), F32)])

        def extra2(oc, qb, bank):
            w = dict(pre["b_ld"])
            w[s["eV"]] = pre["hw_q"][qb]
            return r.on("pe", lambda e, oc=oc, qb=qb, bank=bank: e.matmul(
                bank[:, :], sb["ffn_b"][:, oc * 128:(oc + 1) * 128],
                sb["ffn_hw"][:, qb * cfg.QB:(qb + 1) * cfg.QB],
                start=False, stop=True),
                waits=w, inc=s["eP"])

        import os as _os
        wres = emit_y_proj(bld, r, s, f"{lay}f_wb", dict(arena_rdy),
                           [pspm["pm0"], pspm["pm1"], pspm["pm2"],
                            pspm["pm3"]],
                           bias_ld=pre["bias_ld"],
                           extra=(None if _os.environ.get("MK_NODELTA") == "1"
                                  else extra2),
                           wst_wait=wst_free, arena_q=arena_q,
                           pst_slots=slotsy, setup_cts=setup_cts)
        espm.close()
        stats_eA = emit_stats_evac(bld, r, s, slotsy, wres["slot_last"])
        esy.close()
        y_gate = dict(wres["y_out"])
        arena_reader = {s["eP"]: wres["mm_last"]}
        wst_next = {s["eP"]: wres["mm_last"]}

    if is_attn and lay == "l1":
        esy, psy = bld.psum_scope(
            [("st0", (128, cfg.QB), F32), ("st1", (128, cfg.QB), F32),
             ("st2", (128, cfg.QB), F32)])
        slotsy = stat_slots(cfg, [psy["st0"], psy["st1"], psy["st2"]])
        stats_eA = emit_stats0(bld, r, s, d["ybuf"].ap(), slotsy, setup_cts,
                               gate=y_gate)
        esy.close()

    esb2, pbb2 = bld.psum_scope([("b0", (128, cfg.QB), F32),
                                 ("b1", (128, cfg.QB), F32)])
    ba2 = emit_ln_finalize(bld, r, s, {s["eA"]: stats_eA}, pbb2, setup_cts)
    esb2.close()
    want_stats = (not is_last) and not FORCE_STATS0
    if want_stats:
        es3, ps3 = bld.psum_scope(
            [("st0", (128, cfg.QB), F32), ("st1", (128, cfg.QB), F32),
             ("st2", (128, cfg.QB), F32)])
        slots3 = stat_slots(cfg, [ps3["st0"], ps3["st1"], ps3["st2"]])
    else:
        slots3 = None
    out_sems, slot_last = emit_epilogue(
        bld, r, s, xin, d["ybuf"].ap(), xnext, ba2, slots3, want_stats,
        setup_cts, gate=y_gate)
    handoff = {"setup": setup_cts, "wst_free": wst_next,
               "arena_free": arena_reader,
               "x_ready": dict(out_sems), "sems": s,
               "epi_done": {s["eV"]: bld.ct[s["eV"]], s["eO"]: bld.ct[s["eO"]],
                            s["eA"]: bld.ct[s["eA"]], s["eP"]: bld.ct[s["eP"]]}}
    if want_stats:
        c = emit_stats_evac(bld, r, s, slots3, slot_last)
        es3.close()
        handoff["stats_ready"] = {s["eA"]: c}
    if is_last:
        import os as _os
        if _os.environ.get("MK_DBG_AR") == "1":
            r.on("sy", lambda e: e.dma_start(
                out=bld.dram["dbg_arena"].ap(),
                in_=bld.sb["arena"][:, :, :]),
                waits=dict(out_sems), inc=bld.dma_sem("dbg", 1), incn=16)
        r.on("sy", lambda e: None, waits=dict(out_sems))
    return handoff


def build_program(cfg, blocks):
    bld = Builder(cfg, blocks)
    with bld.nc.Block() as blk:
        prev = None
        for bi, b in enumerate(blocks):
            r = Rec(bld)
            prev = emit_block(bld, r, b, bi, prev, bi == len(blocks) - 1)
            bld.flush(blk, r)
    return bld


MODE_GROUPS = [["L0A", "L0F", "L1A", "L1F"]]
_CACHE = {}


def _get_builder(cfg_key, blocks):
    key = (cfg_key, tuple(blocks))
    if key not in _CACHE:
        _CACHE[key] = build_program(Cfg(mini=(cfg_key == "mini")), list(blocks))
    return _CACHE[key]


def kernel(**inputs):
    cfg = Cfg(mini=False)
    shared = prep_shared(cfg, inputs)
    x_full = np.asarray(inputs["L0_attn_x"], dtype=np.float32)
    B = x_full.shape[0]
    ncores = 8
    core_x = []
    ropes = []
    for c in range(ncores):
        b, half = c // 2, c % 2
        slab = x_full[b, half * cfg.TOK:(half + 1) * cfg.TOK, :].T
        core_x.append(np.ascontiguousarray(
            slab.reshape(cfg.DC, 128, cfg.NQB, cfg.QB).transpose(0, 2, 1, 3)))
        ropes.append(_rope_tables(cfg, half))

    for blocks in MODE_GROUPS:
        bld = _get_builder("full", blocks)
        names = [n for n, t in bld.dram.items()
                 if bld.nc.lookup_mls(t).kind == "ExternalInput"]
        in_maps = []
        for c in range(ncores):
            m = {}
            for n in names:
                if n == "x0":
                    m[n] = core_x[c]
                elif n == "ropeC":
                    m[n] = ropes[c][0]
                elif n == "ropeS":
                    m[n] = ropes[c][1]
                elif n == "iden_b":
                    m[n] = shared["iden_b"]
                else:
                    m[n] = shared[n]
            in_maps.append(m)
        res = run_bass_kernel_spmd(bld.nc, in_maps, core_ids=list(range(ncores)))
        core_x = [np.asarray(res.results[c]["xout"]) for c in range(ncores)]

    out = np.empty((B, 2 * cfg.TOK, cfg.H), dtype=np.float32)
    for c in range(ncores):
        b, half = c // 2, c % 2
        slab = np.asarray(core_x[c]).transpose(0, 2, 1, 3).reshape(
            cfg.H, cfg.TOK)
        out[b, half * cfg.TOK:(half + 1) * cfg.TOK, :] = slab.T
    return out


# version 15
# speedup vs baseline: 1.0996x; 1.0124x over previous
import math
from contextlib import ExitStack

import numpy as np
import ml_dtypes

import concourse.bass as bass
import concourse.mybir as mybir
from concourse.bass_utils import run_bass_kernel_spmd

F32 = mybir.dt.float32
BF16 = mybir.dt.bfloat16
AF = mybir.ActivationFunctionType
ALU = mybir.AluOpType
AX = mybir.AxisListType
BF = ml_dtypes.bfloat16

ENGS = ("sy", "po", "pe", "ac", "ve")
FORCE_STATS0 = False


class Cfg:
    def __init__(self, mini=False):
        if mini:
            self.H, self.TOK, self.NH, self.NKV1, self.QB = 512, 512, 4, 2, 256
        else:
            self.H, self.TOK, self.NH, self.NKV1, self.QB = 4096, 2048, 32, 8, 512
        self.E, self.R = 8, 16
        self.DC = self.H // 128
        self.NQB = self.TOK // self.QB
        self.S = 2 * self.TOK
        self.KT = self.S // 128
        self.KTL = self.TOK // 128
        self.ER = self.E * self.R
        self.REP1 = self.NH // self.NKV1
        assert self.NQB % 2 == 0
        self.NQH = self.NQB // 2
        self.NTT = self.TOK // 128
        self.TPQ = self.QB // 128
        self.SCALE = 1.0 / math.sqrt(128.0)
        self.CBIAS = -30.0
        self.EPS = 1e-5


def _tile_st(Wt):
    d, o = Wt.shape
    dcn, ocn = d // 128, o // 128
    return np.ascontiguousarray(
        Wt.reshape(dcn, 128, ocn, 128).transpose(2, 1, 0, 3)).astype(np.float32)


def _col_bias(b):
    return np.ascontiguousarray(np.asarray(b, np.float32).reshape(-1, 128).T)


def _rope_tables(cfg, half):
    pos = np.arange(cfg.TOK, dtype=np.float64) + half * cfg.TOK
    j = np.arange(64, dtype=np.float64)
    inv = 10000.0 ** (-j / 64.0)
    ang = pos[None, :] * inv[:, None]
    c, s = np.cos(ang), np.sin(ang)
    C = np.concatenate([c, c], 0).astype(np.float32)
    S = np.concatenate([-s, s], 0).astype(np.float32)
    return np.ascontiguousarray(C), np.ascontiguousarray(S)


def _sel_mat(cfg, scaling):
    sel = np.zeros((cfg.E, cfg.ER), np.float32)
    for e in range(cfg.E):
        sel[e, e * cfg.R:(e + 1) * cfg.R] = scaling
    return sel


def prep_shared(cfg, inp):
    f = lambda a: np.asarray(a, dtype=np.float32)
    d = {}
    d["l0a_wq"] = _tile_st(f(inp["L0_attn_Wq"]).T)
    d["l0a_wk"] = _tile_st(f(inp["L0_attn_Wk"]).T)
    d["l0a_wv"] = _tile_st(f(inp["L0_attn_Wv"]).T)
    d["l0a_wo"] = _tile_st(f(inp["L0_attn_Wo"]).T)
    d["l0f_wb"] = _tile_st(f(inp["L0_ffn_W_base"]).T)
    d["l0f_bb"] = _col_bias(inp["L0_ffn_b_base"])
    d["l0f_wg"] = np.ascontiguousarray(
        f(inp["L0_ffn_W_gate"]).T.reshape(cfg.DC, 128, cfg.E).transpose(1, 0, 2))
    d["l0f_wa"] = _tile_st(f(inp["L0_ffn_A"]).reshape(cfg.ER, cfg.H).T)[0]
    d["l0f_wm"] = np.ascontiguousarray(
        f(inp["L0_ffn_B"]).transpose(0, 2, 1).reshape(cfg.ER, cfg.H))
    d["l0f_sel"] = _sel_mat(cfg, 1.0)
    d["l1a_wq"] = _tile_st(f(inp["L1_attn_wq"]))
    d["l1a_bq"] = _col_bias(inp["L1_attn_bq"])
    d["l1a_wk"] = _tile_st(f(inp["L1_attn_wk"]))
    d["l1a_bk"] = _col_bias(inp["L1_attn_bk"])
    d["l1a_wv"] = _tile_st(f(inp["L1_attn_wv"]))
    d["l1a_bv"] = _col_bias(inp["L1_attn_bv"])
    d["l1f_wb"] = _tile_st(f(inp["L1_ffn_W_base"]).T)
    d["l1f_bb"] = _col_bias(inp["L1_ffn_b_base"])
    d["l1f_wg"] = np.ascontiguousarray(
        f(inp["L1_ffn_W_router"]).T.reshape(cfg.DC, 128, cfg.E).transpose(1, 0, 2))
    d["l1f_wa"] = _tile_st(f(inp["L1_ffn_A"]).reshape(cfg.ER, cfg.H).T)[0]
    d["l1f_wm"] = np.ascontiguousarray(
        f(inp["L1_ffn_B"]).transpose(0, 2, 1).reshape(cfg.ER, cfg.H))
    d["l1f_sel"] = _sel_mat(cfg, 2.0)
    d["iden_f"] = np.eye(128, dtype=np.float32)
    d["iden_b"] = np.eye(128, dtype=np.float32)
    return d


WEIGHT_SHAPES = None


class Rec:
    def __init__(self, bld):
        self.b = bld
        self.q = {e: [] for e in ENGS}

    def on(self, eng, fn, waits=None, inc=None, incn=1):
        b = self.b
        wl = []
        if waits:
            for s, v in waits.items():
                if s is None or v is None or v <= 0:
                    continue
                key = (eng, s)
                if b.lastw.get(key, 0) < v:
                    b.lastw[key] = v
                    wl.append((b.sems[s], v))
        inc_h = None
        if inc is not None:
            b.ct[inc] = b.ct.get(inc, 0) + incn
            inc_h = (b.sems[inc], incn)
        newct = b.ct.get(inc, 0) if inc is not None else None

        def thunk(e, wl=wl, fn=fn, inc_h=inc_h):
            for h, v in wl:
                e.wait_ge(h, v)
            ins = fn(e)
            if inc_h is not None:
                ins.then_inc(inc_h[0], inc_h[1])

        self.q[eng].append(thunk)
        return newct


class Builder:

    def __init__(self, cfg, blocks):
        self.cfg = cfg
        self.blocks = blocks
        self.nc = bass.Bass("TRN2", target_bir_lowering=False, debug=False)
        self.ctx = ExitStack()
        self.ct = {}
        self.lastw = {}
        self.sems = {}
        self._semi = 0
        self.dram = {}
        self.sb = {}
        self.dsems = {}
        self.guard = {}
        self.cc_sem = None
        self._declare_dram()
        self._alloc_sbuf()

    def new_sem(self, tag):
        name = f"s{self._semi}_{tag}"
        self._semi += 1
        self.sems[name] = self.ctx.enter_context(self.nc.semaphore(name))
        self.ct[name] = 0
        return name

    def blk_sems(self, tag):
        return {n: self.new_sem(f"{tag}{n}") for n in
                ("eP", "eA", "eV", "eO")}

    def dma_sem(self, stream, slot=0):
        key = (stream, slot)
        if key not in self.dsems:
            self.dsems[key] = self.new_sem(f"d_{stream}{slot}")
        return self.dsems[key]

    def need(self, key):
        return dict(self.guard.get(key, {}))

    def done(self, key, sem, ct):
        g = self.guard.setdefault(key, {})
        if ct is not None and (sem not in g or g[sem] < ct):
            g[sem] = ct

    def set_guard(self, key, d):
        g = self.guard.setdefault(key, {})
        for k, v in d.items():
            if v and (k not in g or g[k] < v):
                g[k] = v

    def _declare_dram(self):
        cfg, nc = self.cfg, self.nc
        H, TOK = cfg.H, cfg.TOK

        def din(name, shape, dt_=F32):
            self.dram[name] = nc.dram_tensor(name, list(shape), dt_, kind="ExternalInput")

        def dint(name, shape, dt_=F32):
            self.dram[name] = nc.dram_tensor(name, list(shape), dt_)

        din("x0", (cfg.DC, cfg.NQB, 128, cfg.QB))
        self.dram["xout"] = nc.dram_tensor(
            "xout", [cfg.DC, cfg.NQB, 128, cfg.QB], F32, kind="ExternalOutput")
        din("ropeC", (128, TOK))
        din("ropeS", (128, TOK))
        din("iden_f", (128, 128))
        din("iden_b", (128, 128))

        shp = {}
        for lay in ("l0", "l1"):
            shp[f"{lay}f_wb"] = (cfg.DC, 128, cfg.DC, 128)
            shp[f"{lay}f_bb"] = (128, cfg.DC)
            shp[f"{lay}f_wg"] = (128, cfg.DC, cfg.E)
            shp[f"{lay}f_wa"] = (128, cfg.DC, 128)
            shp[f"{lay}f_wm"] = (cfg.ER, cfg.H)
            shp[f"{lay}f_sel"] = (cfg.E, cfg.ER)
        for w in ("wq", "wk", "wv", "wo"):
            shp[f"l0a_{w}"] = (cfg.DC, 128, cfg.DC, 128)
        shp["l1a_wq"] = (cfg.DC, 128, cfg.DC, 128)
        shp["l1a_wk"] = (cfg.NKV1, 128, cfg.DC, 128)
        shp["l1a_wv"] = (cfg.NKV1, 128, cfg.DC, 128)
        shp["l1a_bq"] = (128, cfg.DC)
        shp["l1a_bk"] = (128, cfg.NKV1)
        shp["l1a_bv"] = (128, cfg.NKV1)

        need = set()
        for blk in self.blocks:
            lay = blk[:2].lower()
            if blk.endswith("A"):
                if lay == "l0":
                    need |= {"l0a_wq", "l0a_wk", "l0a_wv", "l0a_wo"}
                else:
                    need |= {"l1a_wq", "l1a_bq", "l1a_wk", "l1a_bk", "l1a_wv", "l1a_bv"}
            else:
                need |= {f"{lay}f_{s}" for s in ("wb", "bb", "wg", "wa", "wm", "sel")}
        for n in sorted(need):
            din(n, shp[n])

        import os as _os
        _dbg = _os.environ.get("MK_DBG_MID") == "1"
        tshape = [cfg.DC, cfg.NQB, 128, cfg.QB]
        for i in range(len(self.blocks) - 1):
            if _dbg:
                self.dram[f"xmid{i}"] = nc.dram_tensor(
                    f"xmid{i}", tshape, F32, kind="ExternalOutput")
            else:
                dint(f"xmid{i}", tshape)
        if _dbg:
            self.dram["ybuf"] = nc.dram_tensor("ybuf", tshape, F32,
                                               kind="ExternalOutput")
        else:
            dint("ybuf", tshape)
        if _os.environ.get("MK_DBG_AR") == "1":
            self.dram["dbg_arena"] = nc.dram_tensor(
                "dbg_arena", [128, self.cfg.DC, TOK], BF16, kind="ExternalOutput")
        if _os.environ.get("MK_DBG_RT") == "1":
            cfgE, NTT = self.cfg.E, self.cfg.NTT
            self.dram["dbg_lgtm"] = nc.dram_tensor("dbg_lgtm", [128, NTT * cfgE], F32, kind="ExternalOutput")
            self.dram["dbg_wtm"] = nc.dram_tensor("dbg_wtm", [128, NTT * cfgE], F32, kind="ExternalOutput")
            self.dram["dbg_hw"] = nc.dram_tensor("dbg_hw", [128, TOK], BF16, kind="ExternalOutput")
            self.dram["dbg_h"] = nc.dram_tensor("dbg_h", [128, TOK], F32, kind="ExternalOutput")
            self.dram["dbg_wtb"] = nc.dram_tensor("dbg_wtb", [cfgE, TOK], BF16, kind="ExternalOutput")
        if any(b.endswith("A") for b in self.blocks):
            dint("qrope", (cfg.NH, 128, TOK), BF16)
            dint("kme", (cfg.NH * 128, TOK), BF16)
            dint("kall", (2 * cfg.NH * 128, TOK), BF16)
            dint("vme", (cfg.NH * cfg.KTL * 128, 128), BF16)
            dint("vall", (2 * cfg.NH * cfg.KTL * 128, 128), BF16)

    def xbuf(self, i):
        if i == 0:
            return self.dram["x0"].ap()
        if i == len(self.blocks):
            return self.dram["xout"].ap()
        return self.dram[f"xmid{i-1}"].ap()

    def _alloc_sbuf(self):
        cfg, nc = self.cfg, self.nc
        H, TOK, DC, QB, E = cfg.H, cfg.TOK, cfg.DC, cfg.QB, cfg.E
        rem = nc.sbuf_bytes_remaining
        owner = nc.alloc_sbuf_tensor("OWNER", [128, rem - 256], mybir.dt.uint8)
        base = nc.lookup_mloc(owner).addr
        self._limit = rem - 256

        def at(name, shape, dt_, off):
            off = (off + 31) // 32 * 32
            bpp = int(np.prod(shape[1:])) * mybir.dt.size(dt_)
            assert off + bpp <= self._limit, f"{name}: {off}+{bpp} > {self._limit}"
            t = nc.alloc_sbuf_tensor_at(name, list(shape), dt_, offset=base + off)
            self.sb[name] = t
            return t, (off + bpp + 31) // 32 * 32

        off = 0
        _, off = at("arena", (128, DC, TOK), BF16, off)
        wst_off = off
        _, off = at("wst", (128, 2, DC, 128), BF16, off)
        if cfg.H != 4096:
            _, off = at("Ba", (128, TOK), F32, off)
            _, off = at("Bc", (128, TOK), F32, off)
            _, off = at("ep_x", (128, 4, QB), F32, off)
            _, off = at("ep_t", (128, 2, QB), F32, off)
            _, off = at("ep_y", (128, 4, QB), F32, off)
            _, off = at("ep_o", (128, 2, QB), F32, off)
            _, off = at("ep_b", (128, 2, QB), BF16, off)
            _, off = at("ep_b2", (128, 2, QB), BF16, off)
            _, off = at("prj_raw", (128, 2, QB), F32, off)
            _, off = at("prj_sw", (128, 2, QB), F32, off)
            _, off = at("prj_t1", (128, 2, QB), F32, off)
            _, off = at("prj_qk", (128, 4, QB), BF16, off)
            _, off = at("prj_v", (128, 2, QB), BF16, off)
            _, off = at("prj_vt", (128, 2, cfg.TPQ, 128), BF16, off)
            _, off = at("ropeCsb", (128, TOK), F32, off)
            _, off = at("ropeSsb", (128, TOK), F32, off)
            _, off = at("kbuf", (128, 2, cfg.KT, 128), BF16, off)
            _, off = at("vbuf", (128, 2, cfg.KT, 128), BF16, off)
            _, off = at("qbuf", (128, 2, TOK), BF16, off)
            _, off = at("acc", (128, 2, QB), F32, off)
            _, off = at("ffn_h", (128, TOK), F32, off)
            _, off = at("ffn_hw", (128, TOK), BF16, off)
            _, off = at("lgs", (E, 2, QB), F32, off)
            _, off = at("fy", (128, 4, QB), F32, off)
            _, off = at("wTb", (E, TOK), BF16, off)
            _, off = at("ffn_a", (128, DC, 128), BF16, off)
            _, off = at("ffn_b", (128, H), BF16, off)
            _, off = at("att_y", (128, 4, QB), F32, off)
            _, off = at("att_b", (128, 4, QB), BF16, off)
            _, off = at("att_b2", (128, 4, QB), BF16, off)
        else:
            PH = 46 * 1024 if cfg.H == 4096 else 30 * 1024
            ph0 = off
            off += PH
            o = ph0
            _, o = at("Ba", (128, TOK), F32, o)
            _, o = at("Bc", (128, TOK), F32, o)
            ep_x_off = o
            _, o = at("ep_x", (128, 4, QB), F32, o)
            _, o = at("ep_t", (128, 2, QB), F32, o)
            _, o = at("ep_y", (128, 4, QB), F32, o)
            _, o = at("ep_o", (128, 2, QB), F32, o)
            epb_off = o
            _, o = at("ep_b", (128, 2, QB), BF16, o)
            _, o = at("ep_b2", (128, 2, QB), BF16, o)
            assert o - ph0 <= PH, f"epi overlay {o - ph0} > {PH}"
            self._epb_off = epb_off
            o = ph0
            _, o = at("prj_raw", (128, 2, QB), F32, o)
            _, o = at("prj_sw", (128, 2, QB), F32, o)
            _, o = at("prj_t1", (128, 2, QB), F32, o)
            _, o = at("prj_qk", (128, 4, QB), BF16, o)
            _, o = at("prj_v", (128, 2, QB), BF16, o)
            _, o = at("prj_vt", (128, 2, cfg.TPQ, 128), BF16, o)
            ropec_off = ph0 + (PH - 2 * TOK * 4)
            assert o <= ropec_off
            _, o2 = at("ropeCsb", (128, TOK), F32, ropec_off)
            _, o2 = at("ropeSsb", (128, TOK), F32, o2)
            assert o2 - ph0 <= PH
            o = ph0
            _, o = at("kbuf", (128, 2, cfg.KT, 128), BF16, o)
            _, o = at("vbuf", (128, 2, cfg.KT, 128), BF16, o)
            _, o = at("qbuf", (128, 2, TOK), BF16, o)
            _, o = at("acc", (128, 2, QB), F32, o)
            assert o - ph0 <= PH, f"attn overlay {o - ph0} > {PH}"
            o = ph0
            _, o = at("ffn_h", (128, TOK), F32, o)
            _, o = at("ffn_hw", (128, TOK), BF16, o)
            _, o = at("lgs", (cfg.E, 2, QB), F32, o)
            assert o <= ep_x_off
            _, o = at("fy", (128, 4, QB), F32, ep_x_off)
            o2 = ep_x_off + 4 * QB * 4
            _, o2 = at("wTb", (cfg.E, TOK), BF16, o2)
            _, o2 = at("ffn_a", (128, DC, 128), BF16, o2)
            _, o2 = at("ffn_b", (128, H), BF16, o2)
            assert o2 - ph0 <= PH, f"ffn overlay {o2 - ph0} > {PH}"

            at("att_y", (128, 4, QB), F32, 0)
            at("att_b", (128, 4, QB), BF16, 4 * QB * 4)
            at("att_b2", (128, 4, QB), BF16, 4 * QB * 4 + 4 * QB * 2)

        if cfg.H == 4096:
            assert 6 * QB * 2 <= DC * 128 * 2
            at("pt", (128, 6, QB), BF16, wst_off + DC * 128 * 2)
            at("rb", (128, 2, QB), F32, self._epb_off)
        else:
            _, off = at("pt", (128, 6, QB), BF16, off)
            _, off = at("rb", (128, 2, QB), F32, off)

        _, off = at("stat", (128, max(TOK, 4 * QB)), F32, off)
        _, off = at("iden_fsb", (128, 128), F32, off)
        _, off = at("iden_bsb", (128, 128), BF16, off)
        _, off = at("ones_b", (128, 1), BF16, off)
        _, off = at("ones_f", (128, 128), F32, off)
        _, off = at("konst", (128, 4), F32, off)
        _, off = at("bias_sb", (128, DC), F32, off)
        _, off = at("b_q", (128, DC), F32, off)
        _, off = at("b_k", (128, cfg.NKV1), F32, off)
        _, off = at("b_v", (128, cfg.NKV1), F32, off)
        _, off = at("wg_sb", (128, DC, E), BF16, off)
        _, off = at("sel_sb", (E, cfg.ER), BF16, off)
        _, off = at("r16", (128, 6, cfg.NTT), F32, off)
        _, off = at("r8", (128, 3, E), F32, off)
        _, off = at("lg_tm", (128, cfg.NTT * E), F32, off)
        _, off = at("t_tm", (128, cfg.NTT * E), F32, off)
        _, off = at("w_tm", (128, cfg.NTT * E), F32, off)
        _, off = at("yb2", (128, 2, cfg.QB), BF16, off)
        assert off <= self._limit, f"static overflow {off} > {self._limit}"
        self.sbuf_used = off
        self._psi = 0

    def psum_banks(self, h):
        mloc = self.nc.lookup_mloc(h)
        base = mloc.addr - self.nc.psum_base
        bpp = int(np.prod(h.shape[1:])) * mybir.dt.size(h.dtype)
        b0 = base // 2048
        b1 = (base + bpp - 1) // 2048
        return list(range(b0, b1 + 1))

    def psum_need(self, h):
        w = {}
        for b in self.psum_banks(h):
            for k, v in self.guard.get(("bank", b), {}).items():
                w[k] = max(w.get(k, 0), v)
        return w

    def psum_done(self, h, sem, ct):
        for b in self.psum_banks(h):
            self.done(("bank", b), sem, ct)

    def psum_scope(self, spec):
        es = ExitStack()
        out = {}
        for name, shape, dt_ in spec:
            self._psi += 1
            out[name] = es.enter_context(
                self.nc.psum_tensor(f"{name}_{self._psi}", list(shape), dt_))
        return es, out

    def flush(self, block, rec):
        ename = {"sy": "sync", "po": "gpsimd", "pe": "tensor", "ac": "scalar",
                 "ve": "vector"}
        for e in ENGS:
            thunks = rec.q[e]
            if not thunks:
                continue

            def body(engine, thunks=thunks):
                for t in thunks:
                    t(engine)
            getattr(block, ename[e])(body)


def stat_slots(cfg, banks):
    return [(banks[i // 3], 32 * (i % 3)) for i in range(2 * cfg.NQB)]


RG = [[0, 1], [2, 3], [4, 5], [6, 7]]


def _dma(bld, r, eng, stream, slot, fn, waits=None):
    sem = bld.dma_sem(stream, slot)
    w = bld.need((stream, slot))
    if waits:
        for k, v in waits.items():
            if v:
                w[k] = max(w.get(k, 0), v)
    ct = r.on(eng, fn, waits=w, inc=sem, incn=16)
    return sem, ct


def emit_setup(bld, r, s):
    sb, d = bld.sb, bld.dram
    cts = {}
    cts["iden_f"] = dict([_dma(bld, r, "sy", "iden_f", 0, lambda e: e.dma_start(
        out=sb["iden_fsb"][:, :], in_=d["iden_f"].ap()))])
    cts["iden_b"] = dict([_dma(bld, r, "po", "iden_b", 0, lambda e: e.dma_start(
        out=sb["iden_bsb"][:, :], in_=d["iden_b"].ap()))])
    r.on("po", lambda e: e.memset(sb["ones_b"][:, :], 1.0))
    r.on("po", lambda e: e.memset(sb["ones_f"][:, :], 1.0))
    r.on("po", lambda e: e.memset(sb["konst"][:, 0:1], bld.cfg.CBIAS))
    cts["konst"] = {s["eO"]: r.on("po", lambda e: e.memset(
        sb["konst"][:, 1:2], bld.cfg.EPS), inc=s["eO"])}
    return cts


def emit_stats0(bld, r, s, xin, pst_slots, setup_cts, gate=None):
    cfg, sb = bld.cfg, bld.sb
    DC, NQB, QB = cfg.DC, cfg.NQB, cfg.QB
    s1_ct, s2_ct = {}, {}
    slot_last = {}
    for qb in range(NQB):
        for dc in range(DC):
            idx = qb * DC + dc
            s4, s2 = idx % 4, idx % 2
            dsem, dct = _dma(bld, r, "sy", "x", s4,
                             lambda e, dc=dc, qb=qb, s4=s4: e.dma_start(
                                 out=sb["ep_x"][:, s4, :],
                                 in_=xin[dc, qb]),
                             waits=gate)
            qct = r.on("ac", lambda e, s4=s4, s2=s2: e.square(
                sb["ep_b2"][:, s2, :], sb["ep_x"][:, s4, :]),
                waits={dsem: dct, s["eP"]: s2_ct.get(idx - 2, 0)}, inc=s["eA"])
            b1, m1 = pst_slots[2 * qb]
            b2, m2 = pst_slots[2 * qb + 1]
            w1 = dict(setup_cts["konst"])
            w1[dsem] = max(w1.get(dsem, 0), dct)
            if dc == 0:
                for k2, v2 in bld.psum_need(b1).items():
                    w1[k2] = max(w1.get(k2, 0), v2)
            c1 = r.on("pe", lambda e, b1=b1, m1=m1, s4=s4, dc=dc: e.matmul(
                b1[m1:m1 + 1, :], sb["ones_f"][:, 0:1], sb["ep_x"][:, s4, :],
                start=(dc == 0), stop=(dc == DC - 1)),
                waits=w1, inc=s["eP"])
            s1_ct[idx] = c1
            bld.set_guard(("x", s4), {s["eA"]: qct, s["eP"]: c1})
            w2b = {s["eA"]: qct}
            if dc == 0:
                for k2, v2 in bld.psum_need(b2).items():
                    w2b[k2] = max(w2b.get(k2, 0), v2)
            c2 = r.on("pe", lambda e, b2=b2, m2=m2, s2=s2, dc=dc: e.matmul(
                b2[m2:m2 + 1, :], sb["ones_b"][:, :], sb["ep_b2"][:, s2, :],
                start=(dc == 0), stop=(dc == DC - 1)),
                waits=w2b, inc=s["eP"])
            s2_ct[idx] = c2
            slot_last[2 * qb] = c1
            slot_last[2 * qb + 1] = c2
    return emit_stats_evac(bld, r, s, pst_slots, slot_last)


def emit_stats_evac(bld, r, s, pst_slots, slot_last):
    cfg, sb = bld.cfg, bld.sb
    QB = cfg.QB
    last = 0
    for qb in range(cfg.NQB):
        for k in range(2):
            b, m = pst_slots[2 * qb + k]
            prow = 0 if k == 0 else 32
            last = r.on("ac", lambda e, b=b, m=m, prow=prow, qb=qb: e.copy(
                sb["stat"][prow:prow + 1, qb * QB:(qb + 1) * QB], b[m:m + 1, :]),
                waits={s["eP"]: slot_last[2 * qb + k]}, inc=s["eA"])
            bld.psum_done(b, s["eA"], last)
    return last


def emit_ln_finalize(bld, r, s, stats_ready, pb, setup_cts):
    cfg, sb = bld.cfg, bld.sb
    QB = cfg.QB
    inv_h = 1.0 / cfg.H
    ba_ct = 0
    free_b = {0: 0, 1: 0}
    scr_free = 0
    for qb in range(cfg.NQB):
        w = dict(stats_ready)
        for k2, v2 in setup_cts["konst"].items():
            w[k2] = max(w.get(k2, 0), v2)
        w[s["eA"]] = max(w.get(s["eA"], 0), free_b[0])
        for k2, v2 in bld.psum_need(pb["b0"]).items():
            w[k2] = max(w.get(k2, 0), v2)
        c0 = r.on("pe", lambda e, qb=qb: e.matmul(
            pb["b0"][:, :], sb["ones_f"][0:1, :],
            sb["stat"][0:1, qb * QB:(qb + 1) * QB], start=True, stop=True),
            waits=w, inc=s["eP"])
        w2 = dict(stats_ready)
        for k2, v2 in setup_cts["konst"].items():
            w2[k2] = max(w2.get(k2, 0), v2)
        w2[s["eA"]] = max(w2.get(s["eA"], 0), free_b[1])
        for k2, v2 in bld.psum_need(pb["b1"]).items():
            w2[k2] = max(w2.get(k2, 0), v2)
        c1 = r.on("pe", lambda e, qb=qb: e.matmul(
            pb["b1"][:, :], sb["ones_f"][32:33, :],
            sb["stat"][32:33, qb * QB:(qb + 1) * QB], start=True, stop=True),
            waits=w2, inc=s["eP"])
        cm = r.on("ac", lambda e: e.mul(sb["ep_t"][:, 0, :], pb["b0"][:, :], inv_h),
                  waits={s["eP"]: c0, s["eV"]: scr_free}, inc=s["eA"])
        free_b[0] = cm
        bld.psum_done(pb["b0"], s["eA"], cm)
        ce = r.on("ac", lambda e: e.mul(sb["ep_t"][:, 1, :], pb["b1"][:, :], inv_h),
                  waits={s["eP"]: c1}, inc=s["eA"])
        free_b[1] = ce
        bld.psum_done(pb["b1"], s["eA"], ce)
        cq = r.on("ac", lambda e: e.square(sb["ep_o"][:, 0, :], sb["ep_t"][:, 0, :]),
                  waits={s["eA"]: cm}, inc=s["eA"])
        cv = r.on("ve", lambda e: e.tensor_sub(
            sb["ep_o"][:, 1, :], sb["ep_t"][:, 1, :], sb["ep_o"][:, 0, :]),
            waits={s["eA"]: cq}, inc=s["eV"])
        cs = r.on("ac", lambda e: e.activation(
            sb["ep_t"][:, 1, :], sb["ep_o"][:, 1, :], AF.Ln,
            bias=sb["konst"][:, 1:2], scale=1.0),
            waits=dict(setup_cts["konst"], **{s["eV"]: cv}), inc=s["eA"])
        cr = r.on("ac", lambda e, qb=qb: e.activation(
            sb["Ba"][:, qb * QB:(qb + 1) * QB], sb["ep_t"][:, 1, :],
            AF.Exp, scale=-0.5),
            waits={s["eA"]: cs}, inc=s["eA"])
        ba_ct = r.on("ve", lambda e, qb=qb: e.tensor_mul(
            sb["Bc"][:, qb * QB:(qb + 1) * QB], sb["ep_t"][:, 0, :],
            sb["Ba"][:, qb * QB:(qb + 1) * QB]),
            waits={s["eA"]: cr}, inc=s["eV"])
        scr_free = ba_ct
    return ba_ct


def emit_ln_apply(bld, r, s, xin, ba_ct, extra_wait=None):
    cfg, sb = bld.cfg, bld.sb
    DC, NQB, QB = cfg.DC, cfg.NQB, cfg.QB
    o_ct = {}
    last = 0
    arena_q = []
    for qb in range(NQB):
        for dc in range(DC):
            idx = qb * DC + dc
            s4, s2 = idx % 4, idx % 2
            dsem, dct = _dma(bld, r, "sy", "x", s4,
                             lambda e, dc=dc, qb=qb, s4=s4: e.dma_start(
                                 out=sb["ep_x"][:, s4, :],
                                 in_=xin[dc, qb]),
                             waits=extra_wait)
            c = r.on("po", lambda e, s4=s4, qb=qb: e.tensor_mul(
                sb["ep_x"][:, s4, :], sb["ep_x"][:, s4, :],
                sb["Ba"][:, qb * QB:(qb + 1) * QB]),
                waits={dsem: dct, s["eV"]: ba_ct},
                inc=s["eO"])
            last = r.on("ve", lambda e, s4=s4, dc=dc, qb=qb: e.tensor_tensor(
                bld.sb["arena"][:, dc, qb * QB:(qb + 1) * QB],
                sb["ep_x"][:, s4, :], sb["Bc"][:, qb * QB:(qb + 1) * QB],
                ALU.subtract),
                waits={s["eO"]: c}, inc=s["eV"])
            bld.set_guard(("x", s4), {s["eV"]: last})
            o_ct[idx] = last
        arena_q.append({s["eV"]: last})
    return {s["eV"]: last}, arena_q


def emit_proj(bld, r, s, lay, kind, arena_rdy, pp, ptr, setup_cts,
              bias_ld=None, rope_ld=None, wst_free=None, preloaded=None,
              arena_q=None):
    cfg, sb, d = bld.cfg, bld.sb, bld.dram
    DC, NQB, QB, TPQ = cfg.DC, cfg.NQB, cfg.QB, cfg.TPQ
    if lay == "l0":
        OC, bias = DC, None
        wname = f"l0a_w{kind}"
    else:
        OC = DC if kind == "q" else cfg.NKV1
        bias = {"q": "b_q", "k": "b_k", "v": "b_v"}[kind]
        wname = f"l1a_w{kind}"
    rope = kind in ("q", "k")
    evac_ct, mm_last = {}, {}
    raw_free, tr_ct, vt_ct = {}, {}, {}
    out_sems = {}
    ev = 0
    for oc in range(OC):
        par = oc % 2
        if preloaded and oc in preloaded:
            wsem, wct = preloaded[oc]
        else:
            w = {s["eP"]: mm_last.get(oc - 2, 0)}
            if wst_free:
                for k2, v2 in wst_free.items():
                    w[k2] = max(w.get(k2, 0), v2)
            wsem, wct = _dma(bld, r, "po", "w", par,
                             lambda e, oc=oc, par=par: e.dma_start(
                                 out=sb["wst"][:, par, :, :],
                                 in_=d[wname].ap()[oc]),
                             waits=w)
        for qb in range(NQB):
            bank = pp[qb % 2]
            eidx = oc * NQB + qb
            for dc in range(DC):
                w = {}
                if dc == 0:
                    w = dict(arena_q[qb]) if arena_q else dict(arena_rdy)
                    w[wsem] = max(w.get(wsem, 0), wct)
                    for k2, v2 in bld.psum_need(bank).items():
                        w[k2] = max(w.get(k2, 0), v2)
                c = r.on("pe", lambda e, par=par, dc=dc, qb=qb, bank=bank: e.matmul(
                    bank[:, :], sb["wst"][:, par, dc, :],
                    sb["arena"][:, dc, qb * QB:(qb + 1) * QB],
                    start=(dc == 0), stop=(dc == DC - 1)),
                    waits=w, inc=s["eP"])
            mm_last[oc] = c
            bld.set_guard(("w", par), {s["eP"]: c})
            if rope:
                r2, s4 = ev % 2, ev % 4
                wv = {s["eP"]: c, s["eV"]: raw_free.get(ev - 2, 0)}
                if bias:
                    for k2, v2 in bias_ld.items():
                        wv[k2] = max(wv.get(k2, 0), v2)
                    ct_raw = r.on("ac", lambda e, bank=bank, r2=r2, oc=oc,
                                  bias=bias: e.activation(
                                      sb["prj_raw"][:, r2, :], bank[:, :],
                                      AF.Identity, bias=sb[bias][:, oc:oc + 1],
                                      scale=1.0),
                                  waits=wv, inc=s["eA"])
                else:
                    ct_raw = r.on("ac", lambda e, bank=bank, r2=r2: e.copy(
                        sb["prj_raw"][:, r2, :], bank[:, :]), waits=wv, inc=s["eA"])
                evac_ct[eidx] = ct_raw
                bld.psum_done(bank, s["eA"], ct_raw)
                r.on("ve", lambda e, r2=r2: e.tensor_copy(
                    sb["prj_sw"][0:64, r2, :], sb["prj_raw"][64:128, r2, :]),
                    waits={s["eA"]: ct_raw}, inc=s["eV"])
                c2 = r.on("ve", lambda e, r2=r2: e.tensor_copy(
                    sb["prj_sw"][64:128, r2, :], sb["prj_raw"][0:64, r2, :]),
                    waits={s["eA"]: ct_raw}, inc=s["eV"])
                wv = {s["eA"]: ct_raw}
                for k2, v2 in rope_ld.items():
                    wv[k2] = max(wv.get(k2, 0), v2)
                ct_t1 = r.on("ve", lambda e, r2=r2, qb=qb: e.tensor_mul(
                    sb["prj_t1"][:, r2, :], sb["prj_raw"][:, r2, :],
                    sb["ropeCsb"][:, qb * QB:(qb + 1) * QB]),
                    waits=wv, inc=s["eV"])
                wv2 = {s["eV"]: c2}
                for k2, v2 in rope_ld.items():
                    wv2[k2] = max(wv2.get(k2, 0), v2)
                ct_sw = r.on("ve", lambda e, r2=r2, qb=qb: e.tensor_mul(
                    sb["prj_sw"][:, r2, :], sb["prj_sw"][:, r2, :],
                    sb["ropeSsb"][:, qb * QB:(qb + 1) * QB]),
                    waits=wv2, inc=s["eV"])
                wq = bld.need(("qk", s4))
                wq[s["eV"]] = max(wq.get(s["eV"], 0), ct_t1, ct_sw)
                ct_out = r.on("ve", lambda e, r2=r2, s4=s4: e.tensor_tensor(
                    sb["prj_qk"][:, s4, :], sb["prj_t1"][:, r2, :],
                    sb["prj_sw"][:, r2, :], ALU.add),
                    waits=wq, inc=s["eV"])
                raw_free[ev] = ct_out
                if kind == "q":
                    dst = d["qrope"].ap()[oc, :, qb * QB:(qb + 1) * QB]
                else:
                    dst = d["kme"].ap()[oc * 128:(oc + 1) * 128,
                                        qb * QB:(qb + 1) * QB]
                osem, oct = _dma(bld, r, "sy", "qk", s4,
                                 lambda e, s4=s4, dst=dst: e.dma_start(
                                     out=dst, in_=sb["prj_qk"][:, s4, :]),
                                 waits={s["eV"]: ct_out})
                bld.set_guard(("qk", s4), {osem: oct})
                out_sems[osem] = oct
            else:
                sv = ev % 2
                wv = {s["eP"]: max(c, tr_ct.get(ev - 2, 0))}
                if bias:
                    for k2, v2 in bias_ld.items():
                        wv[k2] = max(wv.get(k2, 0), v2)
                    ct_v = r.on("ac", lambda e, bank=bank, sv=sv, oc=oc, bias=bias:
                                e.activation(sb["prj_v"][:, sv, :], bank[:, :],
                                             AF.Identity,
                                             bias=sb[bias][:, oc:oc + 1], scale=1.0),
                                waits=wv, inc=s["eA"])
                else:
                    ct_v = r.on("ac", lambda e, bank=bank, sv=sv: e.copy(
                        sb["prj_v"][:, sv, :], bank[:, :]), waits=wv, inc=s["eA"])
                evac_ct[eidx] = ct_v
                bld.psum_done(bank, s["eA"], ct_v)
                trc = 0
                wv = {s["eA"]: max(ct_v, vt_ct.get(ev - 1, 0))}
                for k2, v2 in bld.psum_need(ptr).items():
                    wv[k2] = max(wv.get(k2, 0), v2)
                for k2, v2 in setup_cts["iden_b"].items():
                    wv[k2] = max(wv.get(k2, 0), v2)
                for t in range(TPQ):
                    trc = r.on("pe", lambda e, sv=sv, t=t: e.transpose(
                        ptr[:, t * 128:(t + 1) * 128],
                        sb["prj_v"][:, sv, t * 128:(t + 1) * 128],
                        sb["iden_bsb"][:, :]),
                        waits=wv, inc=s["eP"])
                tr_ct[ev] = trc
                wv = bld.need(("vt", sv))
                wv[s["eP"]] = max(wv.get(s["eP"], 0), trc)
                cvt = r.on("ac", lambda e, sv=sv: e.copy(
                    sb["prj_vt"][:, sv, :, :].rearrange("p a b -> p (a b)"),
                    ptr[:, 0:TPQ * 128]),
                    waits=wv, inc=s["eA"])
                vt_ct[ev] = cvt
                bld.psum_done(ptr, s["eA"], cvt)
                vview = d["vme"].ap().rearrange(
                    "(h p kt) dd -> h p kt dd", kt=cfg.KTL, p=128)
                dst = vview[oc, :, qb * TPQ:(qb + 1) * TPQ, :]
                osem, oct = _dma(bld, r, "sy", "vt", sv,
                                 lambda e, sv=sv, dst=dst: e.dma_start(
                                     out=dst, in_=sb["prj_vt"][:, sv, :, :]),
                                 waits={s["eA"]: cvt})
                bld.set_guard(("vt", sv), {osem: oct})
                out_sems[osem] = oct
            ev += 1
    return {"out": out_sems, "mm_last": max(mm_last.values())}


def emit_exchange(bld, r, s, k_out, v_out, nkv):
    d = bld.dram
    cfg = bld.cfg
    if bld.cc_sem is None:
        bld.cc_sem = bld.new_sem("cc")
    cc = bld.cc_sem
    krows = nkv * 128
    vrows = nkv * cfg.KTL * 128
    r.on("po", lambda e: e.collective_compute(
        "AllGather", ALU.bypass, replica_groups=RG,
        ins=[d["kme"].ap()[0:krows, :].opt()],
        outs=[d["kall"].ap()[0:2 * krows, :].opt()]),
        waits=dict(k_out), inc=cc)
    ct = r.on("po", lambda e: e.collective_compute(
        "AllGather", ALU.bypass, replica_groups=RG,
        ins=[d["vme"].ap()[0:vrows, :].opt()],
        outs=[d["vall"].ap()[0:2 * vrows, :].opt()]),
        waits=dict(v_out), inc=cc)
    return {cc: ct}


def emit_attention(bld, r, s, lay, setup_cts, kv_cc, q_ready):
    cfg, sb, d = bld.cfg, bld.sb, bld.dram
    QB, KT, KTL = cfg.QB, cfg.KT, cfg.KTL
    NKV = cfg.NH if lay == "l0" else cfg.NKV1
    REP = 1 if lay == "l0" else cfg.REP1
    PREF = min(8, max(KT - 2, 0))
    es, ps = bld.psum_scope(
        [("sc0", (128, 2 * QB), F32), ("sc1", (128, 2 * QB), F32),
         ("pl", (128, QB), F32), ("po0", (128, QB), F32),
         ("po1", (128, QB), F32), ("bc", (128, QB), F32)])

    kall = d["kall"].ap()[0:2 * NKV * 128, :].rearrange(
        "(rr h p) t -> rr h p t", rr=2, p=128)
    vall = d["vall"].ap()[0:2 * NKV * KTL * 128, :].rearrange(
        "(rr h p kt) dd -> rr h p kt dd", rr=2, kt=KTL, p=128)
    kv_rd, q_rd = {}, {}
    kv_used, q_used = {}, {}
    bank_free = {}
    pt_free = {}
    acc_free = {0: {}, 1: {}}
    o_free = {}
    y_out = {}
    out_eV = 0

    def load_kv(kvi):
        if kvi >= NKV:
            return
        par = kvi % 2
        w = dict(kv_cc)
        for k2, v2 in q_ready.items():
            w[k2] = max(w.get(k2, 0), v2)
        w[s["eP"]] = kv_used.get(kvi - 2, 0)
        c = 0
        for rr in range(2):
            c = r.on("sy", lambda e, rr=rr, kvi=kvi, par=par: e.dma_start(
                out=sb["kbuf"][:, par, rr * KTL:(rr + 1) * KTL, :],
                in_=kall[rr, kvi].rearrange("p (kt j) -> p kt j", j=128)),
                waits=w, inc=bld.dma_sem("kv", par), incn=16)
            c = r.on("sy", lambda e, rr=rr, kvi=kvi, par=par: e.dma_start(
                out=sb["vbuf"][:, par, rr * KTL:(rr + 1) * KTL, :],
                in_=vall[rr, kvi]),
                waits=w, inc=bld.dma_sem("kv", par), incn=16)
        kv_rd[kvi] = {bld.dma_sem("kv", par): c}

    def load_q(qi):
        if qi >= NKV * REP:
            return
        qpar = qi % 2
        qs = bld.dma_sem("q", qpar)
        qc = r.on("sy", lambda e, qi=qi, qpar=qpar: e.dma_start(
            out=sb["qbuf"][:, qpar, :], in_=d["qrope"].ap()[qi]),
            waits=dict(q_ready, **{s["eP"]: q_used.get(qi - 2, 0)}),
            inc=qs, incn=16)
        q_rd[qi] = {qs: qc}

    load_kv(0)
    load_kv(1)
    load_q(0)
    load_q(1)
    for kvi in range(NKV):
        par = kvi % 2
        for rep in range(REP):
            qi = kvi * REP + rep
            qpar = qi % 2
            for half in range(cfg.NQH):
                exp_ct = {}
                sc_ct = {}
                add_ct = {}
                pv_last = 0
                for step in range(KT + 2):
                    if step < KT:
                        kt = step
                        sct = ps["sc0"] if kt % 2 == 0 else ps["sc1"]
                        for j in (0, 1):
                            qb = 2 * half + j
                            if kt >= 2 and j == 0:
                                w = {s["eA"]: exp_ct[kt - 2]}
                            elif kt >= 2:
                                w = {}
                            else:
                                w = dict(kv_rd[kvi])
                                for k2, v2 in q_rd[qi].items():
                                    w[k2] = max(w.get(k2, 0), v2)
                                for k2, v2 in bld.psum_need(sct).items():
                                    w[k2] = max(w.get(k2, 0), v2)
                            c = r.on("pe", lambda e, sct=sct, par=par, kt=kt,
                                     qpar=qpar, qb=qb, j=j: e.matmul(
                                         sct[:, j * QB:(j + 1) * QB],
                                         sb["kbuf"][:, par, kt, :],
                                         sb["qbuf"][:, qpar,
                                                    qb * QB:(qb + 1) * QB],
                                         start=True, stop=True),
                                     waits=w, inc=s["eP"])
                            sc_ct[(j, kt)] = c
                        s0 = (kt * 2) % 6
                        wv = dict(setup_cts["konst"])
                        wv[s["eP"]] = max(wv.get(s["eP"], 0), c)
                        for sl in (s0, s0 + 1):
                            for k2, v2 in pt_free.get(sl, {}).items():
                                wv[k2] = max(wv.get(k2, 0), v2)
                        cexp = r.on("ac", lambda e, sct=sct, s0=s0:
                                    e.activation(
                                        sb["pt"][:, s0:s0 + 2, :].rearrange(
                                            "p a b -> p (a b)"),
                                        sct[:, :],
                                        AF.Exp, bias=sb["konst"][:, 0:1],
                                        scale=cfg.SCALE),
                                    waits=wv, inc=s["eA"])
                        exp_ct[kt] = cexp
                        bld.psum_done(sct, s["eA"], cexp)
                        if kt >= PREF:
                            for j in (0, 1):
                                ptslot = s0 + j
                                eng, esem = (("ve", s["eV"]) if j == 0
                                             else ("po", s["eO"]))
                                wadd = {s["eA"]: cexp}
                                if kt == PREF:
                                    for k2, v2 in acc_free[j].items():
                                        wadd[k2] = max(wadd.get(k2, 0), v2)
                                    cadd = r.on(eng, lambda e, j=j,
                                                ptslot=ptslot: e.tensor_copy(
                                                    sb["acc"][:, j, :],
                                                    sb["pt"][:, ptslot, :]),
                                                waits=wadd, inc=esem)
                                else:
                                    cadd = r.on(eng, lambda e, j=j,
                                                ptslot=ptslot: e.tensor_add(
                                                    sb["acc"][:, j, :],
                                                    sb["acc"][:, j, :],
                                                    sb["pt"][:, ptslot, :]),
                                                waits=wadd, inc=esem)
                                add_ct[(j, kt)] = (esem, cadd)
                    if step >= 2:
                        kt2 = step - 2
                        for j in (0, 1):
                            ptslot = (kt2 * 2 + j) % 6
                            pob = ps["po0"] if j == 0 else ps["po1"]
                            if kt2 < PREF:
                                wl = ({s["eA"]: exp_ct[kt2]} if j == 0
                                      else {})
                                if kt2 == 0:
                                    for k2, v2 in bld.psum_need(
                                            ps["pl"]).items():
                                        wl[k2] = max(wl.get(k2, 0), v2)
                                    for k2, v2 in acc_free[j].items():
                                        wl[k2] = max(wl.get(k2, 0), v2)
                                cl2 = r.on("pe", lambda e, ptslot=ptslot, j=j,
                                           kt2=kt2: e.matmul(
                                               ps["pl"][32 * j:32 * j + 1, :],
                                               sb["ones_b"][:, :],
                                               sb["pt"][:, ptslot, :],
                                               start=(kt2 == 0), stop=False),
                                           waits=wl, inc=s["eP"])
                            w = {s["eA"]: exp_ct[kt2]} if j == 0 else {}
                            if kt2 == 0:
                                for k2, v2 in bld.psum_need(pob).items():
                                    w[k2] = max(w.get(k2, 0), v2)
                            cpv = r.on("pe", lambda e, pob=pob, par=par,
                                       kt2=kt2, ptslot=ptslot: e.matmul(
                                           pob[:, :],
                                           sb["vbuf"][:, par, kt2, :],
                                           sb["pt"][:, ptslot, :],
                                           start=(kt2 == 0),
                                           stop=(kt2 == KT - 1)),
                                       waits=w, inc=s["eP"])
                            pv_last = cpv
                            pf = {s["eP"]: cpv}
                            if kt2 >= PREF:
                                asem, act_ = add_ct[(j, kt2)]
                                pf[asem] = act_
                            pt_free[ptslot] = pf
                kv_used[kvi] = pv_last
                q_used[qi] = pv_last
                for j in (0, 1):
                    qb = 2 * half + j
                    bcbank = ps["bc"]
                    pob = ps["po0"] if j == 0 else ps["po1"]
                    asem, act_ = add_ct[(j, KT - 1)]
                    wred = {asem: act_}
                    red = r.on("pe", lambda e, j=j: e.matmul(
                        ps["pl"][32 * j:32 * j + 1, :], sb["ones_f"][:, 0:1],
                        sb["acc"][:, j, :], start=False, stop=True),
                        waits=wred, inc=s["eP"])
                    acc_free[j] = {s["eP"]: red}
                    cl = r.on("ac", lambda e, j=j: e.activation(
                        sb["stat"][64:65, j * QB:(j + 1) * QB],
                        ps["pl"][32 * j:32 * j + 1, :], AF.Ln),
                        waits={s["eP"]: red}, inc=s["eA"])
                    bld.psum_done(ps["pl"], s["eA"], cl)
                    cr = r.on("ac", lambda e, j=j: e.activation(
                        sb["stat"][64:65, (2 + j) * QB:(3 + j) * QB],
                        sb["stat"][64:65, j * QB:(j + 1) * QB],
                        AF.Exp, scale=-1.0),
                        waits={s["eA"]: cl}, inc=s["eA"])
                    wbc = dict(setup_cts["konst"])
                    wbc[s["eA"]] = max(wbc.get(s["eA"], 0), cr)
                    for k2, v2 in bld.psum_need(bcbank).items():
                        wbc[k2] = max(wbc.get(k2, 0), v2)
                    cbc = r.on("pe", lambda e, bcbank=bcbank, j=j: e.matmul(
                        bcbank[:, :], sb["ones_f"][64:65, :],
                        sb["stat"][64:65, (2 + j) * QB:(3 + j) * QB],
                        start=True, stop=True),
                        waits=wbc, inc=s["eP"])
                    crb = r.on("ac", lambda e, bcbank=bcbank, j=j: e.copy(
                        sb["rb"][:, j, :], bcbank[:, :]),
                        waits={s["eP"]: cbc}, inc=s["eA"])
                    bld.psum_done(bcbank, s["eA"], crb)
                    if lay == "l0":
                        co = r.on("ve", lambda e, pob=pob, qi=qi, qb=qb, j=j:
                                  e.tensor_mul(
                                      sb["arena"][:, qi, qb * QB:(qb + 1) * QB],
                                      pob[:, :], sb["rb"][:, j, :]),
                                  waits={s["eA"]: crb, s["eP"]: pv_last},
                                  inc=s["eV"])
                        o_free[j] = co
                        bld.psum_done(pob, s["eV"], co)
                        out_eV = co
                    else:
                        ss = (qi * cfg.NQB + qb) % 4
                        wv = bld.need(("att_y", ss))
                        wv[s["eA"]] = max(wv.get(s["eA"], 0), crb)
                        wv[s["eP"]] = pv_last
                        co = r.on("ve", lambda e, pob=pob, ss=ss, j=j:
                                  e.tensor_mul(
                                      sb["att_y"][:, ss, :], pob[:, :],
                                      sb["rb"][:, j, :]),
                                  waits=wv, inc=s["eV"])
                        o_free[j] = co
                        bld.psum_done(pob, s["eV"], co)
                        ysem, yct = _dma(bld, r, "sy", "yw", ss,
                                         lambda e, qi=qi, qb=qb, ss=ss:
                                         e.dma_start(
                                             out=d["ybuf"].ap()[qi, qb],
                                             in_=sb["att_y"][:, ss, :]),
                                         waits={s["eV"]: co})
                        bld.set_guard(("att_y", ss), {ysem: yct})
                        y_out[ysem] = yct
            load_q(qi + 2)
        load_kv(kvi + 2)
    res = {"mm_last": bld.ct[s["eP"]], "out_eV": out_eV, "y_out": y_out}
    es.close()
    return res


def emit_y_proj(bld, r, s, wname, rhs_ready, pm, bias_ld=None,
                extra=None, wst_wait=None, arena_q=None, pst_slots=None,
                setup_cts=None):
    cfg, sb, d = bld.cfg, bld.sb, bld.dram
    DC, NQB, QB = cfg.DC, cfg.NQB, cfg.QB
    assert len(pm) >= NQB
    mm_last, evac = {}, {}
    s1p, s2p, slot_last = {}, {}, {}
    y_out = {}
    for oc in range(DC):
        par = oc % 2
        w = {s["eP"]: mm_last.get(oc - 2, 0)}
        if wst_wait:
            for k2, v2 in wst_wait.items():
                w[k2] = max(w.get(k2, 0), v2)
        wsem, wct = _dma(bld, r, "po", "w", par,
                         lambda e, oc=oc, par=par: e.dma_start(
                             out=sb["wst"][:, par, :, :], in_=d[wname].ap()[oc]),
                         waits=w)
        mmq = {}
        for qb in range(NQB):
            bank = pm[qb]
            last_stop = extra is None
            for dc in range(DC):
                w = {}
                if dc == 0:
                    w = dict(arena_q[qb]) if arena_q else dict(rhs_ready)
                    w[wsem] = max(w.get(wsem, 0), wct)
                    for k2, v2 in bld.psum_need(bank).items():
                        w[k2] = max(w.get(k2, 0), v2)
                c = r.on("pe", lambda e, par=par, dc=dc, qb=qb, bank=bank: e.matmul(
                    bank[:, :], sb["wst"][:, par, dc, :],
                    sb["arena"][:, dc, qb * QB:(qb + 1) * QB],
                    start=(dc == 0), stop=(last_stop and dc == DC - 1)),
                    waits=w, inc=s["eP"])
            mmq[qb] = c
        mm_last[oc] = mmq[NQB - 1]
        bld.set_guard(("w", par), {s["eP"]: mmq[NQB - 1]})
        if extra is not None:
            for qb in range(NQB):
                mmq[qb] = extra(oc, qb, pm[qb])
        for qb in range(NQB):
            idx = oc * NQB + qb
            bank = pm[qb]
            s4 = idx % 4
            wv = bld.need(("fy", s4))
            wv[s["eP"]] = max(wv.get(s["eP"], 0), mmq[qb])
            if bias_ld is not None:
                for k2, v2 in bias_ld.items():
                    wv[k2] = max(wv.get(k2, 0), v2)
                ce = r.on("ac", lambda e, bank=bank, s4=s4, oc=oc: e.activation(
                    sb["fy"][:, s4, :], bank[:, :], AF.Identity,
                    bias=sb["bias_sb"][:, oc:oc + 1], scale=1.0),
                    waits=wv, inc=s["eA"])
            else:
                ce = r.on("ac", lambda e, bank=bank, s4=s4: e.copy(
                    sb["fy"][:, s4, :], bank[:, :]), waits=wv, inc=s["eA"])
            evac[idx] = ce
            bld.psum_done(bank, s["eA"], ce)
            ysem, yct = _dma(bld, r, "sy", "yw", s4,
                             lambda e, oc=oc, qb=qb, s4=s4: e.dma_start(
                                 out=d["ybuf"].ap()[oc, qb],
                                 in_=sb["fy"][:, s4, :]),
                             waits={s["eA"]: ce})
            fg = {ysem: yct}
            if pst_slots is not None:
                s2y = idx % 2
                cq = r.on("ac", lambda e, s4=s4, s2y=s2y: e.square(
                    sb["yb2"][:, s2y, :], sb["fy"][:, s4, :]),
                    waits={s["eP"]: s2p.get(idx - 2, 0)}, inc=s["eA"])
                b1, m1 = pst_slots[2 * qb]
                b2, m2 = pst_slots[2 * qb + 1]
                w1 = dict(setup_cts["konst"])
                w1[s["eA"]] = max(w1.get(s["eA"], 0), ce)
                if oc == 0:
                    for k2, v2 in bld.psum_need(b1).items():
                        w1[k2] = max(w1.get(k2, 0), v2)
                c1 = r.on("pe", lambda e, b1=b1, m1=m1, s4=s4, oc=oc: e.matmul(
                    b1[m1:m1 + 1, :], sb["ones_f"][:, 0:1],
                    sb["fy"][:, s4, :],
                    start=(oc == 0), stop=(oc == DC - 1)),
                    waits=w1, inc=s["eP"])
                w2b = {s["eA"]: cq}
                if oc == 0:
                    for k2, v2 in bld.psum_need(b2).items():
                        w2b[k2] = max(w2b.get(k2, 0), v2)
                c2 = r.on("pe", lambda e, b2=b2, m2=m2, s2y=s2y, oc=oc:
                          e.matmul(
                              b2[m2:m2 + 1, :], sb["ones_b"][:, :],
                              sb["yb2"][:, s2y, :],
                              start=(oc == 0), stop=(oc == DC - 1)),
                          waits=w2b, inc=s["eP"])
                s1p[idx] = c1
                s2p[idx] = c2
                slot_last[2 * qb] = c1
                slot_last[2 * qb + 1] = c2
                fg[s["eA"]] = cq
                fg[s["eP"]] = c1
            bld.set_guard(("fy", s4), fg)
            y_out[ysem] = yct
    return {"y_out": y_out, "mm_last": max(mm_last.values()),
            "slot_last": slot_last}


def emit_ffn_pre(bld, r, s, lay, arena_rdy, setup_cts, region_gate=None,
                 arena_q=None):
    cfg, sb, d = bld.cfg, bld.sb, bld.dram
    DC, NQB, QB, TPQ, NTT, E = cfg.DC, cfg.NQB, cfg.QB, cfg.TPQ, cfg.NTT, cfg.E
    la = f"{lay}f"
    es, ps = bld.psum_scope(
        [("plg", (128, QB), F32), ("ptt", (128, 128), F32),
         ("ptt2", (128, QB), F32), ("pwx", (128, QB), F32),
         ("ph0", (128, QB), F32), ("ph1", (128, QB), F32)])
    gW = dict(region_gate) if region_gate else dict(arena_rdy)
    wg_ld = dict([_dma(bld, r, "po", "wg", 0, lambda e: e.dma_start(
        out=sb["wg_sb"][:, :, :], in_=d[f"{la}_wg"].ap()), waits=gW)])
    sel_ld = dict([_dma(bld, r, "po", "sel", 0, lambda e: e.dma_start(
        out=sb["sel_sb"][:, :], in_=d[f"{la}_sel"].ap()), waits=gW)])
    a_ld = dict([_dma(bld, r, "po", "wa", 0, lambda e: e.dma_start(
        out=sb["ffn_a"][:, :, :], in_=d[f"{la}_wa"].ap()), waits=gW)])
    b_ld = dict([_dma(bld, r, "po", "wbm", 0, lambda e: e.dma_start(
        out=sb["ffn_b"][:, :], in_=d[f"{la}_wm"].ap()), waits=dict(arena_rdy))])
    bias_ld = dict([_dma(bld, r, "sy", "fbias", 0, lambda e: e.dma_start(
        out=sb["bias_sb"][:, :], in_=d[f"{la}_bb"].ap()), waits=gW)])

    lg_ct = {}
    tr_last = 0
    for qb in range(NQB):
        sq = qb % 2
        for dc in range(DC):
            w = {}
            if dc == 0:
                w = dict(wg_ld)
                for k2, v2 in (arena_q[qb] if arena_q else arena_rdy).items():
                    w[k2] = max(w.get(k2, 0), v2)
                for k2, v2 in bld.psum_need(ps["plg"]).items():
                    w[k2] = max(w.get(k2, 0), v2)
            c = r.on("pe", lambda e, dc=dc, qb=qb: e.matmul(
                ps["plg"][0:E, :], sb["wg_sb"][:, dc, :],
                sb["arena"][:, dc, qb * QB:(qb + 1) * QB],
                start=(dc == 0), stop=(dc == DC - 1)),
                waits=w, inc=s["eP"])
        ce = r.on("ac", lambda e, sq=sq: e.copy(
            sb["lgs"][:, sq, :], ps["plg"][0:E, :]),
            waits={s["eP"]: c}, inc=s["eA"])
        lg_ct[qb] = ce
        bld.psum_done(ps["plg"], s["eA"], ce)
        wv = dict(setup_cts["iden_f"])
        wv[s["eA"]] = ce
        for k2, v2 in bld.psum_need(ps["ptt"]).items():
            wv[k2] = max(wv.get(k2, 0), v2)
        for t in range(TPQ):
            tt = qb * TPQ + t
            tr_last = r.on("pe", lambda e, sq=sq, t=t, tt=tt: e.transpose(
                ps["ptt"][:, tt * E:(tt + 1) * E],
                sb["lgs"][:, sq, t * 128:(t + 1) * 128],
                sb["iden_fsb"][0:E, 0:E]),
                waits=wv, inc=s["eP"])
    clg = r.on("ac", lambda e: e.copy(sb["lg_tm"][:, :], ps["ptt"][:, 0:NTT * E]),
               waits={s["eP"]: tr_last}, inc=s["eA"])
    bld.psum_done(ps["ptt"], s["eA"], clg)

    hprev = {}
    for qb in range(NQB):
        phb = ps["ph0"] if qb % 2 == 0 else ps["ph1"]
        for dc in range(DC):
            w = {}
            if dc == 0:
                w = dict(a_ld)
                for k2, v2 in (arena_q[qb] if arena_q else arena_rdy).items():
                    w[k2] = max(w.get(k2, 0), v2)
                for k2, v2 in bld.psum_need(phb).items():
                    w[k2] = max(w.get(k2, 0), v2)
            c = r.on("pe", lambda e, dc=dc, qb=qb, phb=phb: e.matmul(
                phb[:, :], sb["ffn_a"][:, dc, :],
                sb["arena"][:, dc, qb * QB:(qb + 1) * QB],
                start=(dc == 0), stop=(dc == DC - 1)),
                waits=w, inc=s["eP"])
        ch = r.on("ac", lambda e, qb=qb, phb=phb: e.copy(
            sb["ffn_h"][:, qb * QB:(qb + 1) * QB], phb[:, :]),
            waits={s["eP"]: c}, inc=s["eA"])
        hprev[qb] = ch
        bld.psum_done(phb, s["eA"], ch)

    mprev = 0
    for tt in range(NTT):
        c1 = r.on("ve", lambda e, tt=tt: e.reduce_max(
            sb["r16"][:, 0, tt:tt + 1], sb["lg_tm"][:, tt * E:(tt + 1) * E],
            axis=AX.X), waits={s["eA"]: clg}, inc=s["eV"])
        c2 = r.on("ve", lambda e, tt=tt: e.tensor_scalar(
            sb["r8"][:, 0, :], sb["lg_tm"][:, tt * E:(tt + 1) * E],
            sb["r16"][:, 0, tt:tt + 1], None, ALU.is_equal),
            waits={s["eV"]: c1}, inc=s["eV"])
        c3 = r.on("ve", lambda e, tt=tt: e.scalar_tensor_tensor(
            sb["t_tm"][:, tt * E:(tt + 1) * E], sb["r8"][:, 0, :], -1e30,
            sb["lg_tm"][:, tt * E:(tt + 1) * E], ALU.mult, ALU.add),
            waits={s["eV"]: c2}, inc=s["eV"])
        mprev = r.on("ve", lambda e, tt=tt: e.reduce_max(
            sb["r16"][:, 1, tt:tt + 1], sb["t_tm"][:, tt * E:(tt + 1) * E],
            axis=AX.X), waits={s["eV"]: c3}, inc=s["eV"])
    cp = r.on("ve", lambda e: e.tensor_sub(
        sb["r16"][:, 2, :], sb["r16"][:, 1, :], sb["r16"][:, 0, :]),
        waits={s["eV"]: mprev}, inc=s["eV"])
    cpe = r.on("ac", lambda e: e.activation(
        sb["r16"][:, 3, :], sb["r16"][:, 2, :], AF.Exp),
        waits={s["eV"]: cp}, inc=s["eA"])
    cd = r.on("ac", lambda e: e.add(sb["r16"][:, 4, :], sb["r16"][:, 3, :], 1.0),
              waits={s["eA"]: cpe}, inc=s["eA"])
    crec = r.on("ve", lambda e: e.reciprocal(sb["r16"][:, 5, :], sb["r16"][:, 4, :]),
                waits={s["eA"]: cd}, inc=s["eV"])
    cw2 = r.on("ve", lambda e: e.tensor_mul(
        sb["r16"][:, 2, :], sb["r16"][:, 3, :], sb["r16"][:, 5, :]),
        waits={s["eV"]: crec}, inc=s["eV"])
    wlast = cw2
    for tt in range(NTT):
        c1 = r.on("ve", lambda e, tt=tt: e.tensor_scalar(
            sb["r8"][:, 0, :], sb["lg_tm"][:, tt * E:(tt + 1) * E],
            sb["r16"][:, 0, tt:tt + 1], None, ALU.is_equal),
            waits={s["eV"]: cw2}, inc=s["eV"])
        c2 = r.on("ve", lambda e, tt=tt: e.tensor_scalar(
            sb["r8"][:, 1, :], sb["t_tm"][:, tt * E:(tt + 1) * E],
            sb["r16"][:, 1, tt:tt + 1], None, ALU.is_equal),
            waits={s["eV"]: c1}, inc=s["eV"])
        c3 = r.on("ve", lambda e, tt=tt: e.tensor_scalar(
            sb["r8"][:, 2, :], sb["r8"][:, 0, :],
            sb["r16"][:, 5, tt:tt + 1], None, ALU.mult),
            waits={s["eV"]: c2}, inc=s["eV"])
        wlast = r.on("ve", lambda e, tt=tt: e.scalar_tensor_tensor(
            sb["w_tm"][:, tt * E:(tt + 1) * E], sb["r8"][:, 1, :],
            sb["r16"][:, 2, tt:tt + 1], sb["r8"][:, 2, :], ALU.mult, ALU.add),
            waits={s["eV"]: c3}, inc=s["eV"])
    hw_last = 0
    wT_ct = {}
    for qb in range(NQB):
        trl = 0
        wv = dict(setup_cts["iden_f"])
        wv[s["eV"]] = wlast
        for k2, v2 in bld.psum_need(ps["ptt2"]).items():
            wv[k2] = max(wv.get(k2, 0), v2)
        for t in range(TPQ):
            tt = qb * TPQ + t
            trl = r.on("pe", lambda e, t=t, tt=tt: e.transpose(
                ps["ptt2"][0:E, t * 128:(t + 1) * 128],
                sb["w_tm"][:, tt * E:(tt + 1) * E], sb["iden_fsb"][:, :]),
                waits=wv, inc=s["eP"])
        cwt = r.on("ac", lambda e, qb=qb: e.copy(
            sb["wTb"][:, qb * QB:(qb + 1) * QB], ps["ptt2"][0:E, 0:QB]),
            waits={s["eP"]: trl}, inc=s["eA"])
        wT_ct[qb] = cwt
        bld.psum_done(ps["ptt2"], s["eA"], cwt)
    hw_q = {}
    for qb in range(NQB):
        wv = dict(sel_ld)
        wv[s["eA"]] = wT_ct[qb]
        for k2, v2 in bld.psum_need(ps["pwx"]).items():
            wv[k2] = max(wv.get(k2, 0), v2)
        cwx = r.on("pe", lambda e, qb=qb: e.matmul(
            ps["pwx"][:, :], sb["sel_sb"][:, :],
            sb["wTb"][:, qb * QB:(qb + 1) * QB], start=True, stop=True),
            waits=wv, inc=s["eP"])
        hw_last = r.on("ve", lambda e, qb=qb: e.tensor_mul(
            sb["ffn_hw"][:, qb * QB:(qb + 1) * QB],
            sb["ffn_h"][:, qb * QB:(qb + 1) * QB], ps["pwx"][:, :]),
            waits={s["eA"]: hprev[qb], s["eP"]: cwx}, inc=s["eV"])
        hw_q[qb] = hw_last
        bld.psum_done(ps["pwx"], s["eV"], hw_last)
    import os as _os
    if _os.environ.get("MK_DBG_RT") == "1":
        for nm, t in (("dbg_lgtm", sb["lg_tm"]), ("dbg_wtm", sb["w_tm"]),
                      ("dbg_hw", sb["ffn_hw"]), ("dbg_h", sb["ffn_h"]),
                      ("dbg_wtb", sb["wTb"])):
            ap = bld.dram[nm].ap()
            r.on("sy", lambda e, ap=ap, t=t: e.dma_start(
                out=ap, in_=t[:, :] if len(t.shape) == 2 else t[:, :, :]),
                waits={s["eV"]: hw_last}, inc=bld.dma_sem("dbg", 0), incn=16)
    es.close()
    return {"hw_eV": hw_last, "hw_q": hw_q, "b_ld": b_ld, "bias_ld": bias_ld}


def emit_epilogue(bld, r, s, xin, yin, xout, ba_ct, pst_slots, want_stats,
                  setup_cts, gate=None):
    cfg, sb = bld.cfg, bld.sb
    DC, NQB, QB = cfg.DC, cfg.NQB, cfg.QB
    slot_last = {}
    s1p, s2p = {}, {}
    out_sems = {}
    for qb in range(NQB):
        for dc in range(DC):
            idx = qb * DC + dc
            s4, s2 = idx % 4, idx % 2
            w = {s["eV"]: ba_ct}
            if gate:
                for k2, v2 in gate.items():
                    w[k2] = max(w.get(k2, 0), v2)
            xsem, dx = _dma(bld, r, "sy", "x", s4,
                            lambda e, dc=dc, qb=qb, s4=s4: e.dma_start(
                                out=sb["ep_x"][:, s4, :],
                                in_=xin[dc, qb]),
                            waits=w)
            ysem, dy = _dma(bld, r, "sy", "y_in", s4,
                            lambda e, dc=dc, qb=qb, s4=s4: e.dma_start(
                                out=sb["ep_y"][:, s4, :],
                                in_=yin[dc, qb]),
                            waits=w)
            ct = r.on("ve", lambda e, s4=s4, qb=qb: e.tensor_mul(
                sb["ep_y"][:, s4, :], sb["ep_y"][:, s4, :],
                sb["Ba"][:, qb * QB:(qb + 1) * QB]),
                waits={ysem: dy, s["eV"]: ba_ct}, inc=s["eV"])
            co = r.on("po", lambda e, s4=s4, qb=qb: e.tensor_tensor(
                sb["ep_x"][:, s4, :], sb["ep_x"][:, s4, :],
                sb["Bc"][:, qb * QB:(qb + 1) * QB], ALU.subtract),
                waits={xsem: dx, s["eV"]: ba_ct}, inc=s["eO"])
            cx = r.on("ve", lambda e, s4=s4: e.tensor_add(
                sb["ep_x"][:, s4, :], sb["ep_x"][:, s4, :],
                sb["ep_y"][:, s4, :]),
                waits={s["eO"]: co}, inc=s["eV"])
            bld.set_guard(("y_in", s4), {s["eV"]: cx})
            osem, oct = _dma(bld, r, "ac", "xo", s4,
                             lambda e, dc=dc, qb=qb, s4=s4: e.dma_start(
                                 out=xout[dc, qb],
                                 in_=sb["ep_x"][:, s4, :]),
                             waits={s["eV"]: cx})
            out_sems[osem] = oct
            xg = {osem: oct}
            if want_stats:
                cq = r.on("ac", lambda e, s4=s4, s2=s2: e.square(
                    sb["ep_b2"][:, s2, :], sb["ep_x"][:, s4, :]),
                    waits={s["eV"]: cx, s["eP"]: s2p.get(idx - 2, 0)},
                    inc=s["eA"])
                b1, m1 = pst_slots[2 * qb]
                b2, m2 = pst_slots[2 * qb + 1]
                w1 = dict(setup_cts["konst"])
                w1[s["eV"]] = max(w1.get(s["eV"], 0), cx)
                if dc == 0:
                    for k2, v2 in bld.psum_need(b1).items():
                        w1[k2] = max(w1.get(k2, 0), v2)
                c1 = r.on("pe", lambda e, b1=b1, m1=m1, s4=s4, dc=dc: e.matmul(
                    b1[m1:m1 + 1, :], sb["ones_f"][:, 0:1],
                    sb["ep_x"][:, s4, :],
                    start=(dc == 0), stop=(dc == DC - 1)),
                    waits=w1, inc=s["eP"])
                w2b = {s["eA"]: cq}
                if dc == 0:
                    for k2, v2 in bld.psum_need(b2).items():
                        w2b[k2] = max(w2b.get(k2, 0), v2)
                c2 = r.on("pe", lambda e, b2=b2, m2=m2, s2=s2, dc=dc: e.matmul(
                    b2[m2:m2 + 1, :], sb["ones_b"][:, :], sb["ep_b2"][:, s2, :],
                    start=(dc == 0), stop=(dc == DC - 1)),
                    waits=w2b, inc=s["eP"])
                s1p[idx] = c1
                s2p[idx] = c2
                slot_last[2 * qb] = c1
                slot_last[2 * qb + 1] = c2
                xg[s["eA"]] = cq
                xg[s["eP"]] = c1
            bld.set_guard(("x", s4), xg)
    return out_sems, (slot_last if want_stats else None)


def emit_block(bld, r, blk, bi, prev, is_last):
    cfg = bld.cfg
    lay = blk[:2].lower()
    is_attn = blk.endswith("A")
    s = bld.blk_sems(blk)
    xin = bld.xbuf(bi)
    xnext = bld.xbuf(bi + 1)
    d, sb = bld.dram, bld.sb

    setup_cts = emit_setup(bld, r, s) if prev is None else prev["setup"]

    if prev is None or "stats_ready" not in prev or FORCE_STATS0:
        es0, ps0 = bld.psum_scope(
            [("st0", (128, cfg.QB), F32), ("st1", (128, cfg.QB), F32),
             ("st2", (128, cfg.QB), F32)])
        slots0 = stat_slots(cfg, [ps0["st0"], ps0["st1"], ps0["st2"]])
        c = emit_stats0(bld, r, s, xin, slots0, setup_cts,
                        gate=(dict(prev["x_ready"]) if prev and "x_ready" in prev
                              else None))
        es0.close()
        stats_ready = {s["eA"]: c}
    else:
        stats_ready = prev["stats_ready"]

    esb, pbb = bld.psum_scope([("b0", (128, cfg.QB), F32),
                               ("b1", (128, cfg.QB), F32)])
    ba = emit_ln_finalize(bld, r, s, stats_ready, pbb, setup_cts)
    esb.close()
    extra = dict(prev.get("arena_free", {})) if prev else {}
    if prev and "x_ready" in prev:
        for k, v in prev["x_ready"].items():
            extra[k] = max(extra.get(k, 0), v)
    epi_gate = dict(prev.get("epi_done", {})) if prev else None

    if is_attn:
        kw = "l0a_wk" if lay == "l0" else "l1a_wk"
        pre_w = {}
        for oc in (0, 1):
            w = dict(prev.get("wst_free", {})) if prev else {}
            pre_w[oc] = _dma(bld, r, "po", "w", oc % 2,
                             lambda e, oc=oc: e.dma_start(
                                 out=bld.sb["wst"][:, oc % 2, :, :],
                                 in_=d[kw].ap()[oc]),
                             waits=w)
    arena_rdy, arena_q = emit_ln_apply(bld, r, s, xin, ba,
                                       extra_wait=extra or None)
    wst_free = dict(prev.get("wst_free", {})) if prev else {}

    if is_attn:
        rl = _dma(bld, r, "sy", "rope", 0, lambda e: e.dma_start(
            out=sb["ropeCsb"][:, :], in_=d["ropeC"].ap()),
            waits=dict(arena_rdy))
        rl2 = _dma(bld, r, "sy", "rope", 0, lambda e: e.dma_start(
            out=sb["ropeSsb"][:, :], in_=d["ropeS"].ap()))
        rope_ld = {rl2[0]: rl2[1]}
        bias_ld = {}
        if lay == "l1":
            for src, dst in (("l1a_bq", "b_q"), ("l1a_bk", "b_k"),
                             ("l1a_bv", "b_v")):
                bl = _dma(bld, r, "sy", "abias", 0,
                          lambda e, src=src, dst=dst: e.dma_start(
                              out=sb[dst][:, :], in_=d[src].ap()),
                          waits=(epi_gate or arena_rdy))
            bias_ld = {bl[0]: bl[1]}
        esp, psp = bld.psum_scope(
            [("pp0", (128, cfg.QB), F32), ("pp1", (128, cfg.QB), F32),
             ("ptr", (128, 1024), BF16)])
        pp = [psp["pp0"], psp["pp1"]]
        kres = emit_proj(bld, r, s, lay, "k", arena_rdy, pp, psp["ptr"],
                         setup_cts, bias_ld, rope_ld, wst_free,
                         preloaded=pre_w, arena_q=arena_q)
        vres = emit_proj(bld, r, s, lay, "v", arena_rdy, pp, psp["ptr"],
                         setup_cts, bias_ld, rope_ld, wst_free,
                         arena_q=arena_q)
        nkv = cfg.NH if lay == "l0" else cfg.NKV1
        kv_cc = emit_exchange(bld, r, s, kres["out"], vres["out"], nkv)
        qres = emit_proj(bld, r, s, lay, "q", arena_rdy, pp, psp["ptr"],
                         setup_cts, bias_ld, rope_ld, wst_free,
                         arena_q=arena_q)
        esp.close()

        if lay == "l1":
            ares = emit_attention(bld, r, s, lay, setup_cts, kv_cc,
                                  qres["out"])
            y_gate = dict(ares["y_out"])
            arena_reader = {s["eP"]: ares["mm_last"], s["eA"]: bld.ct[s["eA"]],
                            s["eV"]: bld.ct[s["eV"]]}
            wst_next = {s["eP"]: ares["mm_last"], s["eV"]: bld.ct[s["eV"]]}
        else:
            ares = emit_attention(bld, r, s, lay, setup_cts, kv_cc, qres["out"])
            esy, psy = bld.psum_scope(
                [("st0", (128, cfg.QB), F32), ("st1", (128, cfg.QB), F32),
                 ("st2", (128, cfg.QB), F32)])
            slotsy = stat_slots(cfg, [psy["st0"], psy["st1"], psy["st2"]])
            espm, pspm = bld.psum_scope(
                [("pm0", (128, cfg.QB), F32), ("pm1", (128, cfg.QB), F32),
                 ("pm2", (128, cfg.QB), F32), ("pm3", (128, cfg.QB), F32)])
            wres = emit_y_proj(
                bld, r, s, "l0a_wo", {s["eV"]: ares["out_eV"]},
                [pspm["pm0"], pspm["pm1"], pspm["pm2"], pspm["pm3"]],
                wst_wait={s["eP"]: ares["mm_last"], s["eV"]: ares["out_eV"]},
                pst_slots=slotsy, setup_cts=setup_cts)
            espm.close()
            stats_eA = emit_stats_evac(bld, r, s, slotsy, wres["slot_last"])
            esy.close()
            y_gate = dict(wres["y_out"])
            arena_reader = {s["eP"]: wres["mm_last"]}
            wst_next = {s["eP"]: wres["mm_last"]}
    else:
        pre = emit_ffn_pre(bld, r, s, lay, arena_rdy, setup_cts,
                           region_gate=epi_gate, arena_q=arena_q)
        esy, psy = bld.psum_scope(
            [("st0", (128, cfg.QB), F32), ("st1", (128, cfg.QB), F32),
             ("st2", (128, cfg.QB), F32)])
        slotsy = stat_slots(cfg, [psy["st0"], psy["st1"], psy["st2"]])
        espm, pspm = bld.psum_scope(
            [("pm0", (128, cfg.QB), F32), ("pm1", (128, cfg.QB), F32),
             ("pm2", (128, cfg.QB), F32), ("pm3", (128, cfg.QB), F32)])

        def extra2(oc, qb, bank):
            w = dict(pre["b_ld"])
            w[s["eV"]] = pre["hw_q"][qb]
            return r.on("pe", lambda e, oc=oc, qb=qb, bank=bank: e.matmul(
                bank[:, :], sb["ffn_b"][:, oc * 128:(oc + 1) * 128],
                sb["ffn_hw"][:, qb * cfg.QB:(qb + 1) * cfg.QB],
                start=False, stop=True),
                waits=w, inc=s["eP"])

        import os as _os
        wres = emit_y_proj(bld, r, s, f"{lay}f_wb", dict(arena_rdy),
                           [pspm["pm0"], pspm["pm1"], pspm["pm2"],
                            pspm["pm3"]],
                           bias_ld=pre["bias_ld"],
                           extra=(None if _os.environ.get("MK_NODELTA") == "1"
                                  else extra2),
                           wst_wait=wst_free, arena_q=arena_q,
                           pst_slots=slotsy, setup_cts=setup_cts)
        espm.close()
        stats_eA = emit_stats_evac(bld, r, s, slotsy, wres["slot_last"])
        esy.close()
        y_gate = dict(wres["y_out"])
        arena_reader = {s["eP"]: wres["mm_last"]}
        wst_next = {s["eP"]: wres["mm_last"]}

    if is_attn and lay == "l1":
        esy, psy = bld.psum_scope(
            [("st0", (128, cfg.QB), F32), ("st1", (128, cfg.QB), F32),
             ("st2", (128, cfg.QB), F32)])
        slotsy = stat_slots(cfg, [psy["st0"], psy["st1"], psy["st2"]])
        stats_eA = emit_stats0(bld, r, s, d["ybuf"].ap(), slotsy, setup_cts,
                               gate=y_gate)
        esy.close()

    esb2, pbb2 = bld.psum_scope([("b0", (128, cfg.QB), F32),
                                 ("b1", (128, cfg.QB), F32)])
    ba2 = emit_ln_finalize(bld, r, s, {s["eA"]: stats_eA}, pbb2, setup_cts)
    esb2.close()
    want_stats = (not is_last) and not FORCE_STATS0
    if want_stats:
        es3, ps3 = bld.psum_scope(
            [("st0", (128, cfg.QB), F32), ("st1", (128, cfg.QB), F32),
             ("st2", (128, cfg.QB), F32)])
        slots3 = stat_slots(cfg, [ps3["st0"], ps3["st1"], ps3["st2"]])
    else:
        slots3 = None
    out_sems, slot_last = emit_epilogue(
        bld, r, s, xin, d["ybuf"].ap(), xnext, ba2, slots3, want_stats,
        setup_cts, gate=y_gate)
    handoff = {"setup": setup_cts, "wst_free": wst_next,
               "arena_free": arena_reader,
               "x_ready": dict(out_sems), "sems": s,
               "epi_done": {s["eV"]: bld.ct[s["eV"]], s["eO"]: bld.ct[s["eO"]],
                            s["eA"]: bld.ct[s["eA"]], s["eP"]: bld.ct[s["eP"]]}}
    if want_stats:
        c = emit_stats_evac(bld, r, s, slots3, slot_last)
        es3.close()
        handoff["stats_ready"] = {s["eA"]: c}
    if is_last:
        import os as _os
        if _os.environ.get("MK_DBG_AR") == "1":
            r.on("sy", lambda e: e.dma_start(
                out=bld.dram["dbg_arena"].ap(),
                in_=bld.sb["arena"][:, :, :]),
                waits=dict(out_sems), inc=bld.dma_sem("dbg", 1), incn=16)
        r.on("sy", lambda e: None, waits=dict(out_sems))
    return handoff


def build_program(cfg, blocks):
    bld = Builder(cfg, blocks)
    with bld.nc.Block() as blk:
        prev = None
        for bi, b in enumerate(blocks):
            r = Rec(bld)
            prev = emit_block(bld, r, b, bi, prev, bi == len(blocks) - 1)
            bld.flush(blk, r)
    return bld


MODE_GROUPS = [["L0A", "L0F", "L1A", "L1F"]]
_CACHE = {}


def _get_builder(cfg_key, blocks):
    key = (cfg_key, tuple(blocks))
    if key not in _CACHE:
        _CACHE[key] = build_program(Cfg(mini=(cfg_key == "mini")), list(blocks))
    return _CACHE[key]


def kernel(**inputs):
    cfg = Cfg(mini=False)
    shared = prep_shared(cfg, inputs)
    x_full = np.asarray(inputs["L0_attn_x"], dtype=np.float32)
    B = x_full.shape[0]
    ncores = 8
    core_x = []
    ropes = []
    for c in range(ncores):
        b, half = c // 2, c % 2
        slab = x_full[b, half * cfg.TOK:(half + 1) * cfg.TOK, :].T
        core_x.append(np.ascontiguousarray(
            slab.reshape(cfg.DC, 128, cfg.NQB, cfg.QB).transpose(0, 2, 1, 3)))
        ropes.append(_rope_tables(cfg, half))

    for blocks in MODE_GROUPS:
        bld = _get_builder("full", blocks)
        names = [n for n, t in bld.dram.items()
                 if bld.nc.lookup_mls(t).kind == "ExternalInput"]
        in_maps = []
        for c in range(ncores):
            m = {}
            for n in names:
                if n == "x0":
                    m[n] = core_x[c]
                elif n == "ropeC":
                    m[n] = ropes[c][0]
                elif n == "ropeS":
                    m[n] = ropes[c][1]
                elif n == "iden_b":
                    m[n] = shared["iden_b"]
                else:
                    m[n] = shared[n]
            in_maps.append(m)
        res = run_bass_kernel_spmd(bld.nc, in_maps, core_ids=list(range(ncores)))
        core_x = [np.asarray(res.results[c]["xout"]) for c in range(ncores)]

    out = np.empty((B, 2 * cfg.TOK, cfg.H), dtype=np.float32)
    for c in range(ncores):
        b, half = c // 2, c % 2
        slab = np.asarray(core_x[c]).transpose(0, 2, 1, 3).reshape(
            cfg.H, cfg.TOK)
        out[b, half * cfg.TOK:(half + 1) * cfg.TOK, :] = slab.T
    return out
